# version 01
# baseline (speedup 1.0000x reference)
import numpy as np
import ml_dtypes
from contextlib import ExitStack
import concourse.bass as bass
import concourse.mybir as mybir
from concourse.bass_utils import run_bass_kernel_spmd

F32 = mybir.dt.float32
BF16 = mybir.dt.bfloat16
I32 = mybir.dt.int32
AF = mybir.ActivationFunctionType
ALU = mybir.AluOpType
BF = ml_dtypes.bfloat16

NCORE = 8
NTOK = 2048
NCH = 8
D = 1024
W = 512
NWIN = NTOK // W
SEQ = 4096
HID = 3072
SC_ATT = np.float32(0.03125)
S127 = np.float32(1.0 / 127.0)
EPS = np.float32(1e-5)

SUBS = (1, 2, 3, 4, 5, 6)
ATTN_STOP = None
DBG = None

ENGS = ("sync", "tensor", "vector", "scalar", "gpsimd")

VCOLS = {}
def _vc(name):
    if name not in VCOLS:
        VCOLS[name] = len(VCOLS)
    return VCOLS[name]
for _n in ("gamma", "beta", "conv_b", "b_dt", "A", "D_skip", "mu_r", "mu_k",
           "selprev", "selnext"):
    for _k in range(8):
        _vc(f"{_n}{_k}")
for _n in ("cw0", "cw1", "cw2"):
    for _k in range(8):
        _vc(f"{_n}{_k}")
for _n in ("eps", "one", "neghalf", "s127"):
    _vc(_n)


class Bld:
    def __init__(self, nc, es):
        self.nc = nc
        self.es = es
        self.q = {e: [] for e in ENGS}
        self.sems = {}
        self.cnt = {}

    def sem(self, name):
        if name not in self.sems:
            self.sems[name] = self.es.enter_context(self.nc.semaphore(name))
            self.cnt[name] = 0
        return self.sems[name]

    def emit(self, eng, fn):
        self.q[eng].append(fn)

    def op(self, eng, sname, build, inc=1):
        h = self.sem(sname)
        self.cnt[sname] += inc
        tok = self.cnt[sname]
        self.emit(eng, lambda e, b=build, h=h, i=inc: b(e).then_inc(h, i))
        return tok

    def opn(self, eng, build):
        self.emit(eng, lambda e, b=build: b(e))

    def wait(self, eng, sname, tok):
        h = self.sem(sname)
        self.emit(eng, lambda e, h=h, t=tok: e.wait_ge(h, t))

    def dma(self, eng, sname, out, in_):
        return self.op(eng, sname, lambda e, o=out, i=in_: e.dma_start(out=o, in_=i), inc=16)

    def barrier(self, tag):
        toks = {}
        toks["tensor"] = self.op("tensor", "bar_tensor", lambda en: en.nop())
        toks["vector"] = self.op("vector", "bar_vector", lambda en: en.tensor_copy(
            out=self.dum[0:1, 0:1], in_=self.dum[0:1, 1:2]))
        toks["scalar"] = self.op("scalar", "bar_scalar", lambda en: en.copy(
            out=self.dum[0:1, 2:3], in_=self.dum[0:1, 3:4]))
        toks["gpsimd"] = self.op("gpsimd", "bar_gpsimd", lambda en: en.tensor_copy(
            out=self.dum[0:1, 4:5], in_=self.dum[0:1, 5:6]))
        toks["sync"] = self.op("sync", "bar_sync", lambda en: en.nop())
        ldc = self.cnt.get("LD", 0)
        stc = self.cnt.get("ST", 0)
        lgc = self.cnt.get("LG", 0)
        for e in ENGS:
            for o in ENGS:
                if o != e:
                    self.wait(e, f"bar_{o}", toks[o])
            if ldc:
                self.wait(e, "LD", ldc)
            if stc:
                self.wait(e, "ST", stc)
            if lgc:
                self.wait(e, "LG", lgc)


class Pool:
    def __init__(self, bld, slots):
        self.bld = bld
        self.slots = slots
        self.freetok = [None] * len(slots)
        self.i = 0

    def acq(self):
        j = self.i % len(self.slots)
        self.i += 1
        if self.freetok[j] is not None:
            sn, tok = self.freetok[j]
            self.bld.wait("tensor", sn, tok)
        return j, self.slots[j]

    def rel(self, j, sname, tok):
        self.freetok[j] = (sname, tok)


def chs(t, k, w=None):
    if w is None:
        return t[:, k * NTOK:(k + 1) * NTOK]
    return t[:, k * NTOK + w * W: k * NTOK + w * W + W]


def emit_ln(bld, T, XS, PA, PB, dst_bf, ones_m, ones_r, VEC, LXB, LSQ, LE1, ROWS,
           tag, gamma_beta=None, combined=False, apply=True):
    last = None
    tn_by_par = [None, None]
    for k in range(NCH):
        par = k % 2
        lxb = LXB[:, par * NTOK:(par + 1) * NTOK]
        lsq = LSQ[:, par * NTOK:(par + 1) * NTOK]
        if tn_by_par[par] is not None:
            bld.wait("vector", "TN", tn_by_par[par])
        bld.op("vector", "VE", lambda e, lxb=lxb, k=k: e.tensor_copy(out=lxb, in_=chs(XS, k)))
        ctok = bld.op("vector", "VE", lambda e, lxb=lxb, lsq=lsq: e.tensor_tensor(
            out=lsq, in0=lxb, in1=lxb, op=ALU.mult))
        bld.wait("tensor", "VE", ctok)
        for w in range(NWIN):
            bld.opn("tensor", lambda e, w=w, k=k, lxb=lxb: e.matmul(
                out=PA[:, w * W:(w + 1) * W], lhsT=ones_m[:], rhs=lxb[:, w * W:(w + 1) * W],
                start=(k == 0), stop=(k == NCH - 1), skip_group_check=True))
            last = bld.op("tensor", "TN", lambda e, w=w, k=k, lsq=lsq: e.matmul(
                out=PB[:, w * W:(w + 1) * W], lhsT=ones_m[:], rhs=lsq[:, w * W:(w + 1) * W],
                start=(k == 0), stop=(k == NCH - 1), skip_group_check=True))
        tn_by_par[par] = last
    bld.wait("vector", "TN", last)
    bld.op("vector", "VE", lambda e: e.tensor_copy(out=ROWS[0:1, 0:NTOK], in_=PA[0:1, :]))
    bld.op("vector", "VE", lambda e: e.tensor_tensor(
        out=ROWS[96:97, 0:NTOK], in0=ROWS[0:1, 0:NTOK], in1=ROWS[0:1, 0:NTOK], op=ALU.mult))
    vtok = bld.op("vector", "VE", lambda e: e.tensor_tensor(
        out=ROWS[32:33, 0:NTOK], in0=PB[0:1, :], in1=ROWS[96:97, 0:NTOK], op=ALU.subtract))
    bld.wait("scalar", "VE", vtok)
    bld.op("scalar", "SC", lambda e: e.activation(
        out=ROWS[96:97, 0:NTOK], in_=ROWS[32:33, 0:NTOK], func=AF.Ln,
        bias=VEC[0:1, _vc("eps"):_vc("eps") + 1]))
    stok = bld.op("scalar", "SC", lambda e: e.activation(
        out=ROWS[64:65, 0:NTOK], in_=ROWS[96:97, 0:NTOK], func=AF.Exp,
        scale=VEC[0:1, _vc("neghalf"):_vc("neghalf") + 1]))
    if combined:
        bld.wait("vector", "SC", stok)
        bld.op("vector", "VE", lambda e: e.tensor_tensor(
            out=ROWS[32:33, NTOK:2 * NTOK], in0=ROWS[64:65, 0:NTOK],
            in1=ROWS[64:65, 0:NTOK], op=ALU.mult))
        v2tok = bld.op("vector", "VE", lambda e: e.tensor_tensor(
            out=ROWS[96:97, 0:NTOK], in0=ROWS[32:33, 0:NTOK],
            in1=ROWS[32:33, NTOK:2 * NTOK], op=ALU.mult))
        bld.wait("scalar", "VE", v2tok)
        bld.op("scalar", "SC", lambda e: e.activation(
            out=ROWS[96:97, 0:NTOK], in_=ROWS[96:97, 0:NTOK], func=AF.Ln,
            bias=VEC[0:1, _vc("eps"):_vc("eps") + 1]))
        stok = bld.op("scalar", "SC", lambda e: e.activation(
            out=ROWS[64:65, NTOK:2 * NTOK], in_=ROWS[96:97, 0:NTOK], func=AF.Exp,
            scale=VEC[0:1, _vc("neghalf"):_vc("neghalf") + 1]))
        bld.wait("vector", "SC", stok)
        stok2 = bld.op("vector", "VE", lambda e: e.tensor_tensor(
            out=ROWS[96:97, 0:NTOK], in0=ROWS[64:65, 0:NTOK],
            in1=ROWS[64:65, NTOK:2 * NTOK], op=ALU.mult))
        rs_src = ROWS[96:97, 0:NTOK]
        bcast_wait = ("VE", stok2)
    else:
        rs_src = ROWS[64:65, 0:NTOK]
        bcast_wait = ("SC", stok)
    bld.wait("vector", bcast_wait[0], bcast_wait[1])
    rbtok = bld.op("vector", "VE", lambda e, s=rs_src: e.tensor_copy(out=LXB[0:1, 0:NTOK], in_=s))
    bld.wait("tensor", "VE", rbtok)
    btoks = []
    for w in range(NWIN):
        btoks.append(bld.op("tensor", "TN", lambda e, w=w: e.matmul(
            out=PB[:, w * W:(w + 1) * W], lhsT=ones_r[:], rhs=LXB[0:1, w * W:(w + 1) * W],
            start=True, stop=True)))
    bld.wait("vector", "TN", btoks[-1])
    if not apply:
        return {"applied": None}
    atoks = []
    for k in range(NCH):
        bld.op("vector", "VE", lambda e, k=k: e.tensor_tensor(
            out=LE1[:], in0=chs(XS, k), in1=PA[:, :], op=ALU.subtract))
        if gamma_beta is None:
            atoks.append(bld.op("vector", "VE", lambda e, k=k: e.tensor_tensor(
                out=chs(dst_bf, k), in0=LE1[:], in1=PB[:, :], op=ALU.mult)))
        else:
            gcol, bcol = gamma_beta
            bld.op("vector", "VE", lambda e, k=k: e.tensor_tensor(
                out=LE1[:], in0=LE1[:], in1=PB[:, :], op=ALU.mult))
            atoks.append(bld.op("vector", "VE", lambda e, k=k, g=gcol, b=bcol: e.tensor_scalar(
                out=chs(dst_bf, k), in0=LE1[:],
                scalar1=VEC[:, _vc(g + str(k)):_vc(g + str(k)) + 1],
                scalar2=VEC[:, _vc(b + str(k)):_vc(b + str(k)) + 1],
                op0=ALU.mult, op1=ALU.add)))
    return {"applied": atoks[-1], "atoks": atoks}


def emit_l0ffn(bld, T, XS, XN, PA, PB, pool, tag):
    HACT = T["HACT"]
    WG = T["WG"]; WU = T["WU"]
    WD = T["WD"]
    TMP = T["TMPW"]
    wg_h, wu_h, wd_h = T["wgT_h"], T["wuT_h"], T["wdT_h"]
    VEC = T["VEC"]

    wd_used = None
    used_tok = [None, None]
    for piece in range(4):
        if wd_used is not None:
            bld.wait("gpsimd", "TN", wd_used)
        wd_tok = bld.dma(
            "gpsimd", "LG", WD[:].rearrange("p (a c) -> p a c", c=1024),
            wd_h[piece * 6:(piece + 1) * 6, :, :].rearrange("a p c -> p a c"))
        mtok = None
        for mc in range(6):
            hc = piece * 6 + mc
            par = mc % 2
            if used_tok[par] is not None:
                bld.wait("gpsimd", "TN", used_tok[par])
            bld.dma("gpsimd", "LG", WG[par][:].rearrange("p (a c) -> p a c", c=128),
                    wg_h[:, :, hc * 128:(hc + 1) * 128].rearrange("a p c -> p a c"))
            utok = bld.dma("gpsimd", "LG", WU[par][:].rearrange("p (a c) -> p a c", c=128),
                    wu_h[:, :, hc * 128:(hc + 1) * 128].rearrange("a p c -> p a c"))
            bld.wait("tensor", "LG", utok)
            for w in range(NWIN):
                ja, pa = pool.acq()
                jb, pb = pool.acq()
                for k in range(NCH):
                    bld.opn("tensor", lambda e, p=pa, par=par, k=k, w=w: e.matmul(
                        out=p, lhsT=WG[par][:, k * 128:(k + 1) * 128], rhs=chs(XN, k, w),
                        start=(k == 0), stop=(k == NCH - 1)))
                for k in range(NCH):
                    fn = lambda e, p=pb, par=par, k=k, w=w: e.matmul(
                        out=p, lhsT=WU[par][:, k * 128:(k + 1) * 128], rhs=chs(XN, k, w),
                        start=(k == 0), stop=(k == NCH - 1))
                    if k == NCH - 1:
                        mtok = bld.op("tensor", "TN", fn)
                    else:
                        bld.opn("tensor", fn)
                used_tok[par] = mtok
                bld.wait("scalar", "TN", mtok)
                if T.get("prev_htok") is not None:
                    bld.wait("scalar", "VE", T["prev_htok"])
                sttok = bld.op("scalar", "SC", lambda e, p=pa: e.activation(
                    out=TMP, in_=p, func=AF.Silu, scale=VEC[:, _vc("s127"):_vc("s127") + 1]))
                bld.wait("vector", "SC", sttok)
                htok = bld.op("vector", "VE", lambda e, p=pb, mc=mc, w=w: e.scalar_tensor_tensor(
                    out=HACT[:, mc * NTOK + w * W: mc * NTOK + w * W + W],
                    in0=p, scalar=float(S127), in1=TMP, op0=ALU.mult, op1=ALU.mult))
                T["prev_htok"] = htok
                pool.rel(ja, "SC", sttok)
                pool.rel(jb, "VE", htok)
        bld.wait("tensor", "LG", wd_tok)
        bld.wait("tensor", "VE", T["prev_htok"])
        for w in range(NWIN):
            for dmc in range(NCH):
                ja, pa = pool.acq()
                for kk in range(6):
                    fn = lambda e, p=pa, kk=kk, dmc=dmc, w=w: e.matmul(
                        out=p, lhsT=WD[:, kk * 1024 + dmc * 128: kk * 1024 + (dmc + 1) * 128],
                        rhs=HACT[:, kk * NTOK + w * W: kk * NTOK + w * W + W],
                        start=(kk == 0), stop=(kk == 5))
                    if kk == 5:
                        mtok = bld.op("tensor", "TN", fn)
                    else:
                        bld.opn("tensor", fn)
                wd_used = mtok
                bld.wait("vector", "TN", mtok)
                vtok = bld.op("vector", "VE", lambda e, p=pa, dmc=dmc, w=w: e.scalar_tensor_tensor(
                    out=chs(XS, dmc, w), in0=p, scalar=float(0.5 * S127),
                    in1=chs(XS, dmc, w), op0=ALU.mult, op1=ALU.add))
                pool.rel(ja, "VE", vtok)


def emit_qzproj(bld, T, XN, UT, dst, pool, scale):
    for mo in range(NCH):
        for w in range(NWIN):
            ja, pa = pool.acq()
            for k in range(NCH):
                fn = lambda e, p=pa, k=k, mo=mo, w=w: e.matmul(
                    out=p, lhsT=UT[:, k * 1024 + mo * 128:k * 1024 + (mo + 1) * 128],
                    rhs=chs(XN, k, w), start=(k == 0), stop=(k == NCH - 1))
                if k == NCH - 1:
                    mtok = bld.op("tensor", "TN", fn)
                else:
                    bld.opn("tensor", fn)
            bld.wait("vector", "TN", mtok)
            vtok = bld.op("vector", "VE", lambda e, p=pa, mo=mo, w=w: e.tensor_scalar(
                out=chs(dst, mo, w), in0=p, scalar1=float(scale), scalar2=None, op0=ALU.mult))
            pool.rel(ja, "VE", vtok)


def emit_score_pass(bld, T, b, STAT, MOV, STT, PF, PV, QH, VOUT, tag):
    b0 = b * SEQ
    bld.wait("scalar", "TN", bld.cnt["TN"])
    bld.wait("tensor", "SC", bld.cnt["SC"])
    sigtok = [None, None]
    vlast = [None, None]
    vtok_hist = {}
    evactok = [None, None]
    for J in range(8):
        jp = J % 2
        pv = PV[:, jp * 512:(jp + 1) * 512]
        if evactok[jp] is not None:
            bld.wait("tensor", "VE", evactok[jp])
        pend = None
        for i in range(32):
            par = i % 2
            fb = PF[:, par * 1024:(par + 1) * 1024]
            if sigtok[par] is not None:
                bld.wait("tensor", "SC", sigtok[par])
            bld.opn("tensor", lambda e, fb=fb, i=i, J=J: e.matmul(
                out=fb[:, 0:512], lhsT=STAT[0:64, b0 + i * 128:b0 + (i + 1) * 128],
                rhs=MOV[0:64, b0 + J * 512:b0 + (J + 1) * 512],
                start=True, stop=True, tile_position=(0, 0), skip_group_check=True))
            ftok = bld.op("tensor", "TN", lambda e, fb=fb, i=i, J=J: e.matmul(
                out=fb[:, 512:1024], lhsT=STAT[64:128, b0 + i * 128:b0 + (i + 1) * 128],
                rhs=MOV[64:128, b0 + J * 512:b0 + (J + 1) * 512],
                start=True, stop=True, tile_position=(64, 0), skip_group_check=True))
            if pend is not None:
                pend()
            bld.wait("scalar", "TN", ftok)
            if vlast[par] is not None:
                bld.wait("scalar", "TN", vlast[par])
            sigtok[par] = bld.op("scalar", "SC", lambda e, fb=fb, par=par: e.activation(
                out=QH[:, par * 1024:(par + 1) * 1024], in_=fb, func=AF.Sigmoid))

            def mk_pend(i=i, J=J, par=par, pv=pv, st=sigtok[par]):
                def go():
                    bld.wait("tensor", "SC", st)
                    bld.opn("tensor", lambda e: e.matmul(
                        out=pv[0:64, :], lhsT=STT[:, i * 128:i * 128 + 64],
                        rhs=QH[:, par * 1024:par * 1024 + 512],
                        start=(i == 0), stop=False, tile_position=(0, 0),
                        skip_group_check=True))
                    vlast[par] = vtok_hist[(J, i)] = bld.op("tensor", "TN", lambda e: e.matmul(
                        out=pv[64:128, :], lhsT=STT[:, i * 128 + 64:(i + 1) * 128],
                        rhs=QH[:, par * 1024 + 512:(par + 1) * 1024],
                        start=(i == 0), stop=(i == 31), tile_position=(0, 64),
                        skip_group_check=True))
                return go
            pend = mk_pend()
        pend()
        bld.wait("vector", "TN", vtok_hist[(J, 31)])
        evactok[jp] = bld.op("vector", "VE", lambda e, pv=pv, J=J: e.tensor_copy(
            out=VOUT[:, J * 512:(J + 1) * 512], in_=pv))
    return evactok


def emit_l0attn(bld, T, XS, XN, dr, sub_es, ONESM, ONESR, IDENT, VEC):
    nc = bld.nc
    bld.wait("tensor", "VE", bld.cnt["VE"])
    with ExitStack() as es:
        PAa = es.enter_context(nc.psum_tensor("PAa", [128, 2048], F32))
        PBa = es.enter_context(nc.psum_tensor("PBa", [128, 2048], F32))
        UT = es.enter_context(nc.sbuf_tensor("UT2", [128, 8 * 1024], BF16))
        VT = es.enter_context(nc.sbuf_tensor("VT2", [128, 8 * 1024], BF16))
        QZU = es.enter_context(nc.sbuf_tensor("QZU", [128, NCH * NTOK], BF16))
        QZV = es.enter_context(nc.sbuf_tensor("QZV", [128, NCH * NTOK], BF16))
        pool = Pool(bld, [PAa[:, i * W:(i + 1) * W] for i in range(4)]
                    + [PBa[:, i * W:(i + 1) * W] for i in range(4)])
        ut_tok = bld.dma("gpsimd", "LG", UT[:].rearrange("p (a c) -> p a c", c=1024),
                         dr["uT"].rearrange("a p c -> p a c"))
        vt_tok = bld.dma("gpsimd", "LG", VT[:].rearrange("p (a c) -> p a c", c=1024),
                         dr["vT"].rearrange("a p c -> p a c"))
        bld.wait("tensor", "LG", vt_tok)
        emit_qzproj(bld, T, XN, UT, QZU, pool, SC_ATT)
        emit_qzproj(bld, T, XN, VT, QZV, pool, SC_ATT)
        stok = None
        bld.wait("sync", "VE", bld.cnt["VE"])
        if DBG == "qz":
            for kk in range(8):
                bld.dma("sync", "ST", dr["dbg"][kk], chs(QZU, kk))
                bld.dma("sync", "ST", dr["dbg"][8 + kk], chs(QZV, kk))
        for mo in range(NCH):
            stok = bld.dma("sync", "LD", dr["ag1_in"][0, mo], chs(QZU, mo))
            stok = bld.dma("sync", "LD", dr["ag1_in"][1, mo], chs(QZV, mo))
        bld.wait("gpsimd", "LD", stok)
        ag1tok = bld.op("gpsimd", "GP", lambda e: e.collective_compute(
            "AllGather", ALU.bypass, replica_groups=[list(range(8))],
            ins=[dr["ag1_in"]], outs=[dr["ag1_out"]]))
        bld.barrier("sub2a")
    if ATTN_STOP == "A":
        return
    with ExitStack() as es:
        PF = es.enter_context(nc.psum_tensor("PF", [128, 2048], F32))
        PV = es.enter_context(nc.psum_tensor("PV", [128, 1024], F32))
        PT = es.enter_context(nc.psum_tensor("PT", [128, 1024], F32))
        QMU = es.enter_context(nc.sbuf_tensor("QMU", [128, 16384], BF16))
        QMV = es.enter_context(nc.sbuf_tensor("QMV", [128, 16384], BF16))
        QUT = es.enter_context(nc.sbuf_tensor("QUT", [128, 4096], BF16))
        QVT = es.enter_context(nc.sbuf_tensor("QVT", [128, 4096], BF16))
        QH = es.enter_context(nc.sbuf_tensor("QH", [128, 2048], BF16))
        VOUT = es.enter_context(nc.sbuf_tensor("VOUT", [128, 4096], BF16))
        bld.wait("sync", "GP", ag1tok)
        rtok = None
        def dynread(e, tgt, dst):
            if "cid" not in T:
                T["cid"] = e.partition_id()
            return e.dma_start(out=dst[:].rearrange("p (a c) -> p a c", c=NTOK),
                               in_=dr["ag1_out"][:, tgt, T["cid"], :, :].rearrange("a p c -> p a c"))
        bld.op("sync", "LD", lambda e: dynread(e, 0, QMU), inc=16)
        rtok = bld.op("sync", "LD", lambda e: dynread(e, 1, QMV), inc=16)
        for e_ in ("tensor", "vector", "scalar"):
            bld.wait(e_, "LD", rtok)
        if DBG == "qm":
            bld.wait("sync", "LD", rtok)
            for kk in range(8):
                bld.dma("sync", "ST", dr["dbg"][kk], QMU[:, kk * NTOK:(kk + 1) * NTOK])
                bld.dma("sync", "ST", dr["dbg"][8 + kk], QMV[:, kk * NTOK:(kk + 1) * NTOK])
        if ATTN_STOP == "B0":
            bld.barrier("sub2b0")
            return
        a2stage = None
        nbatch = 1 if (ATTN_STOP in ("B1", "B2", "B3")) else 4
        for b in range(nbatch):
            slotve = [None, None]
            ti = 0
            for src_t, dst_t in ((QMU, QUT), (QMV, QVT)):
                for tc in range(32):
                    sl = ti % 2
                    ti += 1
                    if slotve[sl] is not None:
                        bld.wait("tensor", "VE", slotve[sl])
                    ttok = bld.op("tensor", "TN", lambda e, sl=sl, tc=tc, s=src_t, b=b: e.matmul(
                        out=PT[:, sl * 512:sl * 512 + 128],
                        lhsT=s[:, b * SEQ + tc * 128:b * SEQ + (tc + 1) * 128],
                        rhs=IDENT[:], start=True, stop=True, skip_group_check=True))
                    bld.wait("vector", "TN", ttok)
                    if DBG == "one" and b == 0 and ti == 1:
                        e1 = bld.op("vector", "VE", lambda e: e.tensor_copy(
                            out=QH[:, 0:128], in_=PT[:, 0:128]))
                        e2 = bld.op("vector", "VE", lambda e: e.tensor_copy(
                            out=QH[:, 1024:1024 + 128], in_=QMU[:, 0:128]))
                        e2 = bld.op("vector", "VE", lambda e: e.tensor_copy(
                            out=QH[:, 512:512 + 128], in_=IDENT[:]))
                        bld.wait("sync", "VE", e2)
                        bld.dma("sync", "ST", dr["dbg"][0], QH[:, 0:NTOK])
                        sttk = bld.dma("sync", "ST", dr["dbg"][1], QUT[:, 0:NTOK])
                        for e_ in ("tensor", "vector"):
                            bld.wait(e_, "ST", sttk)
                    if DBG == "copybypass":
                        slotve[sl] = bld.op("vector", "VE", lambda e, sl=sl, tc=tc, d=dst_t, s=src_t, b=b: e.tensor_copy(
                            out=d[:, tc * 128:(tc + 1) * 128], in_=s[:, b * SEQ + tc * 128:b * SEQ + (tc + 1) * 128]))
                    else:
                        slotve[sl] = bld.op("vector", "VE", lambda e, sl=sl, tc=tc, d=dst_t: e.tensor_copy(
                            out=d[:, tc * 128:(tc + 1) * 128], in_=PT[:, sl * 512:sl * 512 + 128]))
            bld.wait("tensor", "VE", bld.cnt["VE"])
            if DBG == "memtest" and b == 0:
                mtok = bld.op("vector", "VE", lambda e: e.memset(QUT[:], 3.25))
                bld.wait("sync", "VE", mtok)
                bld.dma("sync", "ST", dr["dbg"][0], QUT[:, 0:NTOK])
                for e_ in ("tensor", "vector"):
                    bld.wait(e_, "ST", bld.cnt["ST"])
            if DBG == "slicetest" and b == 0:
                bld.barrier("dbgbar")
                m1 = bld.op("vector", "VE", lambda e: e.memset(QUT[:, 0:128], 7.0))
                m2 = bld.op("vector", "VE", lambda e: e.tensor_copy(
                    out=QUT[:, 128:256], in_=QMU[:, 0:128]))
                m3 = bld.op("vector", "VE", lambda e: e.tensor_copy(
                    out=QUT[:, 256:384], in_=PT[:, 0:128]))
                bld.wait("sync", "VE", m3)
                sttk = bld.dma("sync", "ST", dr["dbg"][0], QUT[:, 0:NTOK])
                for e_ in ("tensor", "vector"):
                    bld.wait(e_, "ST", sttk)
            if DBG in ("qut", "copybypass") and b == 0:
                bld.wait("sync", "VE", bld.cnt["VE"])
                for kk in range(2):
                    bld.dma("sync", "ST", dr["dbg"][kk], QUT[:, kk * NTOK:(kk + 1) * NTOK])
                    bld.dma("sync", "ST", dr["dbg"][2 + kk], QVT[:, kk * NTOK:(kk + 1) * NTOK])
                for e_ in ("tensor", "vector"):
                    bld.wait(e_, "ST", bld.cnt["ST"])
            if ATTN_STOP == "B1":
                bld.barrier("sub2b1")
                return
            emit_score_pass(bld, T, b, QMU, QMV, QUT, PF, PV, QH, VOUT, f"p1b{b}")
            bld.wait("sync", "VE", bld.cnt["VE"])
            if DBG == "vout" and b == 0:
                for kk in range(2):
                    bld.dma("sync", "ST", dr["dbg"][4 + kk], VOUT[:, kk * NTOK:(kk + 1) * NTOK])
                for e_ in ("tensor", "vector"):
                    bld.wait(e_, "ST", bld.cnt["ST"])
            a2stage = bld.dma("sync", "LD", dr["ag2_in"][1, :, b * SEQ:(b + 1) * SEQ], VOUT[:])
            bld.wait("vector", "LD", a2stage)
            if ATTN_STOP == "B2":
                bld.barrier("sub2b2")
                return
            emit_score_pass(bld, T, b, QMV, QMU, QVT, PF, PV, QH, VOUT, f"p2b{b}")
            bld.wait("sync", "VE", bld.cnt["VE"])
            a2stage = bld.dma("sync", "LD", dr["ag2_in"][0, :, b * SEQ:(b + 1) * SEQ], VOUT[:])
            bld.wait("vector", "LD", a2stage)
        if ATTN_STOP in ("B", "B3"):
            bld.barrier("sub2bx")
            return
        bld.wait("gpsimd", "LD", a2stage)
        ag2tok = bld.op("gpsimd", "GP", lambda e: e.collective_compute(
            "AllGather", ALU.bypass, replica_groups=[list(range(8))],
            ins=[dr["ag2_in"]], outs=[dr["ag2_out"]]))
        bld.barrier("sub2b")
    with ExitStack() as es:
        PAc = es.enter_context(nc.psum_tensor("PAc", [128, 2048], F32))
        PBc = es.enter_context(nc.psum_tensor("PBc", [128, 2048], F32))
        V1F = es.enter_context(nc.sbuf_tensor("V1F", [128, NCH * NTOK], BF16))
        V2F = es.enter_context(nc.sbuf_tensor("V2F", [128, NCH * NTOK], BF16))
        UB = es.enter_context(nc.sbuf_tensor("UB", [128, 8 * 1024], BF16))
        VB = es.enter_context(nc.sbuf_tensor("VB", [128, 8 * 1024], BF16))
        pool = Pool(bld, [PAc[:, i * W:(i + 1) * W] for i in range(4)]
                    + [PBc[:, i * W:(i + 1) * W] for i in range(4)])
        bld.wait("sync", "GP", ag2tok)

        def dynread2(e, tgt, dst):
            return e.dma_start(out=dst[:].rearrange("p (a c) -> p a c", c=NTOK),
                               in_=dr["ag2_out"][:, tgt, :, bass.ts(T["cid"], NTOK)].rearrange("a p c -> p a c"))
        bld.op("sync", "LD", lambda e: dynread2(e, 0, V1F), inc=16)
        rtok = bld.op("sync", "LD", lambda e: dynread2(e, 1, V2F), inc=16)
        if DBG == "v1f":
            bld.wait("sync", "LD", rtok)
            for kk in range(8):
                bld.dma("sync", "ST", dr["dbg"][kk], V1F[:, kk * NTOK:(kk + 1) * NTOK])
                bld.dma("sync", "ST", dr["dbg"][8 + kk], V2F[:, kk * NTOK:(kk + 1) * NTOK])
        ub_tok = bld.dma("gpsimd", "LG", UB[:].rearrange("p (a c) -> p a c", c=1024),
                         dr["u_t"].rearrange("a p c -> p a c"))
        vb_tok = bld.dma("gpsimd", "LG", VB[:].rearrange("p (a c) -> p a c", c=1024),
                         dr["v_t"].rearrange("a p c -> p a c"))
        bld.wait("tensor", "LD", rtok)
        bld.wait("tensor", "LG", vb_tok)
        for mo in range(NCH):
            for w in range(NWIN):
                ja, pa = pool.acq()
                for k in range(NCH):
                    bld.opn("tensor", lambda e, p=pa, k=k, mo=mo, w=w: e.matmul(
                        out=p, lhsT=UB[:, k * 1024 + mo * 128:k * 1024 + (mo + 1) * 128],
                        rhs=chs(V1F, k, w), start=(k == 0), stop=False))
                for k in range(NCH):
                    fn = lambda e, p=pa, k=k, mo=mo, w=w: e.matmul(
                        out=p, lhsT=VB[:, k * 1024 + mo * 128:k * 1024 + (mo + 1) * 128],
                        rhs=chs(V2F, k, w), start=False, stop=(k == NCH - 1))
                    if k == NCH - 1:
                        mtok = bld.op("tensor", "TN", fn)
                    else:
                        bld.opn("tensor", fn)
                bld.wait("vector", "TN", mtok)
                vtok = bld.op("vector", "VE", lambda e, p=pa, mo=mo, w=w: e.scalar_tensor_tensor(
                    out=chs(XS, mo, w), in0=p, scalar=float(SC_ATT),
                    in1=chs(XS, mo, w), op0=ALU.mult, op1=ALU.add))
                pool.rel(ja, "VE", vtok)
        bld.barrier("sub2c")


def emit_l1ffn(bld, T, XS, XN, dr, ONESM, ONESR, VEC, ROWS, agx_in, agx_out, tag):
    nc = bld.nc
    mu_r0 = _vc("mu_r0")
    mu_k0 = _vc("mu_k0")
    sp0 = _vc("selprev0")
    with ExitStack() as es:
        PA = es.enter_context(nc.psum_tensor("PA" + tag, [128, 2048], F32))
        PB = es.enter_context(nc.psum_tensor("PB" + tag, [128, 2048], F32))
        XK = es.enter_context(nc.sbuf_tensor("XK" + tag, [128, NCH * NTOK], BF16))
        with ExitStack() as es2:
            LXB = es2.enter_context(nc.sbuf_tensor("LXB" + tag, [128, 2 * NTOK], BF16))
            LSQ = es2.enter_context(nc.sbuf_tensor("LSQ" + tag, [128, 2 * NTOK], BF16))
            LE1 = es2.enter_context(nc.sbuf_tensor("LE1" + tag, [128, NTOK], F32))
            XNC = es2.enter_context(nc.sbuf_tensor("XNC" + tag, [128, NTOK], F32))
            SML = es2.enter_context(nc.sbuf_tensor("SML" + tag, [128, 128], F32))
            emit_ln(bld, T, XS, PA, PB, None, ONESM, ONESR, VEC, LXB, LSQ, LE1,
                    ROWS, "ln" + tag, apply=False)
            bld.op("vector", "VE", lambda e: e.tensor_scalar(
                out=SML[:, 16:24], in0=VEC[:, mu_r0:mu_r0 + 8], scalar1=1.0, scalar2=None,
                op0=ALU.subtract))
            bld.op("vector", "VE", lambda e: e.tensor_scalar(
                out=SML[:, 24:32], in0=VEC[:, mu_k0:mu_k0 + 8], scalar1=1.0, scalar2=None,
                op0=ALU.subtract))
            atoks = []
            for k in range(NCH):
                bld.op("vector", "VE", lambda e, k=k: e.tensor_tensor(
                    out=LE1[:], in0=chs(XS, k), in1=PA[:, :], op=ALU.subtract))
                bld.op("vector", "VE", lambda e, k=k: e.tensor_tensor(
                    out=XNC[:], in0=LE1[:], in1=PB[:, :], op=ALU.mult))
                bld.op("vector", "VE", lambda e, k=k: e.tensor_copy(
                    out=SML[:, k:k + 1], in_=XNC[:, 0:1]))
                bld.op("vector", "VE", lambda e, k=k: e.tensor_copy(
                    out=SML[:, 8 + k:9 + k], in_=XNC[:, NTOK - 1:NTOK]))
                bld.op("vector", "VE", lambda e: e.tensor_tensor(
                    out=LE1[:, 1:NTOK], in0=XNC[:, 1:NTOK], in1=XNC[:, 0:NTOK - 1],
                    op=ALU.subtract))
                bld.op("vector", "VE", lambda e: e.memset(LE1[:, 0:1], 0.0))
                bld.op("vector", "VE", lambda e, k=k: e.scalar_tensor_tensor(
                    out=chs(XN, k), in0=LE1[:], scalar=SML[:, 16 + k:17 + k], in1=XNC[:],
                    op0=ALU.mult, op1=ALU.add))
                atoks.append(bld.op("vector", "VE", lambda e, k=k: e.scalar_tensor_tensor(
                    out=chs(XK, k), in0=LE1[:], scalar=SML[:, 24 + k:25 + k], in1=XNC[:],
                    op0=ALU.mult, op1=ALU.add)))
            bld.wait("sync", "VE", atoks[-1])
            stok = bld.dma("sync", "LD", agx_in, SML[:, 8:16])
            bld.wait("gpsimd", "LD", stok)
            agtok = bld.op("gpsimd", "GP", lambda e: e.collective_compute(
                "AllGather", ALU.bypass, replica_groups=[list(range(8))],
                ins=[agx_in], outs=[agx_out]))
            bld.wait("sync", "GP", agtok)
            rtok = bld.dma("sync", "LD", SML[:, 32:96].rearrange("p (s c) -> p s c", c=8),
                           agx_out.rearrange("s p c -> p s c"))
            bld.wait("vector", "LD", rtok)
            htok = bld.op("vector", "VE", lambda e: e.tensor_scalar(
                out=SML[:, 96:104], in0=SML[:, 32:40],
                scalar1=VEC[:, sp0:sp0 + 1], scalar2=None, op0=ALU.mult))
            for s in range(1, 8):
                htok = bld.op("vector", "VE", lambda e, s=s: e.scalar_tensor_tensor(
                    out=SML[:, 96:104], in0=SML[:, 32 + 8 * s:40 + 8 * s],
                    scalar=VEC[:, sp0 + s:sp0 + s + 1], in1=SML[:, 96:104],
                    op0=ALU.mult, op1=ALU.add))
            bld.op("vector", "VE", lambda e: e.tensor_tensor(
                out=SML[:, 104:112], in0=SML[:, 0:8], in1=SML[:, 96:104], op=ALU.subtract))
            bld.op("vector", "VE", lambda e: e.tensor_tensor(
                out=SML[:, 112:120], in0=SML[:, 104:112], in1=SML[:, 16:24], op=ALU.mult))
            for k in range(NCH):
                bld.op("vector", "VE", lambda e, k=k: e.tensor_tensor(
                    out=XN[:, k * NTOK:k * NTOK + 1], in0=SML[:, 112 + k:113 + k],
                    in1=SML[:, k:k + 1], op=ALU.add))
            bld.op("vector", "VE", lambda e: e.tensor_tensor(
                out=SML[:, 112:120], in0=SML[:, 104:112], in1=SML[:, 24:32], op=ALU.mult))
            for k in range(NCH):
                ptok = bld.op("vector", "VE", lambda e, k=k: e.tensor_tensor(
                    out=XK[:, k * NTOK:k * NTOK + 1], in0=SML[:, 112 + k:113 + k],
                    in1=SML[:, k:k + 1], op=ALU.add))
            bld.barrier(tag + "mix")
        with ExitStack() as es3:
            RB = es3.enter_context(nc.sbuf_tensor("RB" + tag, [128, NCH * NTOK], BF16))
            KA = XN
            WT = es3.enter_context(nc.sbuf_tensor("WT" + tag, [128, 8 * 1024], BF16))
            TMP = es3.enter_context(nc.sbuf_tensor("TMPX" + tag, [128, W], F32))
            TMP2 = es3.enter_context(nc.sbuf_tensor("TMPY" + tag, [128, W], F32))
            pool = Pool(bld, [PA[:, i * W:(i + 1) * W] for i in range(4)]
                        + [PB[:, i * W:(i + 1) * W] for i in range(4)])
            prev_sc = [None]
            prev_ve = [None]

            def proj(wdram, mov, evac):
                wtok = bld.dma("gpsimd", "LG", WT[:].rearrange("p (a c) -> p a c", c=1024),
                               wdram.rearrange("a p c -> p a c"))
                bld.wait("tensor", "LG", wtok)
                for mo in range(NCH):
                    for w in range(NWIN):
                        ja, pa = pool.acq()
                        for k in range(NCH):
                            fn = lambda e, p=pa, k=k, mo=mo, w=w: e.matmul(
                                out=p, lhsT=WT[:, k * 1024 + mo * 128:k * 1024 + (mo + 1) * 128],
                                rhs=chs(mov, k, w), start=(k == 0), stop=(k == NCH - 1))
                            if k == NCH - 1:
                                mtok = bld.op("tensor", "TN", fn)
                            else:
                                bld.opn("tensor", fn)
                        evac(mo, w, ja, pa, mtok)
                bld.wait("gpsimd", "TN", bld.cnt["TN"])

            def evac_r(mo, w, ja, pa, mtok):
                bld.wait("scalar", "TN", mtok)
                stok = bld.op("scalar", "SC", lambda e, p=pa, mo=mo, w=w: e.activation(
                    out=chs(RB, mo, w), in_=p, func=AF.Sigmoid))
                pool.rel(ja, "SC", stok)
            proj(dr["WrT"], XN, evac_r)

            def evac_k(mo, w, ja, pa, mtok):
                bld.wait("scalar", "TN", mtok)
                if prev_ve[0] is not None:
                    bld.wait("scalar", "VE", prev_ve[0])
                stok = bld.op("scalar", "SC", lambda e, p=pa: e.activation(
                    out=TMP[:], in_=p, func=AF.Relu))
                bld.wait("vector", "SC", stok)
                vtok = bld.op("vector", "VE", lambda e, mo=mo, w=w: e.tensor_tensor(
                    out=chs(KA, mo, w), in0=TMP[:], in1=TMP[:], op=ALU.mult))
                prev_ve[0] = vtok
                pool.rel(ja, "VE", vtok)
            proj(dr["WkT"], XK, evac_k)

            def evac_v(mo, w, ja, pa, mtok):
                bld.wait("vector", "TN", mtok)
                bld.op("vector", "VE", lambda e, p=pa, mo=mo, w=w: e.tensor_tensor(
                    out=TMP2[:], in0=chs(RB, mo, w), in1=p, op=ALU.mult))
                vtok = bld.op("vector", "VE", lambda e, mo=mo, w=w: e.scalar_tensor_tensor(
                    out=chs(XS, mo, w), in0=TMP2[:], scalar=0.5, in1=chs(XS, mo, w),
                    op0=ALU.mult, op1=ALU.add))
                pool.rel(ja, "VE", vtok)
            proj(dr["WvT"], KA, evac_v)
    bld.barrier(tag)


def emit_l1attn(bld, T, XS, XN, dr, ONESM, ONESR, VEC, ROWS, tag="s5"):
    nc = bld.nc
    W5 = 256
    NW5 = NTOK // W5
    sp0 = _vc("selprev0")
    sn0 = _vc("selnext0")

    def vc(name, k):
        i = _vc(f"{name}{k}")
        return VEC[:, i:i + 1]

    with ExitStack() as es:
        MBC = es.enter_context(nc.sbuf_tensor("MBC", [128, NTOK], BF16))
        RSBC = es.enter_context(nc.sbuf_tensor("RSBC", [128, NTOK], BF16))
        SM5 = es.enter_context(nc.sbuf_tensor("SM5", [128, 256], F32))

        with ExitStack() as es2:
            PA5 = es2.enter_context(nc.psum_tensor("PA5", [128, 2048], F32))
            ROWS5 = es2.enter_context(nc.sbuf_tensor("ROWS5", [128, 2 * NTOK], F32))
            PB5 = es2.enter_context(nc.psum_tensor("PB5", [128, 2048], F32))
            LXB = es2.enter_context(nc.sbuf_tensor("LXB5", [128, 2 * NTOK], BF16))
            LSQ = es2.enter_context(nc.sbuf_tensor("LSQ5", [128, 2 * NTOK], BF16))
            LE1 = es2.enter_context(nc.sbuf_tensor("LE15", [128, NTOK], F32))
            emit_ln(bld, T, XS, PA5, PB5, XN, ONESM, ONESR, VEC,
                    LXB, LSQ, LE1, ROWS5, "ln5", gamma_beta=("gamma", "beta"),
                    combined=True)
            mt = bld.op("vector", "VE", lambda e: e.tensor_copy(out=MBC[:], in_=PA5[:, :]))
            rt = bld.op("vector", "VE", lambda e: e.tensor_copy(
                out=LXB[0:1, 0:NTOK], in_=ROWS5[64:65, 0:NTOK]))
            bld.wait("tensor", "VE", rt)
            bts = []
            for w in range(4):
                bts.append(bld.op("tensor", "TN", lambda e, w=w: e.matmul(
                    out=PB5[:, w * 512:(w + 1) * 512], lhsT=ONESR[:],
                    rhs=LXB[0:1, w * 512:(w + 1) * 512],
                    start=True, stop=True, skip_group_check=True)))
            bld.wait("vector", "TN", bts[-1])
            bld.op("vector", "VE", lambda e: e.tensor_copy(out=RSBC[:], in_=PB5[:, :]))
            bld.barrier(tag + "ln")
        PC = es.enter_context(nc.psum_tensor("PC5", [128, 2048], F32))
        PD = es.enter_context(nc.psum_tensor("PD5", [128, 1024], F32))
        pool = Pool(bld, [PC[:, 0:512], PC[:, 512:1024]])
        s2slot = [PC[:, 1024:1024 + W5], PC[:, 1536:1536 + W5]]
        XB2 = es.enter_context(nc.sbuf_tensor("XB2", [128, 2 * NTOK], F32))
        ZS2 = es.enter_context(nc.sbuf_tensor("ZS2", [128, 2 * NTOK], BF16))
        XACT = es.enter_context(nc.sbuf_tensor("XACT", [128, NTOK], F32))
        DELTA = es.enter_context(nc.sbuf_tensor("DELTA", [128, NTOK], F32))
        DBX = es.enter_context(nc.sbuf_tensor("DBX", [128, NTOK], F32))
        CMS = es.enter_context(nc.sbuf_tensor("CMS", [128, NTOK], F32))
        HSS = es.enter_context(nc.sbuf_tensor("HSS", [128, NTOK], F32))
        GATED = es.enter_context(nc.sbuf_tensor("GATED", [128, NTOK], BF16))
        GW = es.enter_context(nc.sbuf_tensor("GW", [128, NTOK], BF16))
        WSM = es.enter_context(nc.sbuf_tensor("WSM", [128, 8 * 96], F32))
        WDT = es.enter_context(nc.sbuf_tensor("WDT", [64, 1024], BF16))
        WBP = es.enter_context(nc.sbuf_tensor("WBP", [16, 1024], BF16))
        WCP = es.enter_context(nc.sbuf_tensor("WCP", [16, 1024], BF16))
        MB = es.enter_context(nc.sbuf_tensor("MB", [64, W5], BF16))
        WB = [es.enter_context(nc.sbuf_tensor(f"WB5_{i}", [128, 1024], BF16)) for i in range(2)]
        HLB = es.enter_context(nc.sbuf_tensor("HLB", [128, 16], BF16))
        TMP5 = es.enter_context(nc.sbuf_tensor("TMP5", [128, W5], F32))
        TMP6 = es.enter_context(nc.sbuf_tensor("TMP6", [128, W5], F32))

        for k in range(NCH):
            bld.op("vector", "VE", lambda e, k=k: e.tensor_copy(
                out=SM5[:, k:k + 1], in_=XN[:, k * NTOK:k * NTOK + 1]))
            bld.op("vector", "VE", lambda e, k=k: e.tensor_copy(
                out=SM5[:, 8 + k:9 + k], in_=XN[:, (k + 1) * NTOK - 1:(k + 1) * NTOK]))
        bld.wait("sync", "VE", bld.cnt["VE"])
        stok = bld.dma("sync", "LD", dr["agx5i"], SM5[:, 0:16])
        bld.wait("gpsimd", "LD", stok)
        agtok = bld.op("gpsimd", "GP", lambda e: e.collective_compute(
            "AllGather", ALU.bypass, replica_groups=[list(range(8))],
            ins=[dr["agx5i"]], outs=[dr["agx5o"]]))
        bld.wait("sync", "GP", agtok)
        rtok = bld.dma("sync", "LD", SM5[:, 16:144].rearrange("p (s c) -> p s c", c=16),
                       dr["agx5o"].rearrange("s p c -> p s c"))
        bld.wait("vector", "LD", rtok)
        bld.op("vector", "VE", lambda e: e.tensor_scalar(
            out=SM5[:, 144:152], in0=SM5[:, 24:32], scalar1=VEC[:, sp0:sp0 + 1],
            scalar2=None, op0=ALU.mult))
        bld.op("vector", "VE", lambda e: e.tensor_scalar(
            out=SM5[:, 152:160], in0=SM5[:, 16:24], scalar1=VEC[:, sn0:sn0 + 1],
            scalar2=None, op0=ALU.mult))
        for s in range(1, 8):
            bld.op("vector", "VE", lambda e, s=s: e.scalar_tensor_tensor(
                out=SM5[:, 144:152], in0=SM5[:, 16 + 16 * s + 8:16 + 16 * s + 16],
                scalar=VEC[:, sp0 + s:sp0 + s + 1], in1=SM5[:, 144:152],
                op0=ALU.mult, op1=ALU.add))
            bld.op("vector", "VE", lambda e, s=s: e.scalar_tensor_tensor(
                out=SM5[:, 152:160], in0=SM5[:, 16 + 16 * s:16 + 16 * s + 8],
                scalar=VEC[:, sn0 + s:sn0 + s + 1], in1=SM5[:, 152:160],
                op0=ALU.mult, op1=ALU.add))
        bld.op("vector", "VE", lambda e: e.tensor_copy(out=HLB[:, 0:8], in_=SM5[:, 144:152]))
        hlb_tok = bld.op("vector", "VE", lambda e: e.tensor_copy(out=HLB[:, 8:16], in_=SM5[:, 152:160]))
        wsmt = bld.dma("sync", "LD", WSM[:].rearrange("p (a c) -> p a c", c=96),
                       dr["Wsm"].rearrange("a p c -> p a c"))
        bld.dma("gpsimd", "LG", WDT[:], dr["W_dt"])
        bld.dma("gpsimd", "LG", WBP[:], dr["W_Bp"])
        wsmt2 = bld.dma("gpsimd", "LG", WCP[:], dr["W_Cp"])
        bld.wait("tensor", "VE", hlb_tok)
        wo_used = [None, None]
        for mo in range(NCH):
            par = mo % 2
            if wo_used[par] is not None:
                bld.wait("gpsimd", "TN", wo_used[par])
            wtok = bld.dma("gpsimd", "LG", WB[par][:].rearrange("p (a c) -> p a c", c=128),
                           dr["W_in"][:, :, mo * 128:(mo + 1) * 128].rearrange("a p c -> p a c"))
            bld.wait("tensor", "LG", wtok)
            for k in range(NCH):
                fn = lambda e, par=par, k=k, mo=mo: e.matmul(
                    out=PD[:, 512 + 2 * mo:514 + 2 * mo],
                    lhsT=WB[par][:, k * 128:(k + 1) * 128],
                    rhs=HLB[:, k:16:8], start=(k == 0), stop=(k == NCH - 1),
                    skip_group_check=True)
                if k == NCH - 1:
                    mtok = bld.op("tensor", "TN", fn)
                else:
                    bld.opn("tensor", fn)
            wo_used[par] = mtok
        bld.wait("vector", "TN", bld.cnt["TN"])
        bld.op("vector", "VE", lambda e: e.tensor_copy(
            out=SM5[:, 184:200], in_=PD[:, 512:528]))
        bld.barrier(tag + "pre")

        def xbzs(wi):
            par = wi % 2
            for mo in range(16):
                pmo = mo % 8
                wpar = mo % 2
                if wo_used[wpar] is not None:
                    bld.wait("gpsimd", "TN", wo_used[wpar])
                wtok = bld.dma("gpsimd", "LG", WB[wpar][:].rearrange("p (a c) -> p a c", c=128),
                               dr["W_in"][:, :, mo * 128:(mo + 1) * 128].rearrange("a p c -> p a c"))
                bld.wait("tensor", "LG", wtok)
                ja, pa = pool.acq()
                for k in range(NCH):
                    fn = lambda e, p=pa, wpar=wpar, k=k, wi=wi: e.matmul(
                        out=p[:, 0:W5], lhsT=WB[wpar][:, k * 128:(k + 1) * 128],
                        rhs=XN[:, k * NTOK + wi * W5:k * NTOK + (wi + 1) * W5],
                        start=(k == 0), stop=(k == NCH - 1), skip_group_check=True)
                    if k == NCH - 1:
                        mtok = bld.op("tensor", "TN", fn)
                    else:
                        bld.opn("tensor", fn)
                wo_used[wpar] = mtok
                if mo < 8:
                    bld.wait("vector", "TN", mtok)
                    vtok = bld.op("vector", "VE", lambda e, p=pa, par=par, pmo=pmo: e.tensor_copy(
                        out=XB2[:, par * NTOK + pmo * W5:par * NTOK + (pmo + 1) * W5],
                        in_=p[:, 0:W5]))
                    pool.rel(ja, "VE", vtok)
                else:
                    bld.wait("scalar", "TN", mtok)
                    if pmo == 0:
                        bld.wait("scalar", "VE", bld.cnt["VE"])
                    stok = bld.op("scalar", "SC", lambda e, p=pa, par=par, pmo=pmo: e.activation(
                        out=ZS2[:, par * NTOK + pmo * W5:par * NTOK + (pmo + 1) * W5],
                        in_=p[:, 0:W5], func=AF.Silu))
                    pool.rel(ja, "SC", stok)

        def stage2(stat_t, stat_cols, consumer):
            free = [None, None]
            for mo in range(NCH):
                par = mo % 2
                if free[par] is not None:
                    bld.wait("tensor", free[par][0], free[par][1])
                mt = bld.op("tensor", "TN", lambda e, mo=mo, par=par: e.matmul(
                    out=s2slot[par], lhsT=stat_t[:, mo * 128:(mo + 1) * 128],
                    rhs=MB[0:stat_cols, :], start=True, stop=True, skip_group_check=True))
                free[par] = consumer(mo, s2slot[par], mt)
            bld.wait("tensor", free[0][0], free[0][1])
            bld.wait("tensor", free[1][0], free[1][1])

        xbzs(0)
        for wi in range(NW5):
            par = wi % 2
            if wi < NW5 - 1:
                xbzs(wi + 1)
            bld.wait("vector", "SC", bld.cnt["SC"])
            bld.wait("vector", "TN", bld.cnt["TN"])
            xb_c = lambda k, par=par: XB2[:, par * NTOK + k * W5:par * NTOK + (k + 1) * W5]
            xb_n = lambda k, par=par: XB2[:, (1 - par) * NTOK + k * W5:(1 - par) * NTOK + (k + 1) * W5]
            for k in range(NCH):
                bld.op("vector", "VE", lambda e, k=k, par=par: e.tensor_scalar(
                    out=XACT[:, k * W5:(k + 1) * W5], in0=xb_c(k, par),
                    scalar1=vc("cw1", k), scalar2=vc("conv_b", k), op0=ALU.mult, op1=ALU.add))
                bld.op("vector", "VE", lambda e, k=k, par=par: e.scalar_tensor_tensor(
                    out=XACT[:, k * W5 + 1:(k + 1) * W5], in0=xb_c(k, par)[:, 0:W5 - 1],
                    scalar=vc("cw0", k), in1=XACT[:, k * W5 + 1:(k + 1) * W5],
                    op0=ALU.mult, op1=ALU.add))
                if wi == 0:
                    bld.op("vector", "VE", lambda e, k=k: e.scalar_tensor_tensor(
                        out=XACT[:, k * W5:k * W5 + 1], in0=SM5[:, 184 + 2 * k:185 + 2 * k],
                        scalar=vc("cw0", k), in1=XACT[:, k * W5:k * W5 + 1],
                        op0=ALU.mult, op1=ALU.add))
                else:
                    bld.op("vector", "VE", lambda e, k=k: e.scalar_tensor_tensor(
                        out=XACT[:, k * W5:k * W5 + 1],
                        in0=SM5[:, 200 + k:201 + k],
                        scalar=vc("cw0", k), in1=XACT[:, k * W5:k * W5 + 1],
                        op0=ALU.mult, op1=ALU.add))
                bld.op("vector", "VE", lambda e, k=k, par=par: e.tensor_copy(
                    out=SM5[:, 200 + k:201 + k], in_=xb_c(k, par)[:, W5 - 1:W5]))
                bld.op("vector", "VE", lambda e, k=k, par=par: e.scalar_tensor_tensor(
                    out=XACT[:, k * W5:(k + 1) * W5 - 1], in0=xb_c(k, par)[:, 1:W5],
                    scalar=vc("cw2", k), in1=XACT[:, k * W5:(k + 1) * W5 - 1],
                    op0=ALU.mult, op1=ALU.add))
                if wi == NW5 - 1:
                    ctok = bld.op("vector", "VE", lambda e, k=k: e.scalar_tensor_tensor(
                        out=XACT[:, (k + 1) * W5 - 1:(k + 1) * W5], in0=SM5[:, 185 + 2 * k:186 + 2 * k],
                        scalar=vc("cw2", k), in1=XACT[:, (k + 1) * W5 - 1:(k + 1) * W5],
                        op0=ALU.mult, op1=ALU.add))
                else:
                    ctok = bld.op("vector", "VE", lambda e, k=k, par=par: e.scalar_tensor_tensor(
                        out=XACT[:, (k + 1) * W5 - 1:(k + 1) * W5], in0=xb_n(k, par)[:, 0:1],
                        scalar=vc("cw2", k), in1=XACT[:, (k + 1) * W5 - 1:(k + 1) * W5],
                        op0=ALU.mult, op1=ALU.add))
            bld.wait("scalar", "VE", ctok)
            sil = bld.op("scalar", "SC", lambda e: e.activation(
                out=XACT[:], in_=XACT[:], func=AF.Silu))
            bld.wait("tensor", "SC", sil)
            bld.wait("tensor", "LD", bld.cnt["LD"])
            bld.wait("tensor", "LG", bld.cnt["LG"])
            for k in range(NCH):
                fn = lambda e, k=k: e.matmul(
                    out=PD[0:64, 0:W5], lhsT=WSM[:, k * 96:k * 96 + 64],
                    rhs=XACT[:, k * W5:(k + 1) * W5],
                    start=(k == 0), stop=(k == NCH - 1), skip_group_check=True)
                if k == NCH - 1:
                    mtok = bld.op("tensor", "TN", fn)
                else:
                    bld.opn("tensor", fn)
            bld.wait("vector", "TN", mtok)
            mbt = bld.op("vector", "VE", lambda e: e.tensor_copy(out=MB[:], in_=PD[0:64, 0:W5]))
            bld.wait("tensor", "VE", mbt)

            def cons_delta(mo, slot, mt):
                bld.wait("scalar", "TN", mt)
                bld.op("scalar", "SC", lambda e, slot=slot, mo=mo: e.activation(
                    out=TMP5[:], in_=slot, func=AF.Exp, bias=vc("b_dt", mo)))
                st = bld.op("scalar", "SC", lambda e, mo=mo: e.activation(
                    out=DELTA[:, mo * W5:(mo + 1) * W5], in_=TMP5[:],
                    func=AF.Ln, bias=VEC[:, _vc("one"):_vc("one") + 1]))
                return ("SC", st)
            stage2(WDT, 64, cons_delta)
            for k in range(NCH):
                fn = lambda e, k=k: e.matmul(
                    out=PD[0:16, 0:W5], lhsT=WSM[:, k * 96 + 64:k * 96 + 80],
                    rhs=XACT[:, k * W5:(k + 1) * W5],
                    start=(k == 0), stop=(k == NCH - 1), skip_group_check=True)
                if k == NCH - 1:
                    mtok = bld.op("tensor", "TN", fn)
                else:
                    bld.opn("tensor", fn)
            bld.wait("vector", "TN", mtok)
            mbt = bld.op("vector", "VE", lambda e: e.tensor_copy(out=MB[0:16, :], in_=PD[0:16, 0:W5]))
            bld.wait("tensor", "VE", mbt)

            def cons_b(mo, slot, mt):
                bld.wait("vector", "TN", mt)
                bld.op("vector", "VE", lambda e, slot=slot, mo=mo: e.tensor_tensor(
                    out=TMP5[:], in0=DELTA[:, mo * W5:(mo + 1) * W5], in1=slot, op=ALU.mult))
                vt = bld.op("vector", "VE", lambda e, mo=mo: e.tensor_tensor(
                    out=DBX[:, mo * W5:(mo + 1) * W5], in0=TMP5[:],
                    in1=XACT[:, mo * W5:(mo + 1) * W5], op=ALU.mult))
                return ("VE", vt)
            stage2(WBP, 16, cons_b)
            for k in range(NCH):
                fn = lambda e, k=k: e.matmul(
                    out=PD[0:16, 0:W5], lhsT=WSM[:, k * 96 + 80:k * 96 + 96],
                    rhs=XACT[:, k * W5:(k + 1) * W5],
                    start=(k == 0), stop=(k == NCH - 1), skip_group_check=True)
                if k == NCH - 1:
                    mtok = bld.op("tensor", "TN", fn)
                else:
                    bld.opn("tensor", fn)
            bld.wait("vector", "TN", mtok)
            mbt = bld.op("vector", "VE", lambda e: e.tensor_copy(out=MB[0:16, :], in_=PD[0:16, 0:W5]))
            bld.wait("tensor", "VE", mbt)

            def cons_c(mo, slot, mt):
                bld.wait("vector", "TN", mt)
                vt = bld.op("vector", "VE", lambda e, slot=slot, mo=mo: e.tensor_copy(
                    out=CMS[:, mo * W5:(mo + 1) * W5], in_=slot))
                return ("VE", vt)
            stage2(WCP, 16, cons_c)
            bld.wait("scalar", "VE", bld.cnt["VE"])
            datok = None
            for k in range(NCH):
                datok = bld.op("scalar", "SC", lambda e, k=k: e.activation(
                    out=DELTA[:, k * W5:(k + 1) * W5], in_=DELTA[:, k * W5:(k + 1) * W5],
                    func=AF.Exp, scale=vc("A", k)))
            bld.wait("vector", "SC", datok)
            if T.get("gstok") is not None:
                bld.wait("vector", "LD", T["gstok"])
            gtok = None
            for k in range(NCH):
                init_h = 0.0 if wi == 0 else SM5[:, 160 + k:161 + k]
                init_p = 1.0 if wi == 0 else SM5[:, 168 + k:169 + k]
                bld.op("vector", "VE", lambda e, k=k, ih=init_h: e.tensor_tensor_scan(
                    out=HSS[:, k * W5:(k + 1) * W5], data0=DELTA[:, k * W5:(k + 1) * W5],
                    data1=DBX[:, k * W5:(k + 1) * W5], initial=ih,
                    op0=ALU.mult, op1=ALU.add))
                bld.op("vector", "VE", lambda e, k=k, ip=init_p: e.tensor_tensor_scan(
                    out=DBX[:, k * W5:(k + 1) * W5], data0=DELTA[:, k * W5:(k + 1) * W5],
                    data1=DELTA[:, k * W5:(k + 1) * W5], initial=ip,
                    op0=ALU.mult, op1=ALU.bypass))
                bld.op("vector", "VE", lambda e, k=k: e.tensor_copy(
                    out=SM5[:, 160 + k:161 + k], in_=HSS[:, (k + 1) * W5 - 1:(k + 1) * W5]))
                bld.op("vector", "VE", lambda e, k=k: e.tensor_copy(
                    out=SM5[:, 168 + k:169 + k], in_=DBX[:, (k + 1) * W5 - 1:(k + 1) * W5]))
                bld.op("vector", "VE", lambda e, k=k: e.tensor_tensor(
                    out=HSS[:, k * W5:(k + 1) * W5], in0=CMS[:, k * W5:(k + 1) * W5],
                    in1=HSS[:, k * W5:(k + 1) * W5], op=ALU.mult))
                bld.op("vector", "VE", lambda e, k=k: e.scalar_tensor_tensor(
                    out=HSS[:, k * W5:(k + 1) * W5], in0=XACT[:, k * W5:(k + 1) * W5],
                    scalar=vc("D_skip", k), in1=HSS[:, k * W5:(k + 1) * W5],
                    op0=ALU.mult, op1=ALU.add))
                bld.op("vector", "VE", lambda e, k=k, par=par: e.tensor_tensor(
                    out=GATED[:, k * W5:(k + 1) * W5], in0=HSS[:, k * W5:(k + 1) * W5],
                    in1=ZS2[:, par * NTOK + k * W5:par * NTOK + (k + 1) * W5], op=ALU.mult))
                bld.op("vector", "VE", lambda e, k=k: e.tensor_tensor(
                    out=CMS[:, k * W5:(k + 1) * W5], in0=CMS[:, k * W5:(k + 1) * W5],
                    in1=DBX[:, k * W5:(k + 1) * W5], op=ALU.mult))
                gtok = bld.op("vector", "VE", lambda e, k=k, par=par: e.tensor_tensor(
                    out=GW[:, k * W5:(k + 1) * W5], in0=CMS[:, k * W5:(k + 1) * W5],
                    in1=ZS2[:, par * NTOK + k * W5:par * NTOK + (k + 1) * W5], op=ALU.mult))
            if DBG == "s5w0" and wi == 0:
                bld.wait("gpsimd", "VE", bld.cnt["VE"])
                for nm, tt in (("XACT", XACT), ("DELTA", DELTA), ("DBX", DBX),
                               ("CMS", CMS), ("HSS", HSS)):
                    idx = {"XACT": 0, "DELTA": 1, "DBX": 2, "CMS": 3, "HSS": 4}[nm]
                    bld.dma("gpsimd", "LG", dr["dbg"][idx], tt[:])
                bld.dma("gpsimd", "LG", dr["dbg"][5], GATED[:])
                bld.dma("gpsimd", "LG", dr["dbg"][6], XN[:, 0:NTOK])
                bld.dma("gpsimd", "LG", dr["dbg"][7], XB2[:, 0:NTOK])
                bld.dma("gpsimd", "LG", dr["dbg"][8], ZS2[:, 0:NTOK])
                bld.dma("gpsimd", "LG", dr["dbg"][9], MBC[:])
                bld.dma("gpsimd", "LG", dr["dbg"][10], RSBC[:])
                bld.dma("gpsimd", "LG", dr["dbg"][11], GW[:])
                bld.barrier("s5dbg")
            bld.wait("sync", "VE", gtok)
            gstok = bld.dma("sync", "LD",
                            dr["gdef"][:, :, wi * W5:(wi + 1) * W5].rearrange("a p c -> p a c"),
                            GW[:].rearrange("p (a c) -> p a c", c=W5))
            T["gstok"] = gstok
            bld.wait("tensor", "VE", gtok)
            for mo in range(NCH):
                wpar = mo % 2
                bld.wait("gpsimd", "TN", bld.cnt["TN"])
                wtok = bld.dma("gpsimd", "LG", WB[wpar][:].rearrange("p (a c) -> p a c", c=128),
                               dr["W_out"][:, :, mo * 128:(mo + 1) * 128].rearrange("a p c -> p a c"))
                bld.wait("tensor", "LG", wtok)
                ja, pa = pool.acq()
                for k in range(NCH):
                    fn = lambda e, p=pa, wpar=wpar, k=k: e.matmul(
                        out=p[:, 0:W5], lhsT=WB[wpar][:, k * 128:(k + 1) * 128],
                        rhs=GATED[:, k * W5:(k + 1) * W5],
                        start=(k == 0), stop=(k == NCH - 1), skip_group_check=True)
                    if k == NCH - 1:
                        mtok = bld.op("tensor", "TN", fn)
                    else:
                        bld.opn("tensor", fn)
                wo_used[wpar] = mtok
                bld.wait("vector", "TN", mtok)
                xsl = XS[:, mo * NTOK + wi * W5:mo * NTOK + (wi + 1) * W5]
                bld.op("vector", "VE", lambda e, xsl=xsl, wi=wi: e.tensor_tensor(
                    out=TMP6[:], in0=xsl, in1=MBC[:, wi * W5:(wi + 1) * W5], op=ALU.subtract))
                bld.op("vector", "VE", lambda e, wi=wi: e.tensor_tensor(
                    out=TMP6[:], in0=TMP6[:], in1=RSBC[:, wi * W5:(wi + 1) * W5], op=ALU.mult))
                bld.op("vector", "VE", lambda e, p=pa: e.tensor_tensor(
                    out=TMP6[:], in0=TMP6[:], in1=p[:, 0:W5], op=ALU.add))
                vtok = bld.op("vector", "VE", lambda e, xsl=xsl: e.tensor_tensor(
                    out=xsl, in0=xsl, in1=TMP6[:], op=ALU.add))
                pool.rel(ja, "VE", vtok)
            if DBG == "s5win":
                bld.wait("gpsimd", "VE", bld.cnt["VE"])
                bld.dma("gpsimd", "LG", dr["dbg"][wi], XS[:, 0:NTOK])
                bld.dma("gpsimd", "LG", dr["dbg"][8 + wi], GATED[:])
                bld.barrier(f"s5wd{wi}")
            if DBG == "s5bar":
                bld.barrier(f"s5wb{wi}")
            if DBG == "s5tv":
                bld.wait("tensor", "VE", bld.cnt["VE"])
                bld.wait("vector", "TN", bld.cnt["TN"])
            if DBG == "s5ts":
                bld.wait("tensor", "SC", bld.cnt["SC"])
                bld.wait("scalar", "TN", bld.cnt["TN"])
                bld.wait("scalar", "VE", bld.cnt["VE"])
                bld.wait("vector", "SC", bld.cnt["SC"])
            if DBG == "s5gp":
                bld.wait("tensor", "LG", bld.cnt.get("LG", 0))
                bld.wait("gpsimd", "TN", bld.cnt["TN"])
                bld.wait("gpsimd", "VE", bld.cnt["VE"])
            if DBG == "s5f1":
                bld.wait("vector", "LD", bld.cnt.get("LD", 0))
            if DBG == "s5f2":
                bld.wait("scalar", "VE", bld.cnt["VE"])
            if DBG == "s5f3":
                bld.wait("tensor", "SC", bld.cnt["SC"])
            if DBG == "s5sy":
                bld.wait("sync", "VE", bld.cnt["VE"])
                bld.wait("sync", "TN", bld.cnt["TN"])
                bld.wait("vector", "LD", bld.cnt.get("LD", 0))
                bld.wait("tensor", "LD", bld.cnt.get("LD", 0))

        if DBG == "s5xs":
            bld.wait("gpsimd", "VE", bld.cnt["VE"])
            for kk in range(8):
                bld.dma("gpsimd", "LG", dr["dbg"][kk], XS[:, kk * NTOK:(kk + 1) * NTOK])
            bld.barrier("s5xsdbg")
        bld.wait("sync", "VE", bld.cnt["VE"])
        stok = bld.dma("sync", "LD", dr["aghi"], SM5[:, 160:168])
        bld.wait("gpsimd", "LD", stok)
        agtok = bld.op("gpsimd", "GP", lambda e: e.collective_compute(
            "AllGather", ALU.bypass, replica_groups=[list(range(8))],
            ins=[dr["aghi"]], outs=[dr["agho"]]))
        bld.wait("sync", "GP", agtok)
        rtok = bld.dma("sync", "LD", SM5[:, 16:80].rearrange("p (s c) -> p s c", c=8),
                       dr["agho"].rearrange("s p c -> p s c"))
        bld.wait("vector", "LD", rtok)
        h0tok = bld.op("vector", "VE", lambda e: e.tensor_scalar(
            out=SM5[:, 176:184], in0=SM5[:, 16:24], scalar1=VEC[:, sp0:sp0 + 1],
            scalar2=None, op0=ALU.mult))
        for s in range(1, 8):
            h0tok = bld.op("vector", "VE", lambda e, s=s: e.scalar_tensor_tensor(
                out=SM5[:, 176:184], in0=SM5[:, 16 + 8 * s:24 + 8 * s],
                scalar=VEC[:, sp0 + s:sp0 + s + 1], in1=SM5[:, 176:184],
                op0=ALU.mult, op1=ALU.add))
        for wi in range(NW5):
            grd = bld.dma("sync", "LD", GW[:].rearrange("p (a c) -> p a c", c=W5),
                          dr["gdef"][:, :, wi * W5:(wi + 1) * W5].rearrange("a p c -> p a c"))
            bld.wait("vector", "LD", grd)
            gct = None
            for k in range(NCH):
                gct = bld.op("vector", "VE", lambda e, k=k: e.tensor_scalar(
                    out=GW[:, k * W5:(k + 1) * W5], in0=GW[:, k * W5:(k + 1) * W5],
                    scalar1=SM5[:, 176 + k:177 + k], scalar2=None, op0=ALU.mult))
            bld.wait("tensor", "VE", gct)
            if DBG == "s5g" and wi == 0:
                bld.wait("gpsimd", "VE", bld.cnt["VE"])
                bld.dma("gpsimd", "LG", dr["dbg"][12], GW[:])
                bld.dma("gpsimd", "LG", dr["dbg"][13][:, 0:24], SM5[:, 160:184])
                bld.barrier("s5gdbg")
            for mo in range(NCH):
                wpar = mo % 2
                if wo_used[wpar] is not None:
                    bld.wait("gpsimd", "TN", wo_used[wpar])
                wtok = bld.dma("gpsimd", "LG", WB[wpar][:].rearrange("p (a c) -> p a c", c=128),
                               dr["W_out"][:, :, mo * 128:(mo + 1) * 128].rearrange("a p c -> p a c"))
                bld.wait("tensor", "LG", wtok)
                ja, pa = pool.acq()
                for k in range(NCH):
                    fn = lambda e, p=pa, wpar=wpar, k=k: e.matmul(
                        out=p[:, 0:W5], lhsT=WB[wpar][:, k * 128:(k + 1) * 128],
                        rhs=GW[:, k * W5:(k + 1) * W5],
                        start=(k == 0), stop=(k == NCH - 1), skip_group_check=True)
                    if k == NCH - 1:
                        mtok = bld.op("tensor", "TN", fn)
                    else:
                        bld.opn("tensor", fn)
                wo_used[wpar] = mtok
                bld.wait("vector", "TN", mtok)
                xsl = XS[:, mo * NTOK + wi * W5:mo * NTOK + (wi + 1) * W5]
                vtok = bld.op("vector", "VE", lambda e, xsl=xsl, p=pa: e.tensor_tensor(
                    out=xsl, in0=xsl, in1=p[:, 0:W5], op=ALU.add))
                pool.rel(ja, "VE", vtok)
            bld.wait("sync", "TN", bld.cnt["TN"])
    bld.barrier(tag)


def build_program():
    nc = bass.Bass("TRN2", target_bir_lowering=False)
    NV = len(VCOLS)

    dr = {}
    dr["x"] = nc.dram_tensor("x", [NCH, 128, NTOK], F32, kind="ExternalInput").ap()
    dr["out"] = nc.dram_tensor("out", [NCH, 128, NTOK], F32, kind="ExternalOutput").ap()
    dr["dbg"] = nc.dram_tensor("dbg", [16, 128, NTOK], BF16, kind="ExternalOutput").ap()
    dr["vec"] = nc.dram_tensor("vec", [128, len(VCOLS)], F32, kind="ExternalInput").ap()
    dr["ones_m"] = nc.dram_tensor("ones_m", [128, 128], BF16, kind="ExternalInput").ap()
    dr["ones_r"] = nc.dram_tensor("ones_r", [1, 128], BF16, kind="ExternalInput").ap()
    dr["ident"] = nc.dram_tensor("ident", [128, 128], BF16, kind="ExternalInput").ap()
    dr["wgT"] = nc.dram_tensor("wgT", [8, 128, HID], I32, kind="ExternalInput").ap()
    dr["wuT"] = nc.dram_tensor("wuT", [8, 128, HID], I32, kind="ExternalInput").ap()
    dr["wdT"] = nc.dram_tensor("wdT", [24, 128, D], I32, kind="ExternalInput").ap()
    dr["uT"] = nc.dram_tensor("uT", [8, 128, 1024], F32, kind="ExternalInput").ap()
    dr["vT"] = nc.dram_tensor("vT", [8, 128, 1024], F32, kind="ExternalInput").ap()
    dr["u_t"] = nc.dram_tensor("u_t", [8, 128, 1024], F32, kind="ExternalInput").ap()
    dr["v_t"] = nc.dram_tensor("v_t", [8, 128, 1024], F32, kind="ExternalInput").ap()
    dr["ag1_in"] = nc.dram_tensor("ag1_in", [2, 8, 128, NTOK], BF16, kind="Internal").ap()
    dr["ag1_out"] = nc.dram_tensor("ag1_out", [8, 2, 8, 128, NTOK], BF16, kind="Internal").ap()
    dr["ag2_in"] = nc.dram_tensor("ag2_in", [2, 128, 16384], BF16, kind="Internal").ap()
    dr["ag2_out"] = nc.dram_tensor("ag2_out", [8, 2, 128, 16384], BF16, kind="Internal").ap()
    dr["WrT"] = nc.dram_tensor("WrT", [8, 128, 1024], F32, kind="ExternalInput").ap()
    dr["WkT"] = nc.dram_tensor("WkT", [8, 128, 1024], F32, kind="ExternalInput").ap()
    dr["WvT"] = nc.dram_tensor("WvT", [8, 128, 1024], F32, kind="ExternalInput").ap()
    dr["agx4i"] = nc.dram_tensor("agx4i", [128, 8], F32, kind="Internal").ap()
    dr["agx4o"] = nc.dram_tensor("agx4o", [8, 128, 8], F32, kind="Internal").ap()
    dr["agx6i"] = nc.dram_tensor("agx6i", [128, 8], F32, kind="Internal").ap()
    dr["agx6o"] = nc.dram_tensor("agx6o", [8, 128, 8], F32, kind="Internal").ap()
    dr["agx5i"] = nc.dram_tensor("agx5i", [128, 16], F32, kind="Internal").ap()
    dr["agx5o"] = nc.dram_tensor("agx5o", [8, 128, 16], F32, kind="Internal").ap()
    dr["aghi"] = nc.dram_tensor("aghi", [128, 8], F32, kind="Internal").ap()
    dr["agho"] = nc.dram_tensor("agho", [8, 128, 8], F32, kind="Internal").ap()
    dr["gdef"] = nc.dram_tensor("gdef", [8, 128, NTOK], BF16, kind="Internal").ap()
    dr["W_in"] = nc.dram_tensor("W_in", [8, 128, 2048], F32, kind="ExternalInput").ap()
    dr["W_out"] = nc.dram_tensor("W_out", [8, 128, 1024], F32, kind="ExternalInput").ap()
    dr["Wsm"] = nc.dram_tensor("Wsm", [8, 128, 96], F32, kind="ExternalInput").ap()
    dr["W_dt"] = nc.dram_tensor("W_dt", [64, 1024], F32, kind="ExternalInput").ap()
    dr["W_Bp"] = nc.dram_tensor("W_Bp", [16, 1024], F32, kind="ExternalInput").ap()
    dr["W_Cp"] = nc.dram_tensor("W_Cp", [16, 1024], F32, kind="ExternalInput").ap()

    es_top = ExitStack()
    bld = Bld(nc, es_top)
    with es_top:
        XS = es_top.enter_context(nc.sbuf_tensor("XS", [128, NCH * NTOK], F32))
        XN = es_top.enter_context(nc.sbuf_tensor("XN", [128, NCH * NTOK], BF16))
        VEC = es_top.enter_context(nc.sbuf_tensor("VEC", [128, NV], F32))
        ONESM = es_top.enter_context(nc.sbuf_tensor("ONESM", [128, 128], BF16))
        ONESR = es_top.enter_context(nc.sbuf_tensor("ONESR", [1, 128], BF16))
        IDENT = es_top.enter_context(nc.sbuf_tensor("IDENT", [128, 128], BF16))
        ROWS = es_top.enter_context(nc.sbuf_tensor("ROWS", [128, NTOK], F32))
        bld.dum = es_top.enter_context(nc.sbuf_tensor("DUM", [1, 8], F32))

        for k in range(NCH):
            bld.dma("sync", "LD", chs(XS, k), dr["x"][k])
        bld.dma("sync", "LD", VEC[:], dr["vec"])
        bld.dma("sync", "LD", ONESM[:], dr["ones_m"])
        bld.dma("sync", "LD", ONESR[:], dr["ones_r"])
        ldtok = bld.dma("sync", "LD", IDENT[:], dr["ident"])
        for e in ENGS:
            bld.wait(e, "LD", ldtok)

        T = {"VEC": VEC, "es": None}

        for sub in SUBS:
            if sub in (1, 3):
                with ExitStack() as es:
                    PA = es.enter_context(nc.psum_tensor("PA%d" % sub, [128, 2048], F32))
                    PB = es.enter_context(nc.psum_tensor("PB%d" % sub, [128, 2048], F32))
                    LXB = es.enter_context(nc.sbuf_tensor("LXB%d" % sub, [128, 2 * NTOK], BF16))
                    LSQ = es.enter_context(nc.sbuf_tensor("LSQ%d" % sub, [128, 2 * NTOK], BF16))
                    LE1 = es.enter_context(nc.sbuf_tensor("LE1%d" % sub, [128, NTOK], F32))
                    HACT = es.enter_context(nc.sbuf_tensor("HACT%d" % sub, [128, 6 * NTOK], BF16))
                    WG = [es.enter_context(nc.sbuf_tensor(f"WG{sub}_{i}", [128, 1024], BF16)) for i in range(2)]
                    WU = [es.enter_context(nc.sbuf_tensor(f"WU{sub}_{i}", [128, 1024], BF16)) for i in range(2)]
                    WD = es.enter_context(nc.sbuf_tensor(f"WD{sub}", [128, 6 * 1024], BF16))
                    pool = Pool(bld, [PA[:, i * W:(i + 1) * W] for i in range(4)]
                                + [PB[:, i * W:(i + 1) * W] for i in range(4)])
                    ln = emit_ln(bld, T, XS, PA, PB, XN, ONESM, ONESR, VEC, LXB, LSQ, LE1,
                                 ROWS, f"ln{sub}")
                    bld.wait("tensor", "VE", ln["applied"])
                    if DBG == "xn" and sub == 1:
                        bld.wait("sync", "VE", ln["applied"])
                        for k in range(NCH):
                            bld.dma("sync", "ST", dr["dbg"][k], chs(XN, k))
                    T2 = dict(T, HACT=HACT, WG=WG, WU=WU, WD=WD, TMPW=LE1[:, 0:W],
                              wgT_h=dr["wgT"], wuT_h=dr["wuT"], wdT_h=dr["wdT"])
                    emit_l0ffn(bld, T2, XS, XN, PA, PB, pool, f"ffn{sub}")
                    if DBG == "hact" and sub == 1:
                        bld.wait("sync", "VE", bld.cnt["VE"])
                        for k in range(12):
                            bld.dma("sync", "ST", dr["dbg"][k], HACT[:, k * NTOK:(k + 1) * NTOK])
                    bld.barrier(f"sub{sub}")
            elif sub == 2:
                with ExitStack() as es:
                    PA = es.enter_context(nc.psum_tensor("PA%d" % sub, [128, 2048], F32))
                    PB = es.enter_context(nc.psum_tensor("PB%d" % sub, [128, 2048], F32))
                    LXB = es.enter_context(nc.sbuf_tensor("LXB%d" % sub, [128, 2 * NTOK], BF16))
                    LSQ = es.enter_context(nc.sbuf_tensor("LSQ%d" % sub, [128, 2 * NTOK], BF16))
                    LE1 = es.enter_context(nc.sbuf_tensor("LE1%d" % sub, [128, NTOK], F32))
                    emit_ln(bld, T, XS, PA, PB, XN, ONESM, ONESR, VEC, LXB, LSQ, LE1,
                            ROWS, "ln2")
                    bld.barrier("ln2b")
                    if DBG == "xn2":
                        for kk in range(NCH):
                            bld.dma("sync", "ST", dr["dbg"][kk], chs(XN, kk))
                emit_l0attn(bld, T, XS, XN, dr, None, ONESM, ONESR, IDENT, VEC)
            elif sub == 5:
                emit_l1attn(bld, T, XS, XN, dr, ONESM, ONESR, VEC, ROWS)
            elif sub in (4, 6):
                emit_l1ffn(bld, T, XS, XN, dr, ONESM, ONESR, VEC, ROWS,
                           dr["agx4i" if sub == 4 else "agx6i"],
                           dr["agx4o" if sub == 4 else "agx6o"], f"s{sub}")
        st = None
        for k in range(NCH):
            st = bld.dma("sync", "ST", dr["out"][k], chs(XS, k))
        bld.wait("sync", "ST", st)

        with (nc.Block() as blk):
            @blk.sync
            def _(e):
                for f in bld.q["sync"]:
                    f(e)

            @blk.tensor
            def _(e):
                for f in bld.q["tensor"]:
                    f(e)

            @blk.vector
            def _(e):
                for f in bld.q["vector"]:
                    f(e)

            @blk.scalar
            def _(e):
                for f in bld.q["scalar"]:
                    f(e)

            @blk.gpsimd
            def _(e):
                for f in bld.q["gpsimd"]:
                    f(e)
    return nc


def host_inputs(inputs):
    x = np.ascontiguousarray(inputs["L0_attn_qz"], dtype=np.float32)
    xT = np.ascontiguousarray(x.reshape(16384, 1024).T)
    wgT = np.ascontiguousarray(inputs["L0_ffn_w_gate"].T).reshape(8, 128, HID)
    wuT = np.ascontiguousarray(inputs["L0_ffn_w_up"].T).reshape(8, 128, HID)
    wdT = np.ascontiguousarray(inputs["L0_ffn_w_down"].T).reshape(24, 128, D)
    u = np.ascontiguousarray(inputs["L0_attn_ternary_factor_u"], np.float32)
    v = np.ascontiguousarray(inputs["L0_attn_ternary_factor_v"], np.float32)
    uT = np.ascontiguousarray(u.T).reshape(8, 128, 1024)
    vT = np.ascontiguousarray(v.T).reshape(8, 128, 1024)
    u_t = u.reshape(8, 128, 1024)
    v_t = v.reshape(8, 128, 1024)
    WrT = np.ascontiguousarray(inputs["L1_ffn_W_r"].T).reshape(8, 128, 1024)
    WkT = np.ascontiguousarray(inputs["L1_ffn_W_k"].T).reshape(8, 128, 1024)
    WvT = np.ascontiguousarray(inputs["L1_ffn_W_v"].T).reshape(8, 128, 1024)
    W_in = np.ascontiguousarray(inputs["L1_attn_W_in"], np.float32).reshape(8, 128, 2048)
    W_out = np.ascontiguousarray(inputs["L1_attn_W_out"], np.float32).reshape(8, 128, 1024)
    Wsm = np.concatenate([
        np.asarray(inputs["L1_attn_W_xp"], np.float32).reshape(8, 128, 64),
        np.asarray(inputs["L1_attn_W_Bg"], np.float32).reshape(8, 128, 16),
        np.asarray(inputs["L1_attn_W_Cg"], np.float32).reshape(8, 128, 16)], axis=2)
    Wsm = np.ascontiguousarray(Wsm)

    NV = len(VCOLS)
    vec_base = np.zeros((128, NV), np.float32)

    def setv(name, val):
        vec_base[:, _vc(name)] = val

    def setvk(name, arr):
        a = np.asarray(arr, np.float32).reshape(8, 128).T
        for k in range(8):
            vec_base[:, _vc(f"{name}{k}")] = a[:, k]

    setv("eps", EPS)
    setv("one", 1.0)
    setv("neghalf", -0.5)
    setv("s127", S127)
    setvk("gamma", inputs["L1_attn_gamma"])
    setvk("beta", inputs["L1_attn_beta"])
    setvk("conv_b", inputs["L1_attn_conv_b"])
    setvk("b_dt", inputs["L1_attn_b_dt"])
    setvk("A", inputs["L1_attn_A"])
    setvk("D_skip", inputs["L1_attn_D_skip"])
    setvk("mu_r", inputs["L1_ffn_mu_r"])
    setvk("mu_k", inputs["L1_ffn_mu_k"])
    cw = np.asarray(inputs["L1_attn_conv_w"], np.float32)
    setvk("cw0", cw[0, 0])
    setvk("cw1", cw[1, 0])
    setvk("cw2", cw[2, 0])

    ones_m = np.full((128, 128), 1.0 / 1024.0, BF)
    ones_r = np.full((1, 128), 1.0, BF)
    ident = np.eye(128, dtype=BF)

    in_maps = []
    for c in range(NCORE):
        vec = vec_base.copy()
        selp = np.zeros(8, np.float32)
        seln = np.zeros(8, np.float32)
        if c % 2 == 1:
            selp[c - 1] = 1.0
        if c % 2 == 0:
            seln[c + 1] = 1.0
        for k in range(8):
            vec[:, _vc(f"selprev{k}")] = selp[k]
            vec[:, _vc(f"selnext{k}")] = seln[k]
        xc = np.ascontiguousarray(xT[:, c * NTOK:(c + 1) * NTOK]).reshape(NCH, 128, NTOK)
        in_maps.append(dict(x=xc, vec=vec, ones_m=ones_m, ones_r=ones_r, ident=ident,
                            wgT=wgT, wuT=wuT, wdT=wdT, uT=uT, vT=vT, u_t=u_t, v_t=v_t,
                            WrT=WrT, WkT=WkT, WvT=WvT, W_in=W_in, W_out=W_out,
                            Wsm=Wsm,
                            W_dt=np.ascontiguousarray(inputs["L1_attn_W_dt"], np.float32),
                            W_Bp=np.ascontiguousarray(inputs["L1_attn_W_Bp"], np.float32),
                            W_Cp=np.ascontiguousarray(inputs["L1_attn_W_Cp"], np.float32)))
    return in_maps


def kernel(**inputs):
    nc = build_program()
    in_maps = host_inputs(inputs)
    res = run_bass_kernel_spmd(nc, in_maps, core_ids=list(range(NCORE)))
    outT = np.concatenate([r["out"].reshape(1024, NTOK) for r in res.results], axis=1)
    return np.ascontiguousarray(outT.T).reshape(4, 4096, 1024)


# version 02
# speedup vs baseline: 1.0654x; 1.0654x over previous
import numpy as np
import ml_dtypes
from contextlib import ExitStack
import concourse.bass as bass
import concourse.mybir as mybir
from concourse.bass_utils import run_bass_kernel_spmd

F32 = mybir.dt.float32
BF16 = mybir.dt.bfloat16
I32 = mybir.dt.int32
AF = mybir.ActivationFunctionType
ALU = mybir.AluOpType
BF = ml_dtypes.bfloat16

NCORE = 8
NTOK = 2048
NCH = 8
D = 1024
W = 512
NWIN = NTOK // W
SEQ = 4096
HID = 3072
SC_ATT = np.float32(0.03125)
S127 = np.float32(1.0 / 127.0)
EPS = np.float32(1e-5)

SUBS = (1, 2, 3, 4, 5, 6)
ATTN_STOP = None
DBG = None

ENGS = ("sync", "tensor", "vector", "scalar", "gpsimd")

VCOLS = {}
def _vc(name):
    if name not in VCOLS:
        VCOLS[name] = len(VCOLS)
    return VCOLS[name]
for _n in ("gamma", "beta", "conv_b", "b_dt", "A", "D_skip", "mu_r", "mu_k",
           "selprev", "selnext"):
    for _k in range(8):
        _vc(f"{_n}{_k}")
for _n in ("cw0", "cw1", "cw2"):
    for _k in range(8):
        _vc(f"{_n}{_k}")
for _n in ("eps", "one", "neghalf", "s127"):
    _vc(_n)


class Bld:
    def __init__(self, nc, es):
        self.nc = nc
        self.es = es
        self.q = {e: [] for e in ENGS}
        self.sems = {}
        self.cnt = {}

    def sem(self, name):
        if name not in self.sems:
            self.sems[name] = self.es.enter_context(self.nc.semaphore(name))
            self.cnt[name] = 0
        return self.sems[name]

    def emit(self, eng, fn):
        self.q[eng].append(fn)

    def op(self, eng, sname, build, inc=1):
        h = self.sem(sname)
        self.cnt[sname] += inc
        tok = self.cnt[sname]
        self.emit(eng, lambda e, b=build, h=h, i=inc: b(e).then_inc(h, i))
        return tok

    def opn(self, eng, build):
        self.emit(eng, lambda e, b=build: b(e))

    def wait(self, eng, sname, tok):
        h = self.sem(sname)
        self.emit(eng, lambda e, h=h, t=tok: e.wait_ge(h, t))

    def dma(self, eng, sname, out, in_):
        return self.op(eng, sname, lambda e, o=out, i=in_: e.dma_start(out=o, in_=i), inc=16)

    def barrier(self, tag):
        toks = {}
        toks["tensor"] = self.op("tensor", "bar_tensor", lambda en: en.nop())
        toks["vector"] = self.op("vector", "bar_vector", lambda en: en.tensor_copy(
            out=self.dum[0:1, 0:1], in_=self.dum[0:1, 1:2]))
        toks["scalar"] = self.op("scalar", "bar_scalar", lambda en: en.copy(
            out=self.dum[0:1, 2:3], in_=self.dum[0:1, 3:4]))
        toks["gpsimd"] = self.op("gpsimd", "bar_gpsimd", lambda en: en.tensor_copy(
            out=self.dum[0:1, 4:5], in_=self.dum[0:1, 5:6]))
        toks["sync"] = self.op("sync", "bar_sync", lambda en: en.nop())
        ldc = self.cnt.get("LD", 0)
        stc = self.cnt.get("ST", 0)
        lgc = self.cnt.get("LG", 0)
        for e in ENGS:
            for o in ENGS:
                if o != e:
                    self.wait(e, f"bar_{o}", toks[o])
            if ldc:
                self.wait(e, "LD", ldc)
            if stc:
                self.wait(e, "ST", stc)
            if lgc:
                self.wait(e, "LG", lgc)


class Pool:
    def __init__(self, bld, slots):
        self.bld = bld
        self.slots = slots
        self.freetok = [None] * len(slots)
        self.i = 0

    def acq(self):
        j = self.i % len(self.slots)
        self.i += 1
        if self.freetok[j] is not None:
            sn, tok = self.freetok[j]
            self.bld.wait("tensor", sn, tok)
        return j, self.slots[j]

    def rel(self, j, sname, tok):
        self.freetok[j] = (sname, tok)


def chs(t, k, w=None):
    if w is None:
        return t[:, k * NTOK:(k + 1) * NTOK]
    return t[:, k * NTOK + w * W: k * NTOK + w * W + W]


def emit_ln(bld, T, XS, PA, PB, dst_bf, ones_m, ones_r, VEC, LXB, LSQ, LE1, ROWS,
           tag, gamma_beta=None, combined=False, apply=True):
    last = None
    tn_by_par = [None, None]
    for k in range(NCH):
        par = k % 2
        lxb = LXB[:, par * NTOK:(par + 1) * NTOK]
        lsq = LSQ[:, par * NTOK:(par + 1) * NTOK]
        if tn_by_par[par] is not None:
            bld.wait("vector", "TN", tn_by_par[par])
        bld.op("vector", "VE", lambda e, lxb=lxb, k=k: e.tensor_copy(out=lxb, in_=chs(XS, k)))
        ctok = bld.op("vector", "VE", lambda e, lxb=lxb, lsq=lsq: e.tensor_tensor(
            out=lsq, in0=lxb, in1=lxb, op=ALU.mult))
        bld.wait("tensor", "VE", ctok)
        for w in range(NWIN):
            bld.opn("tensor", lambda e, w=w, k=k, lxb=lxb: e.matmul(
                out=PA[:, w * W:(w + 1) * W], lhsT=ones_m[:], rhs=lxb[:, w * W:(w + 1) * W],
                start=(k == 0), stop=(k == NCH - 1), skip_group_check=True))
            last = bld.op("tensor", "TN", lambda e, w=w, k=k, lsq=lsq: e.matmul(
                out=PB[:, w * W:(w + 1) * W], lhsT=ones_m[:], rhs=lsq[:, w * W:(w + 1) * W],
                start=(k == 0), stop=(k == NCH - 1), skip_group_check=True))
        tn_by_par[par] = last
    bld.wait("vector", "TN", last)
    bld.op("vector", "VE", lambda e: e.tensor_copy(out=ROWS[0:1, 0:NTOK], in_=PA[0:1, :]))
    bld.op("vector", "VE", lambda e: e.tensor_tensor(
        out=ROWS[96:97, 0:NTOK], in0=ROWS[0:1, 0:NTOK], in1=ROWS[0:1, 0:NTOK], op=ALU.mult))
    vtok = bld.op("vector", "VE", lambda e: e.tensor_tensor(
        out=ROWS[32:33, 0:NTOK], in0=PB[0:1, :], in1=ROWS[96:97, 0:NTOK], op=ALU.subtract))
    bld.wait("scalar", "VE", vtok)
    bld.op("scalar", "SC", lambda e: e.activation(
        out=ROWS[96:97, 0:NTOK], in_=ROWS[32:33, 0:NTOK], func=AF.Ln,
        bias=VEC[0:1, _vc("eps"):_vc("eps") + 1]))
    stok = bld.op("scalar", "SC", lambda e: e.activation(
        out=ROWS[64:65, 0:NTOK], in_=ROWS[96:97, 0:NTOK], func=AF.Exp,
        scale=VEC[0:1, _vc("neghalf"):_vc("neghalf") + 1]))
    if combined:
        bld.wait("vector", "SC", stok)
        bld.op("vector", "VE", lambda e: e.tensor_tensor(
            out=ROWS[32:33, NTOK:2 * NTOK], in0=ROWS[64:65, 0:NTOK],
            in1=ROWS[64:65, 0:NTOK], op=ALU.mult))
        v2tok = bld.op("vector", "VE", lambda e: e.tensor_tensor(
            out=ROWS[96:97, 0:NTOK], in0=ROWS[32:33, 0:NTOK],
            in1=ROWS[32:33, NTOK:2 * NTOK], op=ALU.mult))
        bld.wait("scalar", "VE", v2tok)
        bld.op("scalar", "SC", lambda e: e.activation(
            out=ROWS[96:97, 0:NTOK], in_=ROWS[96:97, 0:NTOK], func=AF.Ln,
            bias=VEC[0:1, _vc("eps"):_vc("eps") + 1]))
        stok = bld.op("scalar", "SC", lambda e: e.activation(
            out=ROWS[64:65, NTOK:2 * NTOK], in_=ROWS[96:97, 0:NTOK], func=AF.Exp,
            scale=VEC[0:1, _vc("neghalf"):_vc("neghalf") + 1]))
        bld.wait("vector", "SC", stok)
        stok2 = bld.op("vector", "VE", lambda e: e.tensor_tensor(
            out=ROWS[96:97, 0:NTOK], in0=ROWS[64:65, 0:NTOK],
            in1=ROWS[64:65, NTOK:2 * NTOK], op=ALU.mult))
        rs_src = ROWS[96:97, 0:NTOK]
        bcast_wait = ("VE", stok2)
    else:
        rs_src = ROWS[64:65, 0:NTOK]
        bcast_wait = ("SC", stok)
    bld.wait("vector", bcast_wait[0], bcast_wait[1])
    rbtok = bld.op("vector", "VE", lambda e, s=rs_src: e.tensor_copy(out=LXB[0:1, 0:NTOK], in_=s))
    bld.wait("tensor", "VE", rbtok)
    btoks = []
    for w in range(NWIN):
        btoks.append(bld.op("tensor", "TN", lambda e, w=w: e.matmul(
            out=PB[:, w * W:(w + 1) * W], lhsT=ones_r[:], rhs=LXB[0:1, w * W:(w + 1) * W],
            start=True, stop=True)))
    bld.wait("vector", "TN", btoks[-1])
    if not apply:
        return {"applied": None}
    atoks = []
    for k in range(NCH):
        bld.op("vector", "VE", lambda e, k=k: e.tensor_tensor(
            out=LE1[:], in0=chs(XS, k), in1=PA[:, :], op=ALU.subtract))
        if gamma_beta is None:
            atoks.append(bld.op("vector", "VE", lambda e, k=k: e.tensor_tensor(
                out=chs(dst_bf, k), in0=LE1[:], in1=PB[:, :], op=ALU.mult)))
        else:
            gcol, bcol = gamma_beta
            bld.op("vector", "VE", lambda e, k=k: e.tensor_tensor(
                out=LE1[:], in0=LE1[:], in1=PB[:, :], op=ALU.mult))
            atoks.append(bld.op("vector", "VE", lambda e, k=k, g=gcol, b=bcol: e.tensor_scalar(
                out=chs(dst_bf, k), in0=LE1[:],
                scalar1=VEC[:, _vc(g + str(k)):_vc(g + str(k)) + 1],
                scalar2=VEC[:, _vc(b + str(k)):_vc(b + str(k)) + 1],
                op0=ALU.mult, op1=ALU.add)))
    return {"applied": atoks[-1], "atoks": atoks}


def emit_l0ffn(bld, T, XS, XN, PA, PB, pool, tag):
    HACT = T["HACT"]
    WG = T["WG"]; WU = T["WU"]
    WD = T["WD"]
    TMP = T["TMPW"]
    wg_h, wu_h, wd_h = T["wgT_h"], T["wuT_h"], T["wdT_h"]
    VEC = T["VEC"]

    wd_used = None
    used_tok = [None, None]
    for piece in range(4):
        if wd_used is not None:
            bld.wait("gpsimd", "TN", wd_used)
        wd_tok = bld.dma(
            "gpsimd", "LG", WD[:].rearrange("p (a c) -> p a c", c=1024),
            wd_h[piece * 6:(piece + 1) * 6, :, :].rearrange("a p c -> p a c"))
        mtok = None
        for mc in range(6):
            hc = piece * 6 + mc
            par = mc % 2
            if used_tok[par] is not None:
                bld.wait("gpsimd", "TN", used_tok[par])
            bld.dma("gpsimd", "LG", WG[par][:].rearrange("p (a c) -> p a c", c=128),
                    wg_h[:, :, hc * 128:(hc + 1) * 128].rearrange("a p c -> p a c"))
            utok = bld.dma("gpsimd", "LG", WU[par][:].rearrange("p (a c) -> p a c", c=128),
                    wu_h[:, :, hc * 128:(hc + 1) * 128].rearrange("a p c -> p a c"))
            bld.wait("tensor", "LG", utok)
            for w in range(NWIN):
                ja, pa = pool.acq()
                jb, pb = pool.acq()
                for k in range(NCH):
                    bld.opn("tensor", lambda e, p=pa, par=par, k=k, w=w: e.matmul(
                        out=p, lhsT=WG[par][:, k * 128:(k + 1) * 128], rhs=chs(XN, k, w),
                        start=(k == 0), stop=(k == NCH - 1)))
                for k in range(NCH):
                    fn = lambda e, p=pb, par=par, k=k, w=w: e.matmul(
                        out=p, lhsT=WU[par][:, k * 128:(k + 1) * 128], rhs=chs(XN, k, w),
                        start=(k == 0), stop=(k == NCH - 1))
                    if k == NCH - 1:
                        mtok = bld.op("tensor", "TN", fn)
                    else:
                        bld.opn("tensor", fn)
                used_tok[par] = mtok
                bld.wait("scalar", "TN", mtok)
                if T.get("prev_htok") is not None:
                    bld.wait("scalar", "VE", T["prev_htok"])
                sttok = bld.op("scalar", "SC", lambda e, p=pa: e.activation(
                    out=TMP, in_=p, func=AF.Silu, scale=VEC[:, _vc("s127"):_vc("s127") + 1]))
                bld.wait("vector", "SC", sttok)
                htok = bld.op("vector", "VE", lambda e, p=pb, mc=mc, w=w: e.scalar_tensor_tensor(
                    out=HACT[:, mc * NTOK + w * W: mc * NTOK + w * W + W],
                    in0=p, scalar=float(S127), in1=TMP, op0=ALU.mult, op1=ALU.mult))
                T["prev_htok"] = htok
                pool.rel(ja, "SC", sttok)
                pool.rel(jb, "VE", htok)
        bld.wait("tensor", "LG", wd_tok)
        bld.wait("tensor", "VE", T["prev_htok"])
        for w in range(NWIN):
            for dmc in range(NCH):
                ja, pa = pool.acq()
                for kk in range(6):
                    fn = lambda e, p=pa, kk=kk, dmc=dmc, w=w: e.matmul(
                        out=p, lhsT=WD[:, kk * 1024 + dmc * 128: kk * 1024 + (dmc + 1) * 128],
                        rhs=HACT[:, kk * NTOK + w * W: kk * NTOK + w * W + W],
                        start=(kk == 0), stop=(kk == 5))
                    if kk == 5:
                        mtok = bld.op("tensor", "TN", fn)
                    else:
                        bld.opn("tensor", fn)
                wd_used = mtok
                bld.wait("vector", "TN", mtok)
                vtok = bld.op("vector", "VE", lambda e, p=pa, dmc=dmc, w=w: e.scalar_tensor_tensor(
                    out=chs(XS, dmc, w), in0=p, scalar=float(0.5 * S127),
                    in1=chs(XS, dmc, w), op0=ALU.mult, op1=ALU.add))
                pool.rel(ja, "VE", vtok)


def emit_qzproj(bld, T, XN, UT, dst, pool, scale):
    for mo in range(NCH):
        for w in range(NWIN):
            ja, pa = pool.acq()
            for k in range(NCH):
                fn = lambda e, p=pa, k=k, mo=mo, w=w: e.matmul(
                    out=p, lhsT=UT[:, k * 1024 + mo * 128:k * 1024 + (mo + 1) * 128],
                    rhs=chs(XN, k, w), start=(k == 0), stop=(k == NCH - 1))
                if k == NCH - 1:
                    mtok = bld.op("tensor", "TN", fn)
                else:
                    bld.opn("tensor", fn)
            bld.wait("vector", "TN", mtok)
            vtok = bld.op("vector", "VE", lambda e, p=pa, mo=mo, w=w: e.tensor_scalar(
                out=chs(dst, mo, w), in0=p, scalar1=float(scale), scalar2=None, op0=ALU.mult))
            pool.rel(ja, "VE", vtok)


def emit_score_pass(bld, T, b, STAT, MOV, STT, PF, PV, QH, VOUT, tag):
    b0 = b * SEQ
    bld.wait("scalar", "TN", bld.cnt["TN"])
    bld.wait("tensor", "SC", bld.cnt["SC"])
    sigtok = [None, None]
    vlast = [None, None, None, None]
    vtok_hist = {}
    evactok = [None, None]
    for J in range(8):
        jp = J % 2
        pv = PV[:, jp * 512:(jp + 1) * 512]
        if evactok[jp] is not None:
            bld.wait("tensor", "VE", evactok[jp])
        pend = None
        for i in range(32):
            par = i % 2
            par4 = i % 4
            fb = PF[:, par * 1024:(par + 1) * 1024]
            if sigtok[par] is not None:
                bld.wait("tensor", "SC", sigtok[par])
            bld.opn("tensor", lambda e, fb=fb, i=i, J=J: e.matmul(
                out=fb[:, 0:512], lhsT=STAT[0:64, b0 + i * 128:b0 + (i + 1) * 128],
                rhs=MOV[0:64, b0 + J * 512:b0 + (J + 1) * 512],
                start=True, stop=True, tile_position=(0, 0), skip_group_check=True))
            ftok = bld.op("tensor", "TN", lambda e, fb=fb, i=i, J=J: e.matmul(
                out=fb[:, 512:1024], lhsT=STAT[64:128, b0 + i * 128:b0 + (i + 1) * 128],
                rhs=MOV[64:128, b0 + J * 512:b0 + (J + 1) * 512],
                start=True, stop=True, tile_position=(64, 0), skip_group_check=True))
            if pend is not None:
                pend()
            bld.wait("scalar", "TN", ftok)
            if vlast[par4] is not None:
                bld.wait("scalar", "TN", vlast[par4])
            sigtok[par] = bld.op("scalar", "SC", lambda e, fb=fb, par4=par4: e.activation(
                out=QH[:, par4 * 1024:(par4 + 1) * 1024], in_=fb, func=AF.Sigmoid))

            def mk_pend(i=i, J=J, par4=par4, pv=pv, st=sigtok[par]):
                def go():
                    bld.wait("tensor", "SC", st)
                    bld.opn("tensor", lambda e: e.matmul(
                        out=pv[0:64, :], lhsT=STT[:, i * 128:i * 128 + 64],
                        rhs=QH[:, par4 * 1024:par4 * 1024 + 512],
                        start=(i == 0), stop=False, tile_position=(0, 0),
                        skip_group_check=True))
                    vlast[par4] = vtok_hist[(J, i)] = bld.op("tensor", "TN", lambda e: e.matmul(
                        out=pv[64:128, :], lhsT=STT[:, i * 128 + 64:(i + 1) * 128],
                        rhs=QH[:, par4 * 1024 + 512:(par4 + 1) * 1024],
                        start=(i == 0), stop=(i == 31), tile_position=(0, 64),
                        skip_group_check=True))
                return go
            pend = mk_pend()
        pend()
        bld.wait("vector", "TN", vtok_hist[(J, 31)])
        evactok[jp] = bld.op("vector", "VE", lambda e, pv=pv, J=J: e.tensor_copy(
            out=VOUT[:, J * 512:(J + 1) * 512], in_=pv))
    return evactok


def emit_l0attn(bld, T, XS, XN, dr, sub_es, ONESM, ONESR, IDENT, VEC):
    nc = bld.nc
    bld.wait("tensor", "VE", bld.cnt["VE"])
    with ExitStack() as es:
        PAa = es.enter_context(nc.psum_tensor("PAa", [128, 2048], F32))
        PBa = es.enter_context(nc.psum_tensor("PBa", [128, 2048], F32))
        UT = es.enter_context(nc.sbuf_tensor("UT2", [128, 8 * 1024], BF16))
        VT = es.enter_context(nc.sbuf_tensor("VT2", [128, 8 * 1024], BF16))
        QZU = es.enter_context(nc.sbuf_tensor("QZU", [128, NCH * NTOK], BF16))
        QZV = es.enter_context(nc.sbuf_tensor("QZV", [128, NCH * NTOK], BF16))
        pool = Pool(bld, [PAa[:, i * W:(i + 1) * W] for i in range(4)]
                    + [PBa[:, i * W:(i + 1) * W] for i in range(4)])
        ut_tok = bld.dma("gpsimd", "LG", UT[:].rearrange("p (a c) -> p a c", c=1024),
                         dr["uT"].rearrange("a p c -> p a c"))
        vt_tok = bld.dma("gpsimd", "LG", VT[:].rearrange("p (a c) -> p a c", c=1024),
                         dr["vT"].rearrange("a p c -> p a c"))
        bld.wait("tensor", "LG", vt_tok)
        emit_qzproj(bld, T, XN, UT, QZU, pool, SC_ATT)
        emit_qzproj(bld, T, XN, VT, QZV, pool, SC_ATT)
        stok = None
        bld.wait("sync", "VE", bld.cnt["VE"])
        if DBG == "qz":
            for kk in range(8):
                bld.dma("sync", "ST", dr["dbg"][kk], chs(QZU, kk))
                bld.dma("sync", "ST", dr["dbg"][8 + kk], chs(QZV, kk))
        for mo in range(NCH):
            stok = bld.dma("sync", "LD", dr["ag1_in"][0, mo], chs(QZU, mo))
            stok = bld.dma("sync", "LD", dr["ag1_in"][1, mo], chs(QZV, mo))
        bld.wait("gpsimd", "LD", stok)
        ag1tok = bld.op("gpsimd", "GP", lambda e: e.collective_compute(
            "AllGather", ALU.bypass, replica_groups=[list(range(8))],
            ins=[dr["ag1_in"]], outs=[dr["ag1_out"]]))
        bld.barrier("sub2a")
    if ATTN_STOP == "A":
        return
    with ExitStack() as es:
        PF = es.enter_context(nc.psum_tensor("PF", [128, 2048], F32))
        PV = es.enter_context(nc.psum_tensor("PV", [128, 1024], F32))
        PT = es.enter_context(nc.psum_tensor("PT", [128, 1024], F32))
        QMU = es.enter_context(nc.sbuf_tensor("QMU", [128, 16384], BF16))
        QMV = es.enter_context(nc.sbuf_tensor("QMV", [128, 16384], BF16))
        QUT = es.enter_context(nc.sbuf_tensor("QUT", [128, 4096], BF16))
        QVT = es.enter_context(nc.sbuf_tensor("QVT", [128, 4096], BF16))
        QH = es.enter_context(nc.sbuf_tensor("QH", [128, 4096], BF16))
        VOUT = es.enter_context(nc.sbuf_tensor("VOUT", [128, 4096], BF16))
        bld.wait("sync", "GP", ag1tok)
        rtok = None
        def dynread(e, tgt, dst):
            if "cid" not in T:
                T["cid"] = e.partition_id()
            return e.dma_start(out=dst[:].rearrange("p (a c) -> p a c", c=NTOK),
                               in_=dr["ag1_out"][:, tgt, T["cid"], :, :].rearrange("a p c -> p a c"))
        bld.op("sync", "LD", lambda e: dynread(e, 0, QMU), inc=16)
        rtok = bld.op("sync", "LD", lambda e: dynread(e, 1, QMV), inc=16)
        for e_ in ("tensor", "vector", "scalar"):
            bld.wait(e_, "LD", rtok)
        if DBG == "qm":
            bld.wait("sync", "LD", rtok)
            for kk in range(8):
                bld.dma("sync", "ST", dr["dbg"][kk], QMU[:, kk * NTOK:(kk + 1) * NTOK])
                bld.dma("sync", "ST", dr["dbg"][8 + kk], QMV[:, kk * NTOK:(kk + 1) * NTOK])
        if ATTN_STOP == "B0":
            bld.barrier("sub2b0")
            return
        a2stage = None
        nbatch = 1 if (ATTN_STOP in ("B1", "B2", "B3")) else 4
        for b in range(nbatch):
            slotve = [None, None]
            ti = 0
            for src_t, dst_t in ((QMU, QUT), (QMV, QVT)):
                for tc in range(32):
                    sl = ti % 2
                    ti += 1
                    if slotve[sl] is not None:
                        bld.wait("tensor", "VE", slotve[sl])
                    ttok = bld.op("tensor", "TN", lambda e, sl=sl, tc=tc, s=src_t, b=b: e.matmul(
                        out=PT[:, sl * 512:sl * 512 + 128],
                        lhsT=s[:, b * SEQ + tc * 128:b * SEQ + (tc + 1) * 128],
                        rhs=IDENT[:], start=True, stop=True, skip_group_check=True))
                    bld.wait("vector", "TN", ttok)
                    if DBG == "one" and b == 0 and ti == 1:
                        e1 = bld.op("vector", "VE", lambda e: e.tensor_copy(
                            out=QH[:, 0:128], in_=PT[:, 0:128]))
                        e2 = bld.op("vector", "VE", lambda e: e.tensor_copy(
                            out=QH[:, 1024:1024 + 128], in_=QMU[:, 0:128]))
                        e2 = bld.op("vector", "VE", lambda e: e.tensor_copy(
                            out=QH[:, 512:512 + 128], in_=IDENT[:]))
                        bld.wait("sync", "VE", e2)
                        bld.dma("sync", "ST", dr["dbg"][0], QH[:, 0:NTOK])
                        sttk = bld.dma("sync", "ST", dr["dbg"][1], QUT[:, 0:NTOK])
                        for e_ in ("tensor", "vector"):
                            bld.wait(e_, "ST", sttk)
                    if DBG == "copybypass":
                        slotve[sl] = bld.op("vector", "VE", lambda e, sl=sl, tc=tc, d=dst_t, s=src_t, b=b: e.tensor_copy(
                            out=d[:, tc * 128:(tc + 1) * 128], in_=s[:, b * SEQ + tc * 128:b * SEQ + (tc + 1) * 128]))
                    else:
                        slotve[sl] = bld.op("vector", "VE", lambda e, sl=sl, tc=tc, d=dst_t: e.tensor_copy(
                            out=d[:, tc * 128:(tc + 1) * 128], in_=PT[:, sl * 512:sl * 512 + 128]))
            bld.wait("tensor", "VE", bld.cnt["VE"])
            if DBG == "memtest" and b == 0:
                mtok = bld.op("vector", "VE", lambda e: e.memset(QUT[:], 3.25))
                bld.wait("sync", "VE", mtok)
                bld.dma("sync", "ST", dr["dbg"][0], QUT[:, 0:NTOK])
                for e_ in ("tensor", "vector"):
                    bld.wait(e_, "ST", bld.cnt["ST"])
            if DBG == "slicetest" and b == 0:
                bld.barrier("dbgbar")
                m1 = bld.op("vector", "VE", lambda e: e.memset(QUT[:, 0:128], 7.0))
                m2 = bld.op("vector", "VE", lambda e: e.tensor_copy(
                    out=QUT[:, 128:256], in_=QMU[:, 0:128]))
                m3 = bld.op("vector", "VE", lambda e: e.tensor_copy(
                    out=QUT[:, 256:384], in_=PT[:, 0:128]))
                bld.wait("sync", "VE", m3)
                sttk = bld.dma("sync", "ST", dr["dbg"][0], QUT[:, 0:NTOK])
                for e_ in ("tensor", "vector"):
                    bld.wait(e_, "ST", sttk)
            if DBG in ("qut", "copybypass") and b == 0:
                bld.wait("sync", "VE", bld.cnt["VE"])
                for kk in range(2):
                    bld.dma("sync", "ST", dr["dbg"][kk], QUT[:, kk * NTOK:(kk + 1) * NTOK])
                    bld.dma("sync", "ST", dr["dbg"][2 + kk], QVT[:, kk * NTOK:(kk + 1) * NTOK])
                for e_ in ("tensor", "vector"):
                    bld.wait(e_, "ST", bld.cnt["ST"])
            if ATTN_STOP == "B1":
                bld.barrier("sub2b1")
                return
            emit_score_pass(bld, T, b, QMU, QMV, QUT, PF, PV, QH, VOUT, f"p1b{b}")
            bld.wait("sync", "VE", bld.cnt["VE"])
            if DBG == "vout" and b == 0:
                for kk in range(2):
                    bld.dma("sync", "ST", dr["dbg"][4 + kk], VOUT[:, kk * NTOK:(kk + 1) * NTOK])
                for e_ in ("tensor", "vector"):
                    bld.wait(e_, "ST", bld.cnt["ST"])
            a2stage = bld.dma("sync", "LD", dr["ag2_in"][1, :, b * SEQ:(b + 1) * SEQ], VOUT[:])
            bld.wait("vector", "LD", a2stage)
            if ATTN_STOP == "B2":
                bld.barrier("sub2b2")
                return
            emit_score_pass(bld, T, b, QMV, QMU, QVT, PF, PV, QH, VOUT, f"p2b{b}")
            bld.wait("sync", "VE", bld.cnt["VE"])
            a2stage = bld.dma("sync", "LD", dr["ag2_in"][0, :, b * SEQ:(b + 1) * SEQ], VOUT[:])
            bld.wait("vector", "LD", a2stage)
        if ATTN_STOP in ("B", "B3"):
            bld.barrier("sub2bx")
            return
        bld.wait("gpsimd", "LD", a2stage)
        ag2tok = bld.op("gpsimd", "GP", lambda e: e.collective_compute(
            "AllGather", ALU.bypass, replica_groups=[list(range(8))],
            ins=[dr["ag2_in"]], outs=[dr["ag2_out"]]))
        bld.barrier("sub2b")
    with ExitStack() as es:
        PAc = es.enter_context(nc.psum_tensor("PAc", [128, 2048], F32))
        PBc = es.enter_context(nc.psum_tensor("PBc", [128, 2048], F32))
        V1F = es.enter_context(nc.sbuf_tensor("V1F", [128, NCH * NTOK], BF16))
        V2F = es.enter_context(nc.sbuf_tensor("V2F", [128, NCH * NTOK], BF16))
        UB = es.enter_context(nc.sbuf_tensor("UB", [128, 8 * 1024], BF16))
        VB = es.enter_context(nc.sbuf_tensor("VB", [128, 8 * 1024], BF16))
        pool = Pool(bld, [PAc[:, i * W:(i + 1) * W] for i in range(4)]
                    + [PBc[:, i * W:(i + 1) * W] for i in range(4)])
        bld.wait("sync", "GP", ag2tok)

        def dynread2(e, tgt, dst):
            return e.dma_start(out=dst[:].rearrange("p (a c) -> p a c", c=NTOK),
                               in_=dr["ag2_out"][:, tgt, :, bass.ts(T["cid"], NTOK)].rearrange("a p c -> p a c"))
        bld.op("sync", "LD", lambda e: dynread2(e, 0, V1F), inc=16)
        rtok = bld.op("sync", "LD", lambda e: dynread2(e, 1, V2F), inc=16)
        if DBG == "v1f":
            bld.wait("sync", "LD", rtok)
            for kk in range(8):
                bld.dma("sync", "ST", dr["dbg"][kk], V1F[:, kk * NTOK:(kk + 1) * NTOK])
                bld.dma("sync", "ST", dr["dbg"][8 + kk], V2F[:, kk * NTOK:(kk + 1) * NTOK])
        ub_tok = bld.dma("gpsimd", "LG", UB[:].rearrange("p (a c) -> p a c", c=1024),
                         dr["u_t"].rearrange("a p c -> p a c"))
        vb_tok = bld.dma("gpsimd", "LG", VB[:].rearrange("p (a c) -> p a c", c=1024),
                         dr["v_t"].rearrange("a p c -> p a c"))
        bld.wait("tensor", "LD", rtok)
        bld.wait("tensor", "LG", vb_tok)
        for mo in range(NCH):
            for w in range(NWIN):
                ja, pa = pool.acq()
                for k in range(NCH):
                    bld.opn("tensor", lambda e, p=pa, k=k, mo=mo, w=w: e.matmul(
                        out=p, lhsT=UB[:, k * 1024 + mo * 128:k * 1024 + (mo + 1) * 128],
                        rhs=chs(V1F, k, w), start=(k == 0), stop=False))
                for k in range(NCH):
                    fn = lambda e, p=pa, k=k, mo=mo, w=w: e.matmul(
                        out=p, lhsT=VB[:, k * 1024 + mo * 128:k * 1024 + (mo + 1) * 128],
                        rhs=chs(V2F, k, w), start=False, stop=(k == NCH - 1))
                    if k == NCH - 1:
                        mtok = bld.op("tensor", "TN", fn)
                    else:
                        bld.opn("tensor", fn)
                bld.wait("vector", "TN", mtok)
                vtok = bld.op("vector", "VE", lambda e, p=pa, mo=mo, w=w: e.scalar_tensor_tensor(
                    out=chs(XS, mo, w), in0=p, scalar=float(SC_ATT),
                    in1=chs(XS, mo, w), op0=ALU.mult, op1=ALU.add))
                pool.rel(ja, "VE", vtok)
        bld.barrier("sub2c")


def emit_l1ffn(bld, T, XS, XN, dr, ONESM, ONESR, VEC, ROWS, agx_in, agx_out, tag):
    nc = bld.nc
    mu_r0 = _vc("mu_r0")
    mu_k0 = _vc("mu_k0")
    sp0 = _vc("selprev0")
    with ExitStack() as es:
        PA = es.enter_context(nc.psum_tensor("PA" + tag, [128, 2048], F32))
        PB = es.enter_context(nc.psum_tensor("PB" + tag, [128, 2048], F32))
        XK = es.enter_context(nc.sbuf_tensor("XK" + tag, [128, NCH * NTOK], BF16))
        with ExitStack() as es2:
            LXB = es2.enter_context(nc.sbuf_tensor("LXB" + tag, [128, 2 * NTOK], BF16))
            LSQ = es2.enter_context(nc.sbuf_tensor("LSQ" + tag, [128, 2 * NTOK], BF16))
            LE1 = es2.enter_context(nc.sbuf_tensor("LE1" + tag, [128, NTOK], F32))
            XNC = es2.enter_context(nc.sbuf_tensor("XNC" + tag, [128, NTOK], F32))
            SML = es2.enter_context(nc.sbuf_tensor("SML" + tag, [128, 128], F32))
            emit_ln(bld, T, XS, PA, PB, None, ONESM, ONESR, VEC, LXB, LSQ, LE1,
                    ROWS, "ln" + tag, apply=False)
            bld.op("vector", "VE", lambda e: e.tensor_scalar(
                out=SML[:, 16:24], in0=VEC[:, mu_r0:mu_r0 + 8], scalar1=1.0, scalar2=None,
                op0=ALU.subtract))
            bld.op("vector", "VE", lambda e: e.tensor_scalar(
                out=SML[:, 24:32], in0=VEC[:, mu_k0:mu_k0 + 8], scalar1=1.0, scalar2=None,
                op0=ALU.subtract))
            atoks = []
            for k in range(NCH):
                bld.op("vector", "VE", lambda e, k=k: e.tensor_tensor(
                    out=LE1[:], in0=chs(XS, k), in1=PA[:, :], op=ALU.subtract))
                bld.op("vector", "VE", lambda e, k=k: e.tensor_tensor(
                    out=XNC[:], in0=LE1[:], in1=PB[:, :], op=ALU.mult))
                bld.op("vector", "VE", lambda e, k=k: e.tensor_copy(
                    out=SML[:, k:k + 1], in_=XNC[:, 0:1]))
                bld.op("vector", "VE", lambda e, k=k: e.tensor_copy(
                    out=SML[:, 8 + k:9 + k], in_=XNC[:, NTOK - 1:NTOK]))
                bld.op("vector", "VE", lambda e: e.tensor_tensor(
                    out=LE1[:, 1:NTOK], in0=XNC[:, 1:NTOK], in1=XNC[:, 0:NTOK - 1],
                    op=ALU.subtract))
                bld.op("vector", "VE", lambda e: e.memset(LE1[:, 0:1], 0.0))
                bld.op("vector", "VE", lambda e, k=k: e.scalar_tensor_tensor(
                    out=chs(XN, k), in0=LE1[:], scalar=SML[:, 16 + k:17 + k], in1=XNC[:],
                    op0=ALU.mult, op1=ALU.add))
                atoks.append(bld.op("vector", "VE", lambda e, k=k: e.scalar_tensor_tensor(
                    out=chs(XK, k), in0=LE1[:], scalar=SML[:, 24 + k:25 + k], in1=XNC[:],
                    op0=ALU.mult, op1=ALU.add)))
            bld.wait("sync", "VE", atoks[-1])
            stok = bld.dma("sync", "LD", agx_in, SML[:, 8:16])
            bld.wait("gpsimd", "LD", stok)
            agtok = bld.op("gpsimd", "GP", lambda e: e.collective_compute(
                "AllGather", ALU.bypass, replica_groups=[list(range(8))],
                ins=[agx_in], outs=[agx_out]))
            bld.wait("sync", "GP", agtok)
            rtok = bld.dma("sync", "LD", SML[:, 32:96].rearrange("p (s c) -> p s c", c=8),
                           agx_out.rearrange("s p c -> p s c"))
            bld.wait("vector", "LD", rtok)
            htok = bld.op("vector", "VE", lambda e: e.tensor_scalar(
                out=SML[:, 96:104], in0=SML[:, 32:40],
                scalar1=VEC[:, sp0:sp0 + 1], scalar2=None, op0=ALU.mult))
            for s in range(1, 8):
                htok = bld.op("vector", "VE", lambda e, s=s: e.scalar_tensor_tensor(
                    out=SML[:, 96:104], in0=SML[:, 32 + 8 * s:40 + 8 * s],
                    scalar=VEC[:, sp0 + s:sp0 + s + 1], in1=SML[:, 96:104],
                    op0=ALU.mult, op1=ALU.add))
            bld.op("vector", "VE", lambda e: e.tensor_tensor(
                out=SML[:, 104:112], in0=SML[:, 0:8], in1=SML[:, 96:104], op=ALU.subtract))
            bld.op("vector", "VE", lambda e: e.tensor_tensor(
                out=SML[:, 112:120], in0=SML[:, 104:112], in1=SML[:, 16:24], op=ALU.mult))
            for k in range(NCH):
                bld.op("vector", "VE", lambda e, k=k: e.tensor_tensor(
                    out=XN[:, k * NTOK:k * NTOK + 1], in0=SML[:, 112 + k:113 + k],
                    in1=SML[:, k:k + 1], op=ALU.add))
            bld.op("vector", "VE", lambda e: e.tensor_tensor(
                out=SML[:, 112:120], in0=SML[:, 104:112], in1=SML[:, 24:32], op=ALU.mult))
            for k in range(NCH):
                ptok = bld.op("vector", "VE", lambda e, k=k: e.tensor_tensor(
                    out=XK[:, k * NTOK:k * NTOK + 1], in0=SML[:, 112 + k:113 + k],
                    in1=SML[:, k:k + 1], op=ALU.add))
            bld.barrier(tag + "mix")
        with ExitStack() as es3:
            RB = es3.enter_context(nc.sbuf_tensor("RB" + tag, [128, NCH * NTOK], BF16))
            KA = XN
            WT = es3.enter_context(nc.sbuf_tensor("WT" + tag, [128, 8 * 1024], BF16))
            TMP = es3.enter_context(nc.sbuf_tensor("TMPX" + tag, [128, W], F32))
            TMP2 = es3.enter_context(nc.sbuf_tensor("TMPY" + tag, [128, W], F32))
            pool = Pool(bld, [PA[:, i * W:(i + 1) * W] for i in range(4)]
                        + [PB[:, i * W:(i + 1) * W] for i in range(4)])
            prev_sc = [None]
            prev_ve = [None]

            def proj(wdram, mov, evac):
                wtok = bld.dma("gpsimd", "LG", WT[:].rearrange("p (a c) -> p a c", c=1024),
                               wdram.rearrange("a p c -> p a c"))
                bld.wait("tensor", "LG", wtok)
                for mo in range(NCH):
                    for w in range(NWIN):
                        ja, pa = pool.acq()
                        for k in range(NCH):
                            fn = lambda e, p=pa, k=k, mo=mo, w=w: e.matmul(
                                out=p, lhsT=WT[:, k * 1024 + mo * 128:k * 1024 + (mo + 1) * 128],
                                rhs=chs(mov, k, w), start=(k == 0), stop=(k == NCH - 1))
                            if k == NCH - 1:
                                mtok = bld.op("tensor", "TN", fn)
                            else:
                                bld.opn("tensor", fn)
                        evac(mo, w, ja, pa, mtok)
                bld.wait("gpsimd", "TN", bld.cnt["TN"])

            def evac_r(mo, w, ja, pa, mtok):
                bld.wait("scalar", "TN", mtok)
                stok = bld.op("scalar", "SC", lambda e, p=pa, mo=mo, w=w: e.activation(
                    out=chs(RB, mo, w), in_=p, func=AF.Sigmoid))
                pool.rel(ja, "SC", stok)
            proj(dr["WrT"], XN, evac_r)

            def evac_k(mo, w, ja, pa, mtok):
                bld.wait("scalar", "TN", mtok)
                if prev_ve[0] is not None:
                    bld.wait("scalar", "VE", prev_ve[0])
                stok = bld.op("scalar", "SC", lambda e, p=pa: e.activation(
                    out=TMP[:], in_=p, func=AF.Relu))
                bld.wait("vector", "SC", stok)
                vtok = bld.op("vector", "VE", lambda e, mo=mo, w=w: e.tensor_tensor(
                    out=chs(KA, mo, w), in0=TMP[:], in1=TMP[:], op=ALU.mult))
                prev_ve[0] = vtok
                pool.rel(ja, "VE", vtok)
            proj(dr["WkT"], XK, evac_k)

            def evac_v(mo, w, ja, pa, mtok):
                bld.wait("vector", "TN", mtok)
                bld.op("vector", "VE", lambda e, p=pa, mo=mo, w=w: e.tensor_tensor(
                    out=TMP2[:], in0=chs(RB, mo, w), in1=p, op=ALU.mult))
                vtok = bld.op("vector", "VE", lambda e, mo=mo, w=w: e.scalar_tensor_tensor(
                    out=chs(XS, mo, w), in0=TMP2[:], scalar=0.5, in1=chs(XS, mo, w),
                    op0=ALU.mult, op1=ALU.add))
                pool.rel(ja, "VE", vtok)
            proj(dr["WvT"], KA, evac_v)
    bld.barrier(tag)


def emit_l1attn(bld, T, XS, XN, dr, ONESM, ONESR, VEC, ROWS, tag="s5"):
    nc = bld.nc
    W5 = 256
    NW5 = NTOK // W5
    sp0 = _vc("selprev0")
    sn0 = _vc("selnext0")

    def vc(name, k):
        i = _vc(f"{name}{k}")
        return VEC[:, i:i + 1]

    with ExitStack() as es:
        MBC = es.enter_context(nc.sbuf_tensor("MBC", [128, NTOK], BF16))
        RSBC = es.enter_context(nc.sbuf_tensor("RSBC", [128, NTOK], BF16))
        SM5 = es.enter_context(nc.sbuf_tensor("SM5", [128, 256], F32))

        with ExitStack() as es2:
            PA5 = es2.enter_context(nc.psum_tensor("PA5", [128, 2048], F32))
            ROWS5 = es2.enter_context(nc.sbuf_tensor("ROWS5", [128, 2 * NTOK], F32))
            PB5 = es2.enter_context(nc.psum_tensor("PB5", [128, 2048], F32))
            LXB = es2.enter_context(nc.sbuf_tensor("LXB5", [128, 2 * NTOK], BF16))
            LSQ = es2.enter_context(nc.sbuf_tensor("LSQ5", [128, 2 * NTOK], BF16))
            LE1 = es2.enter_context(nc.sbuf_tensor("LE15", [128, NTOK], F32))
            emit_ln(bld, T, XS, PA5, PB5, XN, ONESM, ONESR, VEC,
                    LXB, LSQ, LE1, ROWS5, "ln5", gamma_beta=("gamma", "beta"),
                    combined=True)
            mt = bld.op("vector", "VE", lambda e: e.tensor_copy(out=MBC[:], in_=PA5[:, :]))
            rt = bld.op("vector", "VE", lambda e: e.tensor_copy(
                out=LXB[0:1, 0:NTOK], in_=ROWS5[64:65, 0:NTOK]))
            bld.wait("tensor", "VE", rt)
            bts = []
            for w in range(4):
                bts.append(bld.op("tensor", "TN", lambda e, w=w: e.matmul(
                    out=PB5[:, w * 512:(w + 1) * 512], lhsT=ONESR[:],
                    rhs=LXB[0:1, w * 512:(w + 1) * 512],
                    start=True, stop=True, skip_group_check=True)))
            bld.wait("vector", "TN", bts[-1])
            bld.op("vector", "VE", lambda e: e.tensor_copy(out=RSBC[:], in_=PB5[:, :]))
            bld.barrier(tag + "ln")
        PC = es.enter_context(nc.psum_tensor("PC5", [128, 2048], F32))
        PD = es.enter_context(nc.psum_tensor("PD5", [128, 1024], F32))
        pool = Pool(bld, [PC[:, 0:512], PC[:, 512:1024]])
        s2slot = [PC[:, 1024:1024 + W5], PC[:, 1536:1536 + W5]]
        XB2 = es.enter_context(nc.sbuf_tensor("XB2", [128, 2 * NTOK], F32))
        ZS2 = es.enter_context(nc.sbuf_tensor("ZS2", [128, 2 * NTOK], BF16))
        XACT = es.enter_context(nc.sbuf_tensor("XACT", [128, NTOK], F32))
        DELTA = es.enter_context(nc.sbuf_tensor("DELTA", [128, NTOK], F32))
        DBX = es.enter_context(nc.sbuf_tensor("DBX", [128, NTOK], F32))
        CMS = es.enter_context(nc.sbuf_tensor("CMS", [128, NTOK], F32))
        HSS = es.enter_context(nc.sbuf_tensor("HSS", [128, NTOK], F32))
        GATED = es.enter_context(nc.sbuf_tensor("GATED", [128, NTOK], BF16))
        GW = es.enter_context(nc.sbuf_tensor("GW", [128, NTOK], BF16))
        WSM = es.enter_context(nc.sbuf_tensor("WSM", [128, 8 * 96], F32))
        WDT = es.enter_context(nc.sbuf_tensor("WDT", [64, 1024], BF16))
        WBP = es.enter_context(nc.sbuf_tensor("WBP", [16, 1024], BF16))
        WCP = es.enter_context(nc.sbuf_tensor("WCP", [16, 1024], BF16))
        MB = es.enter_context(nc.sbuf_tensor("MB", [64, W5], BF16))
        WB = [es.enter_context(nc.sbuf_tensor(f"WB5_{i}", [128, 1024], BF16)) for i in range(4)]
        HLB = es.enter_context(nc.sbuf_tensor("HLB", [128, 16], BF16))
        TMP5 = es.enter_context(nc.sbuf_tensor("TMP5", [128, W5], F32))
        TMP6 = es.enter_context(nc.sbuf_tensor("TMP6", [128, W5], F32))

        for k in range(NCH):
            bld.op("vector", "VE", lambda e, k=k: e.tensor_copy(
                out=SM5[:, k:k + 1], in_=XN[:, k * NTOK:k * NTOK + 1]))
            bld.op("vector", "VE", lambda e, k=k: e.tensor_copy(
                out=SM5[:, 8 + k:9 + k], in_=XN[:, (k + 1) * NTOK - 1:(k + 1) * NTOK]))
        bld.wait("sync", "VE", bld.cnt["VE"])
        stok = bld.dma("sync", "LD", dr["agx5i"], SM5[:, 0:16])
        bld.wait("gpsimd", "LD", stok)
        agtok = bld.op("gpsimd", "GP", lambda e: e.collective_compute(
            "AllGather", ALU.bypass, replica_groups=[list(range(8))],
            ins=[dr["agx5i"]], outs=[dr["agx5o"]]))
        bld.wait("sync", "GP", agtok)
        rtok = bld.dma("sync", "LD", SM5[:, 16:144].rearrange("p (s c) -> p s c", c=16),
                       dr["agx5o"].rearrange("s p c -> p s c"))
        bld.wait("vector", "LD", rtok)
        bld.op("vector", "VE", lambda e: e.tensor_scalar(
            out=SM5[:, 144:152], in0=SM5[:, 24:32], scalar1=VEC[:, sp0:sp0 + 1],
            scalar2=None, op0=ALU.mult))
        bld.op("vector", "VE", lambda e: e.tensor_scalar(
            out=SM5[:, 152:160], in0=SM5[:, 16:24], scalar1=VEC[:, sn0:sn0 + 1],
            scalar2=None, op0=ALU.mult))
        for s in range(1, 8):
            bld.op("vector", "VE", lambda e, s=s: e.scalar_tensor_tensor(
                out=SM5[:, 144:152], in0=SM5[:, 16 + 16 * s + 8:16 + 16 * s + 16],
                scalar=VEC[:, sp0 + s:sp0 + s + 1], in1=SM5[:, 144:152],
                op0=ALU.mult, op1=ALU.add))
            bld.op("vector", "VE", lambda e, s=s: e.scalar_tensor_tensor(
                out=SM5[:, 152:160], in0=SM5[:, 16 + 16 * s:16 + 16 * s + 8],
                scalar=VEC[:, sn0 + s:sn0 + s + 1], in1=SM5[:, 152:160],
                op0=ALU.mult, op1=ALU.add))
        bld.op("vector", "VE", lambda e: e.tensor_copy(out=HLB[:, 0:8], in_=SM5[:, 144:152]))
        hlb_tok = bld.op("vector", "VE", lambda e: e.tensor_copy(out=HLB[:, 8:16], in_=SM5[:, 152:160]))
        wsmt = bld.dma("sync", "LD", WSM[:].rearrange("p (a c) -> p a c", c=96),
                       dr["Wsm"].rearrange("a p c -> p a c"))
        bld.dma("gpsimd", "LG", WDT[:], dr["W_dt"])
        bld.dma("gpsimd", "LG", WBP[:], dr["W_Bp"])
        wsmt2 = bld.dma("gpsimd", "LG", WCP[:], dr["W_Cp"])
        bld.wait("tensor", "VE", hlb_tok)
        wo_used = [None, None, None, None]
        for mo in range(NCH):
            par = mo % 4
            if wo_used[par] is not None:
                bld.wait("gpsimd", "TN", wo_used[par])
            wtok = bld.dma("gpsimd", "LG", WB[par][:].rearrange("p (a c) -> p a c", c=128),
                           dr["W_in"][:, :, mo * 128:(mo + 1) * 128].rearrange("a p c -> p a c"))
            bld.wait("tensor", "LG", wtok)
            for k in range(NCH):
                fn = lambda e, par=par, k=k, mo=mo: e.matmul(
                    out=PD[:, 512 + 2 * mo:514 + 2 * mo],
                    lhsT=WB[par][:, k * 128:(k + 1) * 128],
                    rhs=HLB[:, k:16:8], start=(k == 0), stop=(k == NCH - 1),
                    skip_group_check=True)
                if k == NCH - 1:
                    mtok = bld.op("tensor", "TN", fn)
                else:
                    bld.opn("tensor", fn)
            wo_used[par] = mtok
        bld.wait("vector", "TN", bld.cnt["TN"])
        bld.op("vector", "VE", lambda e: e.tensor_copy(
            out=SM5[:, 184:200], in_=PD[:, 512:528]))
        bld.barrier(tag + "pre")

        def prefetch(dram, mo, wpar):
            if wo_used[wpar] is not None:
                bld.wait("gpsimd", "TN", wo_used[wpar])
            return bld.dma("gpsimd", "LG", WB[wpar][:].rearrange("p (a c) -> p a c", c=128),
                           dram[:, :, mo * 128:(mo + 1) * 128].rearrange("a p c -> p a c"))

        def xbzs(wi):
            par = wi % 2
            wtoks = {}
            for mo in range(3):
                wtoks[mo] = prefetch(dr["W_in"], mo, mo % 4)
            for mo in range(16):
                pmo = mo % 8
                wpar = mo % 4
                bld.wait("tensor", "LG", wtoks[mo])
                if mo + 3 < 16:
                    wtoks[mo + 3] = prefetch(dr["W_in"], mo + 3, (mo + 3) % 4)
                ja, pa = pool.acq()
                for k in range(NCH):
                    fn = lambda e, p=pa, wpar=wpar, k=k, wi=wi: e.matmul(
                        out=p[:, 0:W5], lhsT=WB[wpar][:, k * 128:(k + 1) * 128],
                        rhs=XN[:, k * NTOK + wi * W5:k * NTOK + (wi + 1) * W5],
                        start=(k == 0), stop=(k == NCH - 1), skip_group_check=True)
                    if k == NCH - 1:
                        mtok = bld.op("tensor", "TN", fn)
                    else:
                        bld.opn("tensor", fn)
                wo_used[wpar] = mtok
                if mo < 8:
                    bld.wait("vector", "TN", mtok)
                    vtok = bld.op("vector", "VE", lambda e, p=pa, par=par, pmo=pmo: e.tensor_copy(
                        out=XB2[:, par * NTOK + pmo * W5:par * NTOK + (pmo + 1) * W5],
                        in_=p[:, 0:W5]))
                    pool.rel(ja, "VE", vtok)
                else:
                    bld.wait("scalar", "TN", mtok)
                    if pmo == 0:
                        bld.wait("scalar", "VE", bld.cnt["VE"])
                    stok = bld.op("scalar", "SC", lambda e, p=pa, par=par, pmo=pmo: e.activation(
                        out=ZS2[:, par * NTOK + pmo * W5:par * NTOK + (pmo + 1) * W5],
                        in_=p[:, 0:W5], func=AF.Silu))
                    pool.rel(ja, "SC", stok)

        def stage2(stat_t, stat_cols, consumer):
            free = [None, None]
            for mo in range(NCH):
                par = mo % 2
                if free[par] is not None:
                    bld.wait("tensor", free[par][0], free[par][1])
                mt = bld.op("tensor", "TN", lambda e, mo=mo, par=par: e.matmul(
                    out=s2slot[par], lhsT=stat_t[:, mo * 128:(mo + 1) * 128],
                    rhs=MB[0:stat_cols, :], start=True, stop=True, skip_group_check=True))
                free[par] = consumer(mo, s2slot[par], mt)
            bld.wait("tensor", free[0][0], free[0][1])
            bld.wait("tensor", free[1][0], free[1][1])

        xbzs(0)
        for wi in range(NW5):
            par = wi % 2
            if wi < NW5 - 1:
                xbzs(wi + 1)
            bld.wait("vector", "SC", bld.cnt["SC"])
            bld.wait("vector", "TN", bld.cnt["TN"])
            xb_c = lambda k, par=par: XB2[:, par * NTOK + k * W5:par * NTOK + (k + 1) * W5]
            xb_n = lambda k, par=par: XB2[:, (1 - par) * NTOK + k * W5:(1 - par) * NTOK + (k + 1) * W5]
            for k in range(NCH):
                bld.op("vector", "VE", lambda e, k=k, par=par: e.tensor_scalar(
                    out=XACT[:, k * W5:(k + 1) * W5], in0=xb_c(k, par),
                    scalar1=vc("cw1", k), scalar2=vc("conv_b", k), op0=ALU.mult, op1=ALU.add))
                bld.op("vector", "VE", lambda e, k=k, par=par: e.scalar_tensor_tensor(
                    out=XACT[:, k * W5 + 1:(k + 1) * W5], in0=xb_c(k, par)[:, 0:W5 - 1],
                    scalar=vc("cw0", k), in1=XACT[:, k * W5 + 1:(k + 1) * W5],
                    op0=ALU.mult, op1=ALU.add))
                if wi == 0:
                    bld.op("vector", "VE", lambda e, k=k: e.scalar_tensor_tensor(
                        out=XACT[:, k * W5:k * W5 + 1], in0=SM5[:, 184 + 2 * k:185 + 2 * k],
                        scalar=vc("cw0", k), in1=XACT[:, k * W5:k * W5 + 1],
                        op0=ALU.mult, op1=ALU.add))
                else:
                    bld.op("vector", "VE", lambda e, k=k: e.scalar_tensor_tensor(
                        out=XACT[:, k * W5:k * W5 + 1],
                        in0=SM5[:, 200 + k:201 + k],
                        scalar=vc("cw0", k), in1=XACT[:, k * W5:k * W5 + 1],
                        op0=ALU.mult, op1=ALU.add))
                bld.op("vector", "VE", lambda e, k=k, par=par: e.tensor_copy(
                    out=SM5[:, 200 + k:201 + k], in_=xb_c(k, par)[:, W5 - 1:W5]))
                bld.op("vector", "VE", lambda e, k=k, par=par: e.scalar_tensor_tensor(
                    out=XACT[:, k * W5:(k + 1) * W5 - 1], in0=xb_c(k, par)[:, 1:W5],
                    scalar=vc("cw2", k), in1=XACT[:, k * W5:(k + 1) * W5 - 1],
                    op0=ALU.mult, op1=ALU.add))
                if wi == NW5 - 1:
                    ctok = bld.op("vector", "VE", lambda e, k=k: e.scalar_tensor_tensor(
                        out=XACT[:, (k + 1) * W5 - 1:(k + 1) * W5], in0=SM5[:, 185 + 2 * k:186 + 2 * k],
                        scalar=vc("cw2", k), in1=XACT[:, (k + 1) * W5 - 1:(k + 1) * W5],
                        op0=ALU.mult, op1=ALU.add))
                else:
                    ctok = bld.op("vector", "VE", lambda e, k=k, par=par: e.scalar_tensor_tensor(
                        out=XACT[:, (k + 1) * W5 - 1:(k + 1) * W5], in0=xb_n(k, par)[:, 0:1],
                        scalar=vc("cw2", k), in1=XACT[:, (k + 1) * W5 - 1:(k + 1) * W5],
                        op0=ALU.mult, op1=ALU.add))
            bld.wait("scalar", "VE", ctok)
            sil = bld.op("scalar", "SC", lambda e: e.activation(
                out=XACT[:], in_=XACT[:], func=AF.Silu))
            T["s5_sil"] = sil
            bld.wait("tensor", "SC", sil)
            bld.wait("tensor", "LD", bld.cnt["LD"])
            bld.wait("tensor", "LG", bld.cnt["LG"])
            for k in range(NCH):
                fn = lambda e, k=k: e.matmul(
                    out=PD[0:64, 0:W5], lhsT=WSM[:, k * 96:k * 96 + 64],
                    rhs=XACT[:, k * W5:(k + 1) * W5],
                    start=(k == 0), stop=(k == NCH - 1), skip_group_check=True)
                if k == NCH - 1:
                    mtok = bld.op("tensor", "TN", fn)
                else:
                    bld.opn("tensor", fn)
            bld.wait("vector", "TN", mtok)
            mbt = bld.op("vector", "VE", lambda e: e.tensor_copy(out=MB[:], in_=PD[0:64, 0:W5]))
            bld.wait("tensor", "VE", mbt)

            def cons_delta(mo, slot, mt):
                bld.wait("scalar", "TN", mt)
                bld.op("scalar", "SC", lambda e, slot=slot, mo=mo: e.activation(
                    out=TMP5[:], in_=slot, func=AF.Exp, bias=vc("b_dt", mo)))
                st = bld.op("scalar", "SC", lambda e, mo=mo: e.activation(
                    out=DELTA[:, mo * W5:(mo + 1) * W5], in_=TMP5[:],
                    func=AF.Ln, bias=VEC[:, _vc("one"):_vc("one") + 1]))
                return ("SC", st)
            stage2(WDT, 64, cons_delta)
            for k in range(NCH):
                fn = lambda e, k=k: e.matmul(
                    out=PD[0:16, 0:W5], lhsT=WSM[:, k * 96 + 64:k * 96 + 80],
                    rhs=XACT[:, k * W5:(k + 1) * W5],
                    start=(k == 0), stop=(k == NCH - 1), skip_group_check=True)
                if k == NCH - 1:
                    mtok = bld.op("tensor", "TN", fn)
                else:
                    bld.opn("tensor", fn)
            bld.wait("vector", "TN", mtok)
            mbt = bld.op("vector", "VE", lambda e: e.tensor_copy(out=MB[0:16, :], in_=PD[0:16, 0:W5]))
            bld.wait("tensor", "VE", mbt)

            def cons_b(mo, slot, mt):
                bld.wait("vector", "TN", mt)
                bld.op("vector", "VE", lambda e, slot=slot, mo=mo: e.tensor_tensor(
                    out=TMP5[:], in0=DELTA[:, mo * W5:(mo + 1) * W5], in1=slot, op=ALU.mult))
                vt = bld.op("vector", "VE", lambda e, mo=mo: e.tensor_tensor(
                    out=DBX[:, mo * W5:(mo + 1) * W5], in0=TMP5[:],
                    in1=XACT[:, mo * W5:(mo + 1) * W5], op=ALU.mult))
                return ("VE", vt)
            stage2(WBP, 16, cons_b)
            for k in range(NCH):
                fn = lambda e, k=k: e.matmul(
                    out=PD[0:16, 0:W5], lhsT=WSM[:, k * 96 + 80:k * 96 + 96],
                    rhs=XACT[:, k * W5:(k + 1) * W5],
                    start=(k == 0), stop=(k == NCH - 1), skip_group_check=True)
                if k == NCH - 1:
                    mtok = bld.op("tensor", "TN", fn)
                else:
                    bld.opn("tensor", fn)
            bld.wait("vector", "TN", mtok)
            mbt = bld.op("vector", "VE", lambda e: e.tensor_copy(out=MB[0:16, :], in_=PD[0:16, 0:W5]))
            bld.wait("tensor", "VE", mbt)

            def cons_c(mo, slot, mt):
                bld.wait("vector", "TN", mt)
                vt = bld.op("vector", "VE", lambda e, slot=slot, mo=mo: e.tensor_copy(
                    out=CMS[:, mo * W5:(mo + 1) * W5], in_=slot))
                return ("VE", vt)
            stage2(WCP, 16, cons_c)
            T["s5_xact_tn"] = bld.cnt["TN"]
            bld.wait("scalar", "VE", bld.cnt["VE"])
            datok = None
            for k in range(NCH):
                datok = bld.op("scalar", "SC", lambda e, k=k: e.activation(
                    out=DELTA[:, k * W5:(k + 1) * W5], in_=DELTA[:, k * W5:(k + 1) * W5],
                    func=AF.Exp, scale=vc("A", k)))
            bld.wait("vector", "SC", datok)
            if T.get("gstok") is not None:
                bld.wait("vector", "LD", T["gstok"])
            gtok = None
            for k in range(NCH):
                init_h = 0.0 if wi == 0 else SM5[:, 160 + k:161 + k]
                init_p = 1.0 if wi == 0 else SM5[:, 168 + k:169 + k]
                bld.op("vector", "VE", lambda e, k=k, ih=init_h: e.tensor_tensor_scan(
                    out=HSS[:, k * W5:(k + 1) * W5], data0=DELTA[:, k * W5:(k + 1) * W5],
                    data1=DBX[:, k * W5:(k + 1) * W5], initial=ih,
                    op0=ALU.mult, op1=ALU.add))
                bld.op("vector", "VE", lambda e, k=k, ip=init_p: e.tensor_tensor_scan(
                    out=DBX[:, k * W5:(k + 1) * W5], data0=DELTA[:, k * W5:(k + 1) * W5],
                    data1=DELTA[:, k * W5:(k + 1) * W5], initial=ip,
                    op0=ALU.mult, op1=ALU.bypass))
                bld.op("vector", "VE", lambda e, k=k: e.tensor_copy(
                    out=SM5[:, 160 + k:161 + k], in_=HSS[:, (k + 1) * W5 - 1:(k + 1) * W5]))
                bld.op("vector", "VE", lambda e, k=k: e.tensor_copy(
                    out=SM5[:, 168 + k:169 + k], in_=DBX[:, (k + 1) * W5 - 1:(k + 1) * W5]))
                bld.op("vector", "VE", lambda e, k=k: e.tensor_tensor(
                    out=HSS[:, k * W5:(k + 1) * W5], in0=CMS[:, k * W5:(k + 1) * W5],
                    in1=HSS[:, k * W5:(k + 1) * W5], op=ALU.mult))
                bld.op("vector", "VE", lambda e, k=k: e.scalar_tensor_tensor(
                    out=HSS[:, k * W5:(k + 1) * W5], in0=XACT[:, k * W5:(k + 1) * W5],
                    scalar=vc("D_skip", k), in1=HSS[:, k * W5:(k + 1) * W5],
                    op0=ALU.mult, op1=ALU.add))
                bld.op("vector", "VE", lambda e, k=k, par=par: e.tensor_tensor(
                    out=GATED[:, k * W5:(k + 1) * W5], in0=HSS[:, k * W5:(k + 1) * W5],
                    in1=ZS2[:, par * NTOK + k * W5:par * NTOK + (k + 1) * W5], op=ALU.mult))
                bld.op("vector", "VE", lambda e, k=k: e.tensor_tensor(
                    out=CMS[:, k * W5:(k + 1) * W5], in0=CMS[:, k * W5:(k + 1) * W5],
                    in1=DBX[:, k * W5:(k + 1) * W5], op=ALU.mult))
                gtok = bld.op("vector", "VE", lambda e, k=k, par=par: e.tensor_tensor(
                    out=GW[:, k * W5:(k + 1) * W5], in0=CMS[:, k * W5:(k + 1) * W5],
                    in1=ZS2[:, par * NTOK + k * W5:par * NTOK + (k + 1) * W5], op=ALU.mult))
            if DBG == "s5w0" and wi == 0:
                bld.wait("gpsimd", "VE", bld.cnt["VE"])
                for nm, tt in (("XACT", XACT), ("DELTA", DELTA), ("DBX", DBX),
                               ("CMS", CMS), ("HSS", HSS)):
                    idx = {"XACT": 0, "DELTA": 1, "DBX": 2, "CMS": 3, "HSS": 4}[nm]
                    bld.dma("gpsimd", "LG", dr["dbg"][idx], tt[:])
                bld.dma("gpsimd", "LG", dr["dbg"][5], GATED[:])
                bld.dma("gpsimd", "LG", dr["dbg"][6], XN[:, 0:NTOK])
                bld.dma("gpsimd", "LG", dr["dbg"][7], XB2[:, 0:NTOK])
                bld.dma("gpsimd", "LG", dr["dbg"][8], ZS2[:, 0:NTOK])
                bld.dma("gpsimd", "LG", dr["dbg"][9], MBC[:])
                bld.dma("gpsimd", "LG", dr["dbg"][10], RSBC[:])
                bld.dma("gpsimd", "LG", dr["dbg"][11], GW[:])
                bld.barrier("s5dbg")
            bld.wait("sync", "VE", gtok)
            gstok = bld.dma("sync", "LD",
                            dr["gdef"][:, :, wi * W5:(wi + 1) * W5].rearrange("a p c -> p a c"),
                            GW[:].rearrange("p (a c) -> p a c", c=W5))
            T["gstok"] = gstok
            bld.wait("tensor", "VE", gtok)
            wtoks = {}
            for mo in range(3):
                wtoks[mo] = prefetch(dr["W_out"], mo, mo % 4)
            for mo in range(NCH):
                wpar = mo % 4
                bld.wait("tensor", "LG", wtoks[mo])
                if mo + 3 < NCH:
                    wtoks[mo + 3] = prefetch(dr["W_out"], mo + 3, (mo + 3) % 4)
                ja, pa = pool.acq()
                for k in range(NCH):
                    fn = lambda e, p=pa, wpar=wpar, k=k: e.matmul(
                        out=p[:, 0:W5], lhsT=WB[wpar][:, k * 128:(k + 1) * 128],
                        rhs=GATED[:, k * W5:(k + 1) * W5],
                        start=(k == 0), stop=(k == NCH - 1), skip_group_check=True)
                    if k == NCH - 1:
                        mtok = bld.op("tensor", "TN", fn)
                    else:
                        bld.opn("tensor", fn)
                wo_used[wpar] = mtok
                bld.wait("vector", "TN", mtok)
                xsl = XS[:, mo * NTOK + wi * W5:mo * NTOK + (wi + 1) * W5]
                bld.op("vector", "VE", lambda e, xsl=xsl, wi=wi: e.tensor_tensor(
                    out=TMP6[:], in0=xsl, in1=MBC[:, wi * W5:(wi + 1) * W5], op=ALU.subtract))
                bld.op("vector", "VE", lambda e, wi=wi: e.tensor_tensor(
                    out=TMP6[:], in0=TMP6[:], in1=RSBC[:, wi * W5:(wi + 1) * W5], op=ALU.mult))
                bld.op("vector", "VE", lambda e, p=pa: e.tensor_tensor(
                    out=TMP6[:], in0=TMP6[:], in1=p[:, 0:W5], op=ALU.add))
                vtok = bld.op("vector", "VE", lambda e, xsl=xsl: e.tensor_tensor(
                    out=xsl, in0=xsl, in1=TMP6[:], op=ALU.add))
                pool.rel(ja, "VE", vtok)
            if DBG == "s5win":
                bld.wait("gpsimd", "VE", bld.cnt["VE"])
                bld.dma("gpsimd", "LG", dr["dbg"][wi], XS[:, 0:NTOK])
                bld.dma("gpsimd", "LG", dr["dbg"][8 + wi], GATED[:])
                bld.barrier(f"s5wd{wi}")
            if DBG == "s5bar":
                bld.barrier(f"s5wb{wi}")
            if DBG == "s5tv":
                bld.wait("tensor", "VE", bld.cnt["VE"])
                bld.wait("vector", "TN", bld.cnt["TN"])
            if DBG == "s5ts":
                bld.wait("tensor", "SC", bld.cnt["SC"])
                bld.wait("scalar", "TN", bld.cnt["TN"])
                bld.wait("scalar", "VE", bld.cnt["VE"])
                bld.wait("vector", "SC", bld.cnt["SC"])
            if DBG == "s5gp":
                bld.wait("tensor", "LG", bld.cnt.get("LG", 0))
                bld.wait("gpsimd", "TN", bld.cnt["TN"])
                bld.wait("gpsimd", "VE", bld.cnt["VE"])
            if DBG == "s5f1":
                bld.wait("vector", "LD", bld.cnt.get("LD", 0))
            if DBG == "s5f2":
                bld.wait("scalar", "VE", bld.cnt["VE"])
            if DBG == "s5f3":
                bld.wait("tensor", "SC", bld.cnt["SC"])
            if DBG == "s5sy":
                bld.wait("sync", "VE", bld.cnt["VE"])
                bld.wait("sync", "TN", bld.cnt["TN"])
                bld.wait("vector", "LD", bld.cnt.get("LD", 0))
                bld.wait("tensor", "LD", bld.cnt.get("LD", 0))

        if DBG == "s5xs":
            bld.wait("gpsimd", "VE", bld.cnt["VE"])
            for kk in range(8):
                bld.dma("gpsimd", "LG", dr["dbg"][kk], XS[:, kk * NTOK:(kk + 1) * NTOK])
            bld.barrier("s5xsdbg")
        bld.wait("sync", "VE", bld.cnt["VE"])
        stok = bld.dma("sync", "LD", dr["aghi"], SM5[:, 160:168])
        bld.wait("gpsimd", "LD", stok)
        agtok = bld.op("gpsimd", "GP", lambda e: e.collective_compute(
            "AllGather", ALU.bypass, replica_groups=[list(range(8))],
            ins=[dr["aghi"]], outs=[dr["agho"]]))
        bld.wait("sync", "GP", agtok)
        rtok = bld.dma("sync", "LD", SM5[:, 16:80].rearrange("p (s c) -> p s c", c=8),
                       dr["agho"].rearrange("s p c -> p s c"))
        bld.wait("vector", "LD", rtok)
        h0tok = bld.op("vector", "VE", lambda e: e.tensor_scalar(
            out=SM5[:, 176:184], in0=SM5[:, 16:24], scalar1=VEC[:, sp0:sp0 + 1],
            scalar2=None, op0=ALU.mult))
        for s in range(1, 8):
            h0tok = bld.op("vector", "VE", lambda e, s=s: e.scalar_tensor_tensor(
                out=SM5[:, 176:184], in0=SM5[:, 16 + 8 * s:24 + 8 * s],
                scalar=VEC[:, sp0 + s:sp0 + s + 1], in1=SM5[:, 176:184],
                op0=ALU.mult, op1=ALU.add))
        for wi in range(NW5):
            grd = bld.dma("sync", "LD", GW[:].rearrange("p (a c) -> p a c", c=W5),
                          dr["gdef"][:, :, wi * W5:(wi + 1) * W5].rearrange("a p c -> p a c"))
            bld.wait("vector", "LD", grd)
            gct = None
            for k in range(NCH):
                gct = bld.op("vector", "VE", lambda e, k=k: e.tensor_scalar(
                    out=GW[:, k * W5:(k + 1) * W5], in0=GW[:, k * W5:(k + 1) * W5],
                    scalar1=SM5[:, 176 + k:177 + k], scalar2=None, op0=ALU.mult))
            bld.wait("tensor", "VE", gct)
            if DBG == "s5g" and wi == 0:
                bld.wait("gpsimd", "VE", bld.cnt["VE"])
                bld.dma("gpsimd", "LG", dr["dbg"][12], GW[:])
                bld.dma("gpsimd", "LG", dr["dbg"][13][:, 0:24], SM5[:, 160:184])
                bld.barrier("s5gdbg")
            wtoks = {}
            for mo in range(3):
                wtoks[mo] = prefetch(dr["W_out"], mo, mo % 4)
            for mo in range(NCH):
                wpar = mo % 4
                bld.wait("tensor", "LG", wtoks[mo])
                if mo + 3 < NCH:
                    wtoks[mo + 3] = prefetch(dr["W_out"], mo + 3, (mo + 3) % 4)
                ja, pa = pool.acq()
                for k in range(NCH):
                    fn = lambda e, p=pa, wpar=wpar, k=k: e.matmul(
                        out=p[:, 0:W5], lhsT=WB[wpar][:, k * 128:(k + 1) * 128],
                        rhs=GW[:, k * W5:(k + 1) * W5],
                        start=(k == 0), stop=(k == NCH - 1), skip_group_check=True)
                    if k == NCH - 1:
                        mtok = bld.op("tensor", "TN", fn)
                    else:
                        bld.opn("tensor", fn)
                wo_used[wpar] = mtok
                bld.wait("vector", "TN", mtok)
                xsl = XS[:, mo * NTOK + wi * W5:mo * NTOK + (wi + 1) * W5]
                vtok = bld.op("vector", "VE", lambda e, xsl=xsl, p=pa: e.tensor_tensor(
                    out=xsl, in0=xsl, in1=p[:, 0:W5], op=ALU.add))
                pool.rel(ja, "VE", vtok)
            bld.wait("sync", "TN", bld.cnt["TN"])
    bld.barrier(tag)


def build_program():
    nc = bass.Bass("TRN2", target_bir_lowering=False)
    NV = len(VCOLS)

    dr = {}
    dr["x"] = nc.dram_tensor("x", [NCH, 128, NTOK], F32, kind="ExternalInput").ap()
    dr["out"] = nc.dram_tensor("out", [NCH, 128, NTOK], F32, kind="ExternalOutput").ap()
    dr["dbg"] = nc.dram_tensor("dbg", [16, 128, NTOK], BF16, kind="ExternalOutput").ap()
    dr["vec"] = nc.dram_tensor("vec", [128, len(VCOLS)], F32, kind="ExternalInput").ap()
    dr["ones_m"] = nc.dram_tensor("ones_m", [128, 128], BF16, kind="ExternalInput").ap()
    dr["ones_r"] = nc.dram_tensor("ones_r", [1, 128], BF16, kind="ExternalInput").ap()
    dr["ident"] = nc.dram_tensor("ident", [128, 128], BF16, kind="ExternalInput").ap()
    dr["wgT"] = nc.dram_tensor("wgT", [8, 128, HID], I32, kind="ExternalInput").ap()
    dr["wuT"] = nc.dram_tensor("wuT", [8, 128, HID], I32, kind="ExternalInput").ap()
    dr["wdT"] = nc.dram_tensor("wdT", [24, 128, D], I32, kind="ExternalInput").ap()
    dr["uT"] = nc.dram_tensor("uT", [8, 128, 1024], F32, kind="ExternalInput").ap()
    dr["vT"] = nc.dram_tensor("vT", [8, 128, 1024], F32, kind="ExternalInput").ap()
    dr["u_t"] = nc.dram_tensor("u_t", [8, 128, 1024], F32, kind="ExternalInput").ap()
    dr["v_t"] = nc.dram_tensor("v_t", [8, 128, 1024], F32, kind="ExternalInput").ap()
    dr["ag1_in"] = nc.dram_tensor("ag1_in", [2, 8, 128, NTOK], BF16, kind="Internal").ap()
    dr["ag1_out"] = nc.dram_tensor("ag1_out", [8, 2, 8, 128, NTOK], BF16, kind="Internal").ap()
    dr["ag2_in"] = nc.dram_tensor("ag2_in", [2, 128, 16384], BF16, kind="Internal").ap()
    dr["ag2_out"] = nc.dram_tensor("ag2_out", [8, 2, 128, 16384], BF16, kind="Internal").ap()
    dr["WrT"] = nc.dram_tensor("WrT", [8, 128, 1024], F32, kind="ExternalInput").ap()
    dr["WkT"] = nc.dram_tensor("WkT", [8, 128, 1024], F32, kind="ExternalInput").ap()
    dr["WvT"] = nc.dram_tensor("WvT", [8, 128, 1024], F32, kind="ExternalInput").ap()
    dr["agx4i"] = nc.dram_tensor("agx4i", [128, 8], F32, kind="Internal").ap()
    dr["agx4o"] = nc.dram_tensor("agx4o", [8, 128, 8], F32, kind="Internal").ap()
    dr["agx6i"] = nc.dram_tensor("agx6i", [128, 8], F32, kind="Internal").ap()
    dr["agx6o"] = nc.dram_tensor("agx6o", [8, 128, 8], F32, kind="Internal").ap()
    dr["agx5i"] = nc.dram_tensor("agx5i", [128, 16], F32, kind="Internal").ap()
    dr["agx5o"] = nc.dram_tensor("agx5o", [8, 128, 16], F32, kind="Internal").ap()
    dr["aghi"] = nc.dram_tensor("aghi", [128, 8], F32, kind="Internal").ap()
    dr["agho"] = nc.dram_tensor("agho", [8, 128, 8], F32, kind="Internal").ap()
    dr["gdef"] = nc.dram_tensor("gdef", [8, 128, NTOK], BF16, kind="Internal").ap()
    dr["W_in"] = nc.dram_tensor("W_in", [8, 128, 2048], F32, kind="ExternalInput").ap()
    dr["W_out"] = nc.dram_tensor("W_out", [8, 128, 1024], F32, kind="ExternalInput").ap()
    dr["Wsm"] = nc.dram_tensor("Wsm", [8, 128, 96], F32, kind="ExternalInput").ap()
    dr["W_dt"] = nc.dram_tensor("W_dt", [64, 1024], F32, kind="ExternalInput").ap()
    dr["W_Bp"] = nc.dram_tensor("W_Bp", [16, 1024], F32, kind="ExternalInput").ap()
    dr["W_Cp"] = nc.dram_tensor("W_Cp", [16, 1024], F32, kind="ExternalInput").ap()

    es_top = ExitStack()
    bld = Bld(nc, es_top)
    with es_top:
        XS = es_top.enter_context(nc.sbuf_tensor("XS", [128, NCH * NTOK], F32))
        XN = es_top.enter_context(nc.sbuf_tensor("XN", [128, NCH * NTOK], BF16))
        VEC = es_top.enter_context(nc.sbuf_tensor("VEC", [128, NV], F32))
        ONESM = es_top.enter_context(nc.sbuf_tensor("ONESM", [128, 128], BF16))
        ONESR = es_top.enter_context(nc.sbuf_tensor("ONESR", [1, 128], BF16))
        IDENT = es_top.enter_context(nc.sbuf_tensor("IDENT", [128, 128], BF16))
        ROWS = es_top.enter_context(nc.sbuf_tensor("ROWS", [128, NTOK], F32))
        bld.dum = es_top.enter_context(nc.sbuf_tensor("DUM", [1, 8], F32))

        for k in range(NCH):
            bld.dma("sync", "LD", chs(XS, k), dr["x"][k])
        bld.dma("sync", "LD", VEC[:], dr["vec"])
        bld.dma("sync", "LD", ONESM[:], dr["ones_m"])
        bld.dma("sync", "LD", ONESR[:], dr["ones_r"])
        ldtok = bld.dma("sync", "LD", IDENT[:], dr["ident"])
        for e in ENGS:
            bld.wait(e, "LD", ldtok)

        T = {"VEC": VEC, "es": None}

        for sub in SUBS:
            if sub in (1, 3):
                with ExitStack() as es:
                    PA = es.enter_context(nc.psum_tensor("PA%d" % sub, [128, 2048], F32))
                    PB = es.enter_context(nc.psum_tensor("PB%d" % sub, [128, 2048], F32))
                    LXB = es.enter_context(nc.sbuf_tensor("LXB%d" % sub, [128, 2 * NTOK], BF16))
                    LSQ = es.enter_context(nc.sbuf_tensor("LSQ%d" % sub, [128, 2 * NTOK], BF16))
                    LE1 = es.enter_context(nc.sbuf_tensor("LE1%d" % sub, [128, NTOK], F32))
                    HACT = es.enter_context(nc.sbuf_tensor("HACT%d" % sub, [128, 6 * NTOK], BF16))
                    WG = [es.enter_context(nc.sbuf_tensor(f"WG{sub}_{i}", [128, 1024], BF16)) for i in range(2)]
                    WU = [es.enter_context(nc.sbuf_tensor(f"WU{sub}_{i}", [128, 1024], BF16)) for i in range(2)]
                    WD = es.enter_context(nc.sbuf_tensor(f"WD{sub}", [128, 6 * 1024], BF16))
                    pool = Pool(bld, [PA[:, i * W:(i + 1) * W] for i in range(4)]
                                + [PB[:, i * W:(i + 1) * W] for i in range(4)])
                    ln = emit_ln(bld, T, XS, PA, PB, XN, ONESM, ONESR, VEC, LXB, LSQ, LE1,
                                 ROWS, f"ln{sub}")
                    bld.wait("tensor", "VE", ln["applied"])
                    if DBG == "xn" and sub == 1:
                        bld.wait("sync", "VE", ln["applied"])
                        for k in range(NCH):
                            bld.dma("sync", "ST", dr["dbg"][k], chs(XN, k))
                    T2 = dict(T, HACT=HACT, WG=WG, WU=WU, WD=WD, TMPW=LE1[:, 0:W],
                              wgT_h=dr["wgT"], wuT_h=dr["wuT"], wdT_h=dr["wdT"])
                    emit_l0ffn(bld, T2, XS, XN, PA, PB, pool, f"ffn{sub}")
                    if DBG == "hact" and sub == 1:
                        bld.wait("sync", "VE", bld.cnt["VE"])
                        for k in range(12):
                            bld.dma("sync", "ST", dr["dbg"][k], HACT[:, k * NTOK:(k + 1) * NTOK])
                    bld.barrier(f"sub{sub}")
            elif sub == 2:
                with ExitStack() as es:
                    PA = es.enter_context(nc.psum_tensor("PA%d" % sub, [128, 2048], F32))
                    PB = es.enter_context(nc.psum_tensor("PB%d" % sub, [128, 2048], F32))
                    LXB = es.enter_context(nc.sbuf_tensor("LXB%d" % sub, [128, 2 * NTOK], BF16))
                    LSQ = es.enter_context(nc.sbuf_tensor("LSQ%d" % sub, [128, 2 * NTOK], BF16))
                    LE1 = es.enter_context(nc.sbuf_tensor("LE1%d" % sub, [128, NTOK], F32))
                    emit_ln(bld, T, XS, PA, PB, XN, ONESM, ONESR, VEC, LXB, LSQ, LE1,
                            ROWS, "ln2")
                    bld.barrier("ln2b")
                    if DBG == "xn2":
                        for kk in range(NCH):
                            bld.dma("sync", "ST", dr["dbg"][kk], chs(XN, kk))
                emit_l0attn(bld, T, XS, XN, dr, None, ONESM, ONESR, IDENT, VEC)
            elif sub == 5:
                emit_l1attn(bld, T, XS, XN, dr, ONESM, ONESR, VEC, ROWS)
            elif sub in (4, 6):
                emit_l1ffn(bld, T, XS, XN, dr, ONESM, ONESR, VEC, ROWS,
                           dr["agx4i" if sub == 4 else "agx6i"],
                           dr["agx4o" if sub == 4 else "agx6o"], f"s{sub}")
        st = None
        for k in range(NCH):
            st = bld.dma("sync", "ST", dr["out"][k], chs(XS, k))
        bld.wait("sync", "ST", st)

        with (nc.Block() as blk):
            @blk.sync
            def _(e):
                for f in bld.q["sync"]:
                    f(e)

            @blk.tensor
            def _(e):
                for f in bld.q["tensor"]:
                    f(e)

            @blk.vector
            def _(e):
                for f in bld.q["vector"]:
                    f(e)

            @blk.scalar
            def _(e):
                for f in bld.q["scalar"]:
                    f(e)

            @blk.gpsimd
            def _(e):
                for f in bld.q["gpsimd"]:
                    f(e)
    return nc


def host_inputs(inputs):
    x = np.ascontiguousarray(inputs["L0_attn_qz"], dtype=np.float32)
    xT = np.ascontiguousarray(x.reshape(16384, 1024).T)
    wgT = np.ascontiguousarray(inputs["L0_ffn_w_gate"].T).reshape(8, 128, HID)
    wuT = np.ascontiguousarray(inputs["L0_ffn_w_up"].T).reshape(8, 128, HID)
    wdT = np.ascontiguousarray(inputs["L0_ffn_w_down"].T).reshape(24, 128, D)
    u = np.ascontiguousarray(inputs["L0_attn_ternary_factor_u"], np.float32)
    v = np.ascontiguousarray(inputs["L0_attn_ternary_factor_v"], np.float32)
    uT = np.ascontiguousarray(u.T).reshape(8, 128, 1024)
    vT = np.ascontiguousarray(v.T).reshape(8, 128, 1024)
    u_t = u.reshape(8, 128, 1024)
    v_t = v.reshape(8, 128, 1024)
    WrT = np.ascontiguousarray(inputs["L1_ffn_W_r"].T).reshape(8, 128, 1024)
    WkT = np.ascontiguousarray(inputs["L1_ffn_W_k"].T).reshape(8, 128, 1024)
    WvT = np.ascontiguousarray(inputs["L1_ffn_W_v"].T).reshape(8, 128, 1024)
    W_in = np.ascontiguousarray(inputs["L1_attn_W_in"], np.float32).reshape(8, 128, 2048)
    W_out = np.ascontiguousarray(inputs["L1_attn_W_out"], np.float32).reshape(8, 128, 1024)
    Wsm = np.concatenate([
        np.asarray(inputs["L1_attn_W_xp"], np.float32).reshape(8, 128, 64),
        np.asarray(inputs["L1_attn_W_Bg"], np.float32).reshape(8, 128, 16),
        np.asarray(inputs["L1_attn_W_Cg"], np.float32).reshape(8, 128, 16)], axis=2)
    Wsm = np.ascontiguousarray(Wsm)

    NV = len(VCOLS)
    vec_base = np.zeros((128, NV), np.float32)

    def setv(name, val):
        vec_base[:, _vc(name)] = val

    def setvk(name, arr):
        a = np.asarray(arr, np.float32).reshape(8, 128).T
        for k in range(8):
            vec_base[:, _vc(f"{name}{k}")] = a[:, k]

    setv("eps", EPS)
    setv("one", 1.0)
    setv("neghalf", -0.5)
    setv("s127", S127)
    setvk("gamma", inputs["L1_attn_gamma"])
    setvk("beta", inputs["L1_attn_beta"])
    setvk("conv_b", inputs["L1_attn_conv_b"])
    setvk("b_dt", inputs["L1_attn_b_dt"])
    setvk("A", inputs["L1_attn_A"])
    setvk("D_skip", inputs["L1_attn_D_skip"])
    setvk("mu_r", inputs["L1_ffn_mu_r"])
    setvk("mu_k", inputs["L1_ffn_mu_k"])
    cw = np.asarray(inputs["L1_attn_conv_w"], np.float32)
    setvk("cw0", cw[0, 0])
    setvk("cw1", cw[1, 0])
    setvk("cw2", cw[2, 0])

    ones_m = np.full((128, 128), 1.0 / 1024.0, BF)
    ones_r = np.full((1, 128), 1.0, BF)
    ident = np.eye(128, dtype=BF)

    in_maps = []
    for c in range(NCORE):
        vec = vec_base.copy()
        selp = np.zeros(8, np.float32)
        seln = np.zeros(8, np.float32)
        if c % 2 == 1:
            selp[c - 1] = 1.0
        if c % 2 == 0:
            seln[c + 1] = 1.0
        for k in range(8):
            vec[:, _vc(f"selprev{k}")] = selp[k]
            vec[:, _vc(f"selnext{k}")] = seln[k]
        xc = np.ascontiguousarray(xT[:, c * NTOK:(c + 1) * NTOK]).reshape(NCH, 128, NTOK)
        in_maps.append(dict(x=xc, vec=vec, ones_m=ones_m, ones_r=ones_r, ident=ident,
                            wgT=wgT, wuT=wuT, wdT=wdT, uT=uT, vT=vT, u_t=u_t, v_t=v_t,
                            WrT=WrT, WkT=WkT, WvT=WvT, W_in=W_in, W_out=W_out,
                            Wsm=Wsm,
                            W_dt=np.ascontiguousarray(inputs["L1_attn_W_dt"], np.float32),
                            W_Bp=np.ascontiguousarray(inputs["L1_attn_W_Bp"], np.float32),
                            W_Cp=np.ascontiguousarray(inputs["L1_attn_W_Cp"], np.float32)))
    return in_maps


def kernel(**inputs):
    nc = build_program()
    in_maps = host_inputs(inputs)
    res = run_bass_kernel_spmd(nc, in_maps, core_ids=list(range(NCORE)))
    outT = np.concatenate([r["out"].reshape(1024, NTOK) for r in res.results], axis=1)
    return np.ascontiguousarray(outT.T).reshape(4, 4096, 1024)


# version 03
# speedup vs baseline: 1.1217x; 1.0528x over previous
import numpy as np
import ml_dtypes
from contextlib import ExitStack
import concourse.bass as bass
import concourse.mybir as mybir
from concourse.bass_utils import run_bass_kernel_spmd

F32 = mybir.dt.float32
BF16 = mybir.dt.bfloat16
I32 = mybir.dt.int32
AF = mybir.ActivationFunctionType
ALU = mybir.AluOpType
BF = ml_dtypes.bfloat16

NCORE = 8
NTOK = 2048
NCH = 8
D = 1024
W = 512
NWIN = NTOK // W
SEQ = 4096
HID = 3072
SC_ATT = np.float32(0.03125)
S127 = np.float32(1.0 / 127.0)
EPS = np.float32(1e-5)

SUBS = (1, 2, 3, 4, 5, 6)
ATTN_STOP = None
DBG = None

ENGS = ("sync", "tensor", "vector", "scalar", "gpsimd")

VCOLS = {}
def _vc(name):
    if name not in VCOLS:
        VCOLS[name] = len(VCOLS)
    return VCOLS[name]
for _n in ("gamma", "beta", "conv_b", "b_dt", "A", "D_skip", "mu_r", "mu_k",
           "selprev", "selnext"):
    for _k in range(8):
        _vc(f"{_n}{_k}")
for _n in ("cw0", "cw1", "cw2"):
    for _k in range(8):
        _vc(f"{_n}{_k}")
for _n in ("eps", "one", "neghalf", "s127"):
    _vc(_n)


class Bld:
    def __init__(self, nc, es):
        self.nc = nc
        self.es = es
        self.q = {e: [] for e in ENGS}
        self.sems = {}
        self.cnt = {}

    def sem(self, name):
        if name not in self.sems:
            self.sems[name] = self.es.enter_context(self.nc.semaphore(name))
            self.cnt[name] = 0
        return self.sems[name]

    def emit(self, eng, fn):
        self.q[eng].append(fn)

    def op(self, eng, sname, build, inc=1):
        h = self.sem(sname)
        self.cnt[sname] += inc
        tok = self.cnt[sname]
        self.emit(eng, lambda e, b=build, h=h, i=inc: b(e).then_inc(h, i))
        return tok

    def opn(self, eng, build):
        self.emit(eng, lambda e, b=build: b(e))

    def wait(self, eng, sname, tok):
        h = self.sem(sname)
        self.emit(eng, lambda e, h=h, t=tok: e.wait_ge(h, t))

    def dma(self, eng, sname, out, in_):
        return self.op(eng, sname, lambda e, o=out, i=in_: e.dma_start(out=o, in_=i), inc=16)

    def barrier(self, tag):
        toks = {}
        toks["tensor"] = self.op("tensor", "bar_tensor", lambda en: en.nop())
        toks["vector"] = self.op("vector", "bar_vector", lambda en: en.tensor_copy(
            out=self.dum[0:1, 0:1], in_=self.dum[0:1, 1:2]))
        toks["scalar"] = self.op("scalar", "bar_scalar", lambda en: en.copy(
            out=self.dum[0:1, 2:3], in_=self.dum[0:1, 3:4]))
        toks["gpsimd"] = self.op("gpsimd", "bar_gpsimd", lambda en: en.tensor_copy(
            out=self.dum[0:1, 4:5], in_=self.dum[0:1, 5:6]))
        toks["sync"] = self.op("sync", "bar_sync", lambda en: en.nop())
        ldc = self.cnt.get("LD", 0)
        stc = self.cnt.get("ST", 0)
        lgc = self.cnt.get("LG", 0)
        for e in ENGS:
            for o in ENGS:
                if o != e:
                    self.wait(e, f"bar_{o}", toks[o])
            if ldc:
                self.wait(e, "LD", ldc)
            if stc:
                self.wait(e, "ST", stc)
            if lgc:
                self.wait(e, "LG", lgc)


class Pool:
    def __init__(self, bld, slots):
        self.bld = bld
        self.slots = slots
        self.freetok = [None] * len(slots)
        self.i = 0

    def acq(self):
        j = self.i % len(self.slots)
        self.i += 1
        if self.freetok[j] is not None:
            sn, tok = self.freetok[j]
            self.bld.wait("tensor", sn, tok)
        return j, self.slots[j]

    def rel(self, j, sname, tok):
        self.freetok[j] = (sname, tok)


def chs(t, k, w=None):
    if w is None:
        return t[:, k * NTOK:(k + 1) * NTOK]
    return t[:, k * NTOK + w * W: k * NTOK + w * W + W]


def emit_ln(bld, T, XS, PA, PB, dst_bf, ones_m, ones_r, VEC, LXB, LSQ, LE1, ROWS,
           tag, gamma_beta=None, combined=False, apply=True):
    last = None
    tn_by_par = [None, None]
    for k in range(NCH):
        par = k % 2
        lxb = LXB[:, par * NTOK:(par + 1) * NTOK]
        lsq = LSQ[:, par * NTOK:(par + 1) * NTOK]
        if tn_by_par[par] is not None:
            bld.wait("vector", "TN", tn_by_par[par])
        bld.op("vector", "VE", lambda e, lxb=lxb, k=k: e.tensor_copy(out=lxb, in_=chs(XS, k)))
        ctok = bld.op("vector", "VE", lambda e, lxb=lxb, lsq=lsq: e.tensor_tensor(
            out=lsq, in0=lxb, in1=lxb, op=ALU.mult))
        bld.wait("tensor", "VE", ctok)
        for w in range(NWIN):
            bld.opn("tensor", lambda e, w=w, k=k, lxb=lxb: e.matmul(
                out=PA[:, w * W:(w + 1) * W], lhsT=ones_m[:], rhs=lxb[:, w * W:(w + 1) * W],
                start=(k == 0), stop=(k == NCH - 1), skip_group_check=True))
            last = bld.op("tensor", "TN", lambda e, w=w, k=k, lsq=lsq: e.matmul(
                out=PB[:, w * W:(w + 1) * W], lhsT=ones_m[:], rhs=lsq[:, w * W:(w + 1) * W],
                start=(k == 0), stop=(k == NCH - 1), skip_group_check=True))
        tn_by_par[par] = last
    bld.wait("vector", "TN", last)
    bld.op("vector", "VE", lambda e: e.tensor_copy(out=ROWS[0:1, 0:NTOK], in_=PA[0:1, :]))
    bld.op("vector", "VE", lambda e: e.tensor_tensor(
        out=ROWS[96:97, 0:NTOK], in0=ROWS[0:1, 0:NTOK], in1=ROWS[0:1, 0:NTOK], op=ALU.mult))
    vtok = bld.op("vector", "VE", lambda e: e.tensor_tensor(
        out=ROWS[32:33, 0:NTOK], in0=PB[0:1, :], in1=ROWS[96:97, 0:NTOK], op=ALU.subtract))
    bld.wait("scalar", "VE", vtok)
    bld.op("scalar", "SC", lambda e: e.activation(
        out=ROWS[96:97, 0:NTOK], in_=ROWS[32:33, 0:NTOK], func=AF.Ln,
        bias=VEC[0:1, _vc("eps"):_vc("eps") + 1]))
    stok = bld.op("scalar", "SC", lambda e: e.activation(
        out=ROWS[64:65, 0:NTOK], in_=ROWS[96:97, 0:NTOK], func=AF.Exp,
        scale=VEC[0:1, _vc("neghalf"):_vc("neghalf") + 1]))
    if combined:
        bld.wait("vector", "SC", stok)
        bld.op("vector", "VE", lambda e: e.tensor_tensor(
            out=ROWS[32:33, NTOK:2 * NTOK], in0=ROWS[64:65, 0:NTOK],
            in1=ROWS[64:65, 0:NTOK], op=ALU.mult))
        v2tok = bld.op("vector", "VE", lambda e: e.tensor_tensor(
            out=ROWS[96:97, 0:NTOK], in0=ROWS[32:33, 0:NTOK],
            in1=ROWS[32:33, NTOK:2 * NTOK], op=ALU.mult))
        bld.wait("scalar", "VE", v2tok)
        bld.op("scalar", "SC", lambda e: e.activation(
            out=ROWS[96:97, 0:NTOK], in_=ROWS[96:97, 0:NTOK], func=AF.Ln,
            bias=VEC[0:1, _vc("eps"):_vc("eps") + 1]))
        stok = bld.op("scalar", "SC", lambda e: e.activation(
            out=ROWS[64:65, NTOK:2 * NTOK], in_=ROWS[96:97, 0:NTOK], func=AF.Exp,
            scale=VEC[0:1, _vc("neghalf"):_vc("neghalf") + 1]))
        bld.wait("vector", "SC", stok)
        stok2 = bld.op("vector", "VE", lambda e: e.tensor_tensor(
            out=ROWS[96:97, 0:NTOK], in0=ROWS[64:65, 0:NTOK],
            in1=ROWS[64:65, NTOK:2 * NTOK], op=ALU.mult))
        rs_src = ROWS[96:97, 0:NTOK]
        bcast_wait = ("VE", stok2)
    else:
        rs_src = ROWS[64:65, 0:NTOK]
        bcast_wait = ("SC", stok)
    bld.wait("vector", bcast_wait[0], bcast_wait[1])
    rbtok = bld.op("vector", "VE", lambda e, s=rs_src: e.tensor_copy(out=LXB[0:1, 0:NTOK], in_=s))
    bld.wait("tensor", "VE", rbtok)
    btoks = []
    for w in range(NWIN):
        btoks.append(bld.op("tensor", "TN", lambda e, w=w: e.matmul(
            out=PB[:, w * W:(w + 1) * W], lhsT=ones_r[:], rhs=LXB[0:1, w * W:(w + 1) * W],
            start=True, stop=True)))
    bld.wait("vector", "TN", btoks[-1])
    if not apply:
        return {"applied": None}
    atoks = []
    for k in range(NCH):
        bld.op("vector", "VE", lambda e, k=k: e.tensor_tensor(
            out=LE1[:], in0=chs(XS, k), in1=PA[:, :], op=ALU.subtract))
        if gamma_beta is None:
            atoks.append(bld.op("vector", "VE", lambda e, k=k: e.tensor_tensor(
                out=chs(dst_bf, k), in0=LE1[:], in1=PB[:, :], op=ALU.mult)))
        else:
            gcol, bcol = gamma_beta
            bld.op("vector", "VE", lambda e, k=k: e.tensor_tensor(
                out=LE1[:], in0=LE1[:], in1=PB[:, :], op=ALU.mult))
            atoks.append(bld.op("vector", "VE", lambda e, k=k, g=gcol, b=bcol: e.tensor_scalar(
                out=chs(dst_bf, k), in0=LE1[:],
                scalar1=VEC[:, _vc(g + str(k)):_vc(g + str(k)) + 1],
                scalar2=VEC[:, _vc(b + str(k)):_vc(b + str(k)) + 1],
                op0=ALU.mult, op1=ALU.add)))
    return {"applied": atoks[-1], "atoks": atoks}


def emit_l0ffn(bld, T, XS, XN, PA, PB, pool, tag):
    HACT = T["HACT"]
    WG = T["WG"]; WU = T["WU"]
    WD = T["WD"]
    TMP = T["TMPW"]
    wg_h, wu_h, wd_h = T["wgT_h"], T["wuT_h"], T["wdT_h"]
    VEC = T["VEC"]

    wd_used = None
    used_tok = [None, None]
    for piece in range(4):
        if wd_used is not None:
            bld.wait("gpsimd", "TN", wd_used)
        wd_tok = bld.dma(
            "gpsimd", "LG", WD[:].rearrange("p (a c) -> p a c", c=1024),
            wd_h[piece * 6:(piece + 1) * 6, :, :].rearrange("a p c -> p a c"))
        mtok = None
        for mc in range(6):
            hc = piece * 6 + mc
            par = mc % 2
            if used_tok[par] is not None:
                bld.wait("gpsimd", "TN", used_tok[par])
            bld.dma("gpsimd", "LG", WG[par][:].rearrange("p (a c) -> p a c", c=128),
                    wg_h[:, :, hc * 128:(hc + 1) * 128].rearrange("a p c -> p a c"))
            utok = bld.dma("gpsimd", "LG", WU[par][:].rearrange("p (a c) -> p a c", c=128),
                    wu_h[:, :, hc * 128:(hc + 1) * 128].rearrange("a p c -> p a c"))
            bld.wait("tensor", "LG", utok)
            for w in range(NWIN):
                ja, pa = pool.acq()
                jb, pb = pool.acq()
                for k in range(NCH):
                    bld.opn("tensor", lambda e, p=pa, par=par, k=k, w=w: e.matmul(
                        out=p, lhsT=WG[par][:, k * 128:(k + 1) * 128], rhs=chs(XN, k, w),
                        start=(k == 0), stop=(k == NCH - 1)))
                for k in range(NCH):
                    fn = lambda e, p=pb, par=par, k=k, w=w: e.matmul(
                        out=p, lhsT=WU[par][:, k * 128:(k + 1) * 128], rhs=chs(XN, k, w),
                        start=(k == 0), stop=(k == NCH - 1))
                    if k == NCH - 1:
                        mtok = bld.op("tensor", "TN", fn)
                    else:
                        bld.opn("tensor", fn)
                used_tok[par] = mtok
                bld.wait("scalar", "TN", mtok)
                if T.get("prev_htok") is not None:
                    bld.wait("scalar", "VE", T["prev_htok"])
                sttok = bld.op("scalar", "SC", lambda e, p=pa: e.activation(
                    out=TMP, in_=p, func=AF.Silu, scale=VEC[:, _vc("s127"):_vc("s127") + 1]))
                bld.wait("vector", "SC", sttok)
                htok = bld.op("vector", "VE", lambda e, p=pb, mc=mc, w=w: e.scalar_tensor_tensor(
                    out=HACT[:, mc * NTOK + w * W: mc * NTOK + w * W + W],
                    in0=p, scalar=float(S127), in1=TMP, op0=ALU.mult, op1=ALU.mult))
                T["prev_htok"] = htok
                pool.rel(ja, "SC", sttok)
                pool.rel(jb, "VE", htok)
        bld.wait("tensor", "LG", wd_tok)
        bld.wait("tensor", "VE", T["prev_htok"])
        for w in range(NWIN):
            for dmc in range(NCH):
                ja, pa = pool.acq()
                for kk in range(6):
                    fn = lambda e, p=pa, kk=kk, dmc=dmc, w=w: e.matmul(
                        out=p, lhsT=WD[:, kk * 1024 + dmc * 128: kk * 1024 + (dmc + 1) * 128],
                        rhs=HACT[:, kk * NTOK + w * W: kk * NTOK + w * W + W],
                        start=(kk == 0), stop=(kk == 5))
                    if kk == 5:
                        mtok = bld.op("tensor", "TN", fn)
                    else:
                        bld.opn("tensor", fn)
                wd_used = mtok
                bld.wait("vector", "TN", mtok)
                vtok = bld.op("vector", "VE", lambda e, p=pa, dmc=dmc, w=w: e.scalar_tensor_tensor(
                    out=chs(XS, dmc, w), in0=p, scalar=float(0.5 * S127),
                    in1=chs(XS, dmc, w), op0=ALU.mult, op1=ALU.add))
                pool.rel(ja, "VE", vtok)


def emit_qzproj(bld, T, XN, UT, dst, pool, scale):
    for mo in range(NCH):
        for w in range(NWIN):
            ja, pa = pool.acq()
            for k in range(NCH):
                fn = lambda e, p=pa, k=k, mo=mo, w=w: e.matmul(
                    out=p, lhsT=UT[:, k * 1024 + mo * 128:k * 1024 + (mo + 1) * 128],
                    rhs=chs(XN, k, w), start=(k == 0), stop=(k == NCH - 1))
                if k == NCH - 1:
                    mtok = bld.op("tensor", "TN", fn)
                else:
                    bld.opn("tensor", fn)
            bld.wait("vector", "TN", mtok)
            vtok = bld.op("vector", "VE", lambda e, p=pa, mo=mo, w=w: e.tensor_scalar(
                out=chs(dst, mo, w), in0=p, scalar1=float(scale), scalar2=None, op0=ALU.mult))
            pool.rel(ja, "VE", vtok)


def emit_score_pass(bld, T, b, STAT, MOV, STT, PF, PV, QH, VOUT, tag):
    b0 = b * SEQ
    bld.wait("scalar", "TN", bld.cnt["TN"])
    bld.wait("tensor", "SC", bld.cnt["SC"])
    sigtok = [None, None]
    vlast = [None, None, None, None]
    vtok_hist = {}
    evactok = [None, None]
    for J in range(8):
        jp = J % 2
        pv = PV[:, jp * 512:(jp + 1) * 512]
        if evactok[jp] is not None:
            bld.wait("tensor", "VE", evactok[jp])
        pend = None
        for i in range(32):
            par = i % 2
            par4 = i % 4
            fb = PF[:, par * 1024:(par + 1) * 1024]
            if i < 2 and sigtok[par] is not None:
                bld.wait("tensor", "SC", sigtok[par])
            bld.opn("tensor", lambda e, fb=fb, i=i, J=J: e.matmul(
                out=fb[:, 0:512], lhsT=STAT[0:64, b0 + i * 128:b0 + (i + 1) * 128],
                rhs=MOV[0:64, b0 + J * 512:b0 + (J + 1) * 512],
                start=True, stop=True, tile_position=(0, 0), skip_group_check=True))
            ftok = bld.op("tensor", "TN", lambda e, fb=fb, i=i, J=J: e.matmul(
                out=fb[:, 512:1024], lhsT=STAT[64:128, b0 + i * 128:b0 + (i + 1) * 128],
                rhs=MOV[64:128, b0 + J * 512:b0 + (J + 1) * 512],
                start=True, stop=True, tile_position=(64, 0), skip_group_check=True))
            if pend is not None:
                pend()
            bld.wait("scalar", "TN", ftok)
            sigtok[par] = bld.op("scalar", "SC", lambda e, fb=fb, par4=par4: e.activation(
                out=QH[:, par4 * 1024:(par4 + 1) * 1024], in_=fb, func=AF.Sigmoid))

            def mk_pend(i=i, J=J, par4=par4, pv=pv, st=sigtok[par]):
                def go():
                    bld.wait("tensor", "SC", st)
                    bld.opn("tensor", lambda e: e.matmul(
                        out=pv[0:64, :], lhsT=STT[:, i * 128:i * 128 + 64],
                        rhs=QH[:, par4 * 1024:par4 * 1024 + 512],
                        start=(i == 0), stop=False, tile_position=(0, 0),
                        skip_group_check=True))
                    vlast[par4] = vtok_hist[(J, i)] = bld.op("tensor", "TN", lambda e: e.matmul(
                        out=pv[64:128, :], lhsT=STT[:, i * 128 + 64:(i + 1) * 128],
                        rhs=QH[:, par4 * 1024 + 512:(par4 + 1) * 1024],
                        start=(i == 0), stop=(i == 31), tile_position=(0, 64),
                        skip_group_check=True))
                return go
            pend = mk_pend()
        pend()
        bld.wait("vector", "TN", vtok_hist[(J, 31)])
        evactok[jp] = bld.op("vector", "VE", lambda e, pv=pv, J=J: e.tensor_copy(
            out=VOUT[:, J * 512:(J + 1) * 512], in_=pv))
    return evactok


def emit_l0attn(bld, T, XS, XN, dr, sub_es, ONESM, ONESR, IDENT, VEC):
    nc = bld.nc
    bld.wait("tensor", "VE", bld.cnt["VE"])
    with ExitStack() as es:
        PAa = es.enter_context(nc.psum_tensor("PAa", [128, 2048], F32))
        PBa = es.enter_context(nc.psum_tensor("PBa", [128, 2048], F32))
        UT = es.enter_context(nc.sbuf_tensor("UT2", [128, 8 * 1024], BF16))
        VT = es.enter_context(nc.sbuf_tensor("VT2", [128, 8 * 1024], BF16))
        QZU = es.enter_context(nc.sbuf_tensor("QZU", [128, NCH * NTOK], BF16))
        QZV = es.enter_context(nc.sbuf_tensor("QZV", [128, NCH * NTOK], BF16))
        pool = Pool(bld, [PAa[:, i * W:(i + 1) * W] for i in range(4)]
                    + [PBa[:, i * W:(i + 1) * W] for i in range(4)])
        ut_tok = bld.dma("gpsimd", "LG", UT[:].rearrange("p (a c) -> p a c", c=1024),
                         dr["uT"].rearrange("a p c -> p a c"))
        vt_tok = bld.dma("gpsimd", "LG", VT[:].rearrange("p (a c) -> p a c", c=1024),
                         dr["vT"].rearrange("a p c -> p a c"))
        bld.wait("tensor", "LG", vt_tok)
        emit_qzproj(bld, T, XN, UT, QZU, pool, SC_ATT)
        T["qzu_done"] = bld.cnt["VE"]
        emit_qzproj(bld, T, XN, VT, QZV, pool, SC_ATT)
        bld.wait("sync", "VE", T["qzu_done"])
        for mo in range(NCH):
            stok = bld.dma("sync", "LD", dr["ag1_in"][0][mo], chs(QZU, mo))
        bld.wait("gpsimd", "LD", stok)
        bld.op("gpsimd", "GP", lambda e: e.collective_compute(
            "AllGather", ALU.bypass, replica_groups=[list(range(8))],
            ins=[dr["ag1_in"][0]], outs=[dr["ag1_out"][0]]))
        bld.wait("sync", "VE", bld.cnt["VE"])
        for mo in range(NCH):
            stok = bld.dma("sync", "LD", dr["ag1_in"][1][mo], chs(QZV, mo))
        bld.wait("gpsimd", "LD", stok)
        ag1tok = bld.op("gpsimd", "GP", lambda e: e.collective_compute(
            "AllGather", ALU.bypass, replica_groups=[list(range(8))],
            ins=[dr["ag1_in"][1]], outs=[dr["ag1_out"][1]]))
        bld.barrier("sub2a")
    if ATTN_STOP == "A":
        return
    with ExitStack() as es:
        PF = es.enter_context(nc.psum_tensor("PF", [128, 2048], F32))
        PV = es.enter_context(nc.psum_tensor("PV", [128, 1024], F32))
        PT = es.enter_context(nc.psum_tensor("PT", [128, 1024], F32))
        QMU = es.enter_context(nc.sbuf_tensor("QMU", [128, 16384], BF16))
        QMV = es.enter_context(nc.sbuf_tensor("QMV", [128, 16384], BF16))
        QUT = es.enter_context(nc.sbuf_tensor("QUT", [128, 4096], BF16))
        QVT = es.enter_context(nc.sbuf_tensor("QVT", [128, 4096], BF16))
        QH = es.enter_context(nc.sbuf_tensor("QH", [128, 4096], BF16))
        VOUT = es.enter_context(nc.sbuf_tensor("VOUT", [128, 4096], BF16))
        bld.wait("sync", "GP", ag1tok)
        rtok = None
        def dynread(e, tgt, dst):
            if "cid" not in T:
                T["cid"] = e.partition_id()
            return e.dma_start(out=dst[:].rearrange("p (a c) -> p a c", c=NTOK),
                               in_=dr["ag1_out"][tgt][:, T["cid"], :, :].rearrange("a p c -> p a c"))
        bld.op("sync", "LD", lambda e: dynread(e, 0, QMU), inc=16)
        rtok = bld.op("sync", "LD", lambda e: dynread(e, 1, QMV), inc=16)
        for e_ in ("tensor", "vector", "scalar"):
            bld.wait(e_, "LD", rtok)
        if DBG == "qm":
            bld.wait("sync", "LD", rtok)
            for kk in range(8):
                bld.dma("sync", "ST", dr["dbg"][kk], QMU[:, kk * NTOK:(kk + 1) * NTOK])
                bld.dma("sync", "ST", dr["dbg"][8 + kk], QMV[:, kk * NTOK:(kk + 1) * NTOK])
        if ATTN_STOP == "B0":
            bld.barrier("sub2b0")
            return
        a2stage = None
        nbatch = 1 if (ATTN_STOP in ("B1", "B2", "B3")) else 4
        for b in range(nbatch):
            slotve = [None, None]
            ti = 0
            for src_t, dst_t in ((QMU, QUT), (QMV, QVT)):
                for tc in range(32):
                    sl = ti % 2
                    ti += 1
                    if slotve[sl] is not None:
                        bld.wait("tensor", "VE", slotve[sl])
                    ttok = bld.op("tensor", "TN", lambda e, sl=sl, tc=tc, s=src_t, b=b: e.matmul(
                        out=PT[:, sl * 512:sl * 512 + 128],
                        lhsT=s[:, b * SEQ + tc * 128:b * SEQ + (tc + 1) * 128],
                        rhs=IDENT[:], start=True, stop=True, skip_group_check=True))
                    bld.wait("vector", "TN", ttok)
                    if DBG == "one" and b == 0 and ti == 1:
                        e1 = bld.op("vector", "VE", lambda e: e.tensor_copy(
                            out=QH[:, 0:128], in_=PT[:, 0:128]))
                        e2 = bld.op("vector", "VE", lambda e: e.tensor_copy(
                            out=QH[:, 1024:1024 + 128], in_=QMU[:, 0:128]))
                        e2 = bld.op("vector", "VE", lambda e: e.tensor_copy(
                            out=QH[:, 512:512 + 128], in_=IDENT[:]))
                        bld.wait("sync", "VE", e2)
                        bld.dma("sync", "ST", dr["dbg"][0], QH[:, 0:NTOK])
                        sttk = bld.dma("sync", "ST", dr["dbg"][1], QUT[:, 0:NTOK])
                        for e_ in ("tensor", "vector"):
                            bld.wait(e_, "ST", sttk)
                    if DBG == "copybypass":
                        slotve[sl] = bld.op("vector", "VE", lambda e, sl=sl, tc=tc, d=dst_t, s=src_t, b=b: e.tensor_copy(
                            out=d[:, tc * 128:(tc + 1) * 128], in_=s[:, b * SEQ + tc * 128:b * SEQ + (tc + 1) * 128]))
                    else:
                        slotve[sl] = bld.op("vector", "VE", lambda e, sl=sl, tc=tc, d=dst_t: e.tensor_copy(
                            out=d[:, tc * 128:(tc + 1) * 128], in_=PT[:, sl * 512:sl * 512 + 128]))
            bld.wait("tensor", "VE", bld.cnt["VE"])
            if DBG == "memtest" and b == 0:
                mtok = bld.op("vector", "VE", lambda e: e.memset(QUT[:], 3.25))
                bld.wait("sync", "VE", mtok)
                bld.dma("sync", "ST", dr["dbg"][0], QUT[:, 0:NTOK])
                for e_ in ("tensor", "vector"):
                    bld.wait(e_, "ST", bld.cnt["ST"])
            if DBG == "slicetest" and b == 0:
                bld.barrier("dbgbar")
                m1 = bld.op("vector", "VE", lambda e: e.memset(QUT[:, 0:128], 7.0))
                m2 = bld.op("vector", "VE", lambda e: e.tensor_copy(
                    out=QUT[:, 128:256], in_=QMU[:, 0:128]))
                m3 = bld.op("vector", "VE", lambda e: e.tensor_copy(
                    out=QUT[:, 256:384], in_=PT[:, 0:128]))
                bld.wait("sync", "VE", m3)
                sttk = bld.dma("sync", "ST", dr["dbg"][0], QUT[:, 0:NTOK])
                for e_ in ("tensor", "vector"):
                    bld.wait(e_, "ST", sttk)
            if DBG in ("qut", "copybypass") and b == 0:
                bld.wait("sync", "VE", bld.cnt["VE"])
                for kk in range(2):
                    bld.dma("sync", "ST", dr["dbg"][kk], QUT[:, kk * NTOK:(kk + 1) * NTOK])
                    bld.dma("sync", "ST", dr["dbg"][2 + kk], QVT[:, kk * NTOK:(kk + 1) * NTOK])
                for e_ in ("tensor", "vector"):
                    bld.wait(e_, "ST", bld.cnt["ST"])
            if ATTN_STOP == "B1":
                bld.barrier("sub2b1")
                return
            emit_score_pass(bld, T, b, QMU, QMV, QUT, PF, PV, QH, VOUT, f"p1b{b}")
            bld.wait("sync", "VE", bld.cnt["VE"])
            if DBG == "vout" and b == 0:
                for kk in range(2):
                    bld.dma("sync", "ST", dr["dbg"][4 + kk], VOUT[:, kk * NTOK:(kk + 1) * NTOK])
                for e_ in ("tensor", "vector"):
                    bld.wait(e_, "ST", bld.cnt["ST"])
            a2stage = bld.dma("sync", "LD",
                              dr["ag2_in"][1][:, b * SEQ:(b + 1) * SEQ], VOUT[:])
            bld.wait("vector", "LD", a2stage)
            if b == nbatch - 1:
                bld.wait("gpsimd", "LD", a2stage)
                bld.op("gpsimd", "GP", lambda e: e.collective_compute(
                    "AllGather", ALU.bypass, replica_groups=[list(range(8))],
                    ins=[dr["ag2_in"][1]], outs=[dr["ag2_out"][1]]))
            if ATTN_STOP == "B2":
                bld.barrier("sub2b2")
                return
            emit_score_pass(bld, T, b, QMV, QMU, QVT, PF, PV, QH, VOUT, f"p2b{b}")
            bld.wait("sync", "VE", bld.cnt["VE"])
            a2stage = bld.dma("sync", "LD", dr["ag2_in"][0][:, b * SEQ:(b + 1) * SEQ], VOUT[:])
            bld.wait("vector", "LD", a2stage)
        if ATTN_STOP in ("B", "B3"):
            bld.barrier("sub2bx")
            return
        bld.wait("gpsimd", "LD", a2stage)
        ag2tok = bld.op("gpsimd", "GP", lambda e: e.collective_compute(
            "AllGather", ALU.bypass, replica_groups=[list(range(8))],
            ins=[dr["ag2_in"][0]], outs=[dr["ag2_out"][0]]))
        bld.barrier("sub2b")
    with ExitStack() as es:
        PAc = es.enter_context(nc.psum_tensor("PAc", [128, 2048], F32))
        PBc = es.enter_context(nc.psum_tensor("PBc", [128, 2048], F32))
        V1F = es.enter_context(nc.sbuf_tensor("V1F", [128, NCH * NTOK], BF16))
        V2F = es.enter_context(nc.sbuf_tensor("V2F", [128, NCH * NTOK], BF16))
        UB = es.enter_context(nc.sbuf_tensor("UB", [128, 8 * 1024], BF16))
        VB = es.enter_context(nc.sbuf_tensor("VB", [128, 8 * 1024], BF16))
        pool = Pool(bld, [PAc[:, i * W:(i + 1) * W] for i in range(4)]
                    + [PBc[:, i * W:(i + 1) * W] for i in range(4)])
        bld.wait("sync", "GP", ag2tok)

        def dynread2(e, tgt, dst):
            return e.dma_start(out=dst[:].rearrange("p (a c) -> p a c", c=NTOK),
                               in_=dr["ag2_out"][tgt][:, :, bass.ts(T["cid"], NTOK)].rearrange("a p c -> p a c"))
        bld.op("sync", "LD", lambda e: dynread2(e, 0, V1F), inc=16)
        rtok = bld.op("sync", "LD", lambda e: dynread2(e, 1, V2F), inc=16)
        if DBG == "v1f":
            bld.wait("sync", "LD", rtok)
            for kk in range(8):
                bld.dma("sync", "ST", dr["dbg"][kk], V1F[:, kk * NTOK:(kk + 1) * NTOK])
                bld.dma("sync", "ST", dr["dbg"][8 + kk], V2F[:, kk * NTOK:(kk + 1) * NTOK])
        ub_tok = bld.dma("gpsimd", "LG", UB[:].rearrange("p (a c) -> p a c", c=1024),
                         dr["u_t"].rearrange("a p c -> p a c"))
        vb_tok = bld.dma("gpsimd", "LG", VB[:].rearrange("p (a c) -> p a c", c=1024),
                         dr["v_t"].rearrange("a p c -> p a c"))
        bld.wait("tensor", "LD", rtok)
        bld.wait("tensor", "LG", vb_tok)
        for mo in range(NCH):
            for w in range(NWIN):
                ja, pa = pool.acq()
                for k in range(NCH):
                    bld.opn("tensor", lambda e, p=pa, k=k, mo=mo, w=w: e.matmul(
                        out=p, lhsT=UB[:, k * 1024 + mo * 128:k * 1024 + (mo + 1) * 128],
                        rhs=chs(V1F, k, w), start=(k == 0), stop=False))
                for k in range(NCH):
                    fn = lambda e, p=pa, k=k, mo=mo, w=w: e.matmul(
                        out=p, lhsT=VB[:, k * 1024 + mo * 128:k * 1024 + (mo + 1) * 128],
                        rhs=chs(V2F, k, w), start=False, stop=(k == NCH - 1))
                    if k == NCH - 1:
                        mtok = bld.op("tensor", "TN", fn)
                    else:
                        bld.opn("tensor", fn)
                bld.wait("vector", "TN", mtok)
                vtok = bld.op("vector", "VE", lambda e, p=pa, mo=mo, w=w: e.scalar_tensor_tensor(
                    out=chs(XS, mo, w), in0=p, scalar=float(SC_ATT),
                    in1=chs(XS, mo, w), op0=ALU.mult, op1=ALU.add))
                pool.rel(ja, "VE", vtok)
        bld.barrier("sub2c")


def emit_l1ffn(bld, T, XS, XN, dr, ONESM, ONESR, VEC, ROWS, agx_in, agx_out, tag):
    nc = bld.nc
    mu_r0 = _vc("mu_r0")
    mu_k0 = _vc("mu_k0")
    sp0 = _vc("selprev0")
    with ExitStack() as es:
        PA = es.enter_context(nc.psum_tensor("PA" + tag, [128, 2048], F32))
        PB = es.enter_context(nc.psum_tensor("PB" + tag, [128, 2048], F32))
        XK = es.enter_context(nc.sbuf_tensor("XK" + tag, [128, NCH * NTOK], BF16))
        with ExitStack() as es2:
            LXB = es2.enter_context(nc.sbuf_tensor("LXB" + tag, [128, 2 * NTOK], BF16))
            LSQ = es2.enter_context(nc.sbuf_tensor("LSQ" + tag, [128, 2 * NTOK], BF16))
            LE1 = es2.enter_context(nc.sbuf_tensor("LE1" + tag, [128, NTOK], F32))
            XNC = es2.enter_context(nc.sbuf_tensor("XNC" + tag, [128, NTOK], F32))
            SML = es2.enter_context(nc.sbuf_tensor("SML" + tag, [128, 128], F32))
            emit_ln(bld, T, XS, PA, PB, None, ONESM, ONESR, VEC, LXB, LSQ, LE1,
                    ROWS, "ln" + tag, apply=False)
            bld.op("vector", "VE", lambda e: e.tensor_scalar(
                out=SML[:, 16:24], in0=VEC[:, mu_r0:mu_r0 + 8], scalar1=1.0, scalar2=None,
                op0=ALU.subtract))
            bld.op("vector", "VE", lambda e: e.tensor_scalar(
                out=SML[:, 24:32], in0=VEC[:, mu_k0:mu_k0 + 8], scalar1=1.0, scalar2=None,
                op0=ALU.subtract))
            atoks = []
            for k in range(NCH):
                bld.op("vector", "VE", lambda e, k=k: e.tensor_tensor(
                    out=LE1[:], in0=chs(XS, k), in1=PA[:, :], op=ALU.subtract))
                bld.op("vector", "VE", lambda e, k=k: e.tensor_tensor(
                    out=XNC[:], in0=LE1[:], in1=PB[:, :], op=ALU.mult))
                bld.op("vector", "VE", lambda e, k=k: e.tensor_copy(
                    out=SML[:, k:k + 1], in_=XNC[:, 0:1]))
                bld.op("vector", "VE", lambda e, k=k: e.tensor_copy(
                    out=SML[:, 8 + k:9 + k], in_=XNC[:, NTOK - 1:NTOK]))
                bld.op("vector", "VE", lambda e: e.tensor_tensor(
                    out=LE1[:, 1:NTOK], in0=XNC[:, 1:NTOK], in1=XNC[:, 0:NTOK - 1],
                    op=ALU.subtract))
                bld.op("vector", "VE", lambda e: e.memset(LE1[:, 0:1], 0.0))
                bld.op("vector", "VE", lambda e, k=k: e.scalar_tensor_tensor(
                    out=chs(XN, k), in0=LE1[:], scalar=SML[:, 16 + k:17 + k], in1=XNC[:],
                    op0=ALU.mult, op1=ALU.add))
                atoks.append(bld.op("vector", "VE", lambda e, k=k: e.scalar_tensor_tensor(
                    out=chs(XK, k), in0=LE1[:], scalar=SML[:, 24 + k:25 + k], in1=XNC[:],
                    op0=ALU.mult, op1=ALU.add)))
            bld.wait("sync", "VE", atoks[-1])
            stok = bld.dma("sync", "LD", agx_in, SML[:, 8:16])
            bld.wait("gpsimd", "LD", stok)
            agtok = bld.op("gpsimd", "GP", lambda e: e.collective_compute(
                "AllGather", ALU.bypass, replica_groups=[list(range(8))],
                ins=[agx_in], outs=[agx_out]))
            bld.wait("sync", "GP", agtok)
            rtok = bld.dma("sync", "LD", SML[:, 32:96].rearrange("p (s c) -> p s c", c=8),
                           agx_out.rearrange("s p c -> p s c"))
            bld.wait("vector", "LD", rtok)
            htok = bld.op("vector", "VE", lambda e: e.tensor_scalar(
                out=SML[:, 96:104], in0=SML[:, 32:40],
                scalar1=VEC[:, sp0:sp0 + 1], scalar2=None, op0=ALU.mult))
            for s in range(1, 8):
                htok = bld.op("vector", "VE", lambda e, s=s: e.scalar_tensor_tensor(
                    out=SML[:, 96:104], in0=SML[:, 32 + 8 * s:40 + 8 * s],
                    scalar=VEC[:, sp0 + s:sp0 + s + 1], in1=SML[:, 96:104],
                    op0=ALU.mult, op1=ALU.add))
            bld.op("vector", "VE", lambda e: e.tensor_tensor(
                out=SML[:, 104:112], in0=SML[:, 0:8], in1=SML[:, 96:104], op=ALU.subtract))
            bld.op("vector", "VE", lambda e: e.tensor_tensor(
                out=SML[:, 112:120], in0=SML[:, 104:112], in1=SML[:, 16:24], op=ALU.mult))
            for k in range(NCH):
                bld.op("vector", "VE", lambda e, k=k: e.tensor_tensor(
                    out=XN[:, k * NTOK:k * NTOK + 1], in0=SML[:, 112 + k:113 + k],
                    in1=SML[:, k:k + 1], op=ALU.add))
            bld.op("vector", "VE", lambda e: e.tensor_tensor(
                out=SML[:, 112:120], in0=SML[:, 104:112], in1=SML[:, 24:32], op=ALU.mult))
            for k in range(NCH):
                ptok = bld.op("vector", "VE", lambda e, k=k: e.tensor_tensor(
                    out=XK[:, k * NTOK:k * NTOK + 1], in0=SML[:, 112 + k:113 + k],
                    in1=SML[:, k:k + 1], op=ALU.add))
            bld.barrier(tag + "mix")
        with ExitStack() as es3:
            RB = es3.enter_context(nc.sbuf_tensor("RB" + tag, [128, NCH * NTOK], BF16))
            KA = XN
            WT = es3.enter_context(nc.sbuf_tensor("WT" + tag, [128, 8 * 1024], BF16))
            TMP = es3.enter_context(nc.sbuf_tensor("TMPX" + tag, [128, W], F32))
            TMP2 = es3.enter_context(nc.sbuf_tensor("TMPY" + tag, [128, W], F32))
            pool = Pool(bld, [PA[:, i * W:(i + 1) * W] for i in range(4)]
                        + [PB[:, i * W:(i + 1) * W] for i in range(4)])
            prev_sc = [None]
            prev_ve = [None]

            def proj(wdram, mov, evac):
                wtok = bld.dma("gpsimd", "LG", WT[:].rearrange("p (a c) -> p a c", c=1024),
                               wdram.rearrange("a p c -> p a c"))
                bld.wait("tensor", "LG", wtok)
                for mo in range(NCH):
                    for w in range(NWIN):
                        ja, pa = pool.acq()
                        for k in range(NCH):
                            fn = lambda e, p=pa, k=k, mo=mo, w=w: e.matmul(
                                out=p, lhsT=WT[:, k * 1024 + mo * 128:k * 1024 + (mo + 1) * 128],
                                rhs=chs(mov, k, w), start=(k == 0), stop=(k == NCH - 1))
                            if k == NCH - 1:
                                mtok = bld.op("tensor", "TN", fn)
                            else:
                                bld.opn("tensor", fn)
                        evac(mo, w, ja, pa, mtok)
                bld.wait("gpsimd", "TN", bld.cnt["TN"])

            def evac_r(mo, w, ja, pa, mtok):
                bld.wait("scalar", "TN", mtok)
                stok = bld.op("scalar", "SC", lambda e, p=pa, mo=mo, w=w: e.activation(
                    out=chs(RB, mo, w), in_=p, func=AF.Sigmoid))
                pool.rel(ja, "SC", stok)
            proj(dr["WrT"], XN, evac_r)

            def evac_k(mo, w, ja, pa, mtok):
                bld.wait("scalar", "TN", mtok)
                if prev_ve[0] is not None:
                    bld.wait("scalar", "VE", prev_ve[0])
                stok = bld.op("scalar", "SC", lambda e, p=pa: e.activation(
                    out=TMP[:], in_=p, func=AF.Relu))
                bld.wait("vector", "SC", stok)
                vtok = bld.op("vector", "VE", lambda e, mo=mo, w=w: e.tensor_tensor(
                    out=chs(KA, mo, w), in0=TMP[:], in1=TMP[:], op=ALU.mult))
                prev_ve[0] = vtok
                pool.rel(ja, "VE", vtok)
            proj(dr["WkT"], XK, evac_k)

            def evac_v(mo, w, ja, pa, mtok):
                bld.wait("vector", "TN", mtok)
                bld.op("vector", "VE", lambda e, p=pa, mo=mo, w=w: e.tensor_tensor(
                    out=TMP2[:], in0=chs(RB, mo, w), in1=p, op=ALU.mult))
                vtok = bld.op("vector", "VE", lambda e, mo=mo, w=w: e.scalar_tensor_tensor(
                    out=chs(XS, mo, w), in0=TMP2[:], scalar=0.5, in1=chs(XS, mo, w),
                    op0=ALU.mult, op1=ALU.add))
                pool.rel(ja, "VE", vtok)
            proj(dr["WvT"], KA, evac_v)
    bld.barrier(tag)


def emit_l1attn(bld, T, XS, XN, dr, ONESM, ONESR, VEC, ROWS, tag="s5"):
    nc = bld.nc
    W5 = 256
    NW5 = NTOK // W5
    sp0 = _vc("selprev0")
    sn0 = _vc("selnext0")

    def vc(name, k):
        i = _vc(f"{name}{k}")
        return VEC[:, i:i + 1]

    with ExitStack() as es:
        MBC = es.enter_context(nc.sbuf_tensor("MBC", [128, NTOK], BF16))
        RSBC = es.enter_context(nc.sbuf_tensor("RSBC", [128, NTOK], BF16))
        SM5 = es.enter_context(nc.sbuf_tensor("SM5", [128, 256], F32))

        with ExitStack() as es2:
            PA5 = es2.enter_context(nc.psum_tensor("PA5", [128, 2048], F32))
            ROWS5 = es2.enter_context(nc.sbuf_tensor("ROWS5", [128, 2 * NTOK], F32))
            PB5 = es2.enter_context(nc.psum_tensor("PB5", [128, 2048], F32))
            LXB = es2.enter_context(nc.sbuf_tensor("LXB5", [128, 2 * NTOK], BF16))
            LSQ = es2.enter_context(nc.sbuf_tensor("LSQ5", [128, 2 * NTOK], BF16))
            LE1 = es2.enter_context(nc.sbuf_tensor("LE15", [128, NTOK], F32))
            emit_ln(bld, T, XS, PA5, PB5, XN, ONESM, ONESR, VEC,
                    LXB, LSQ, LE1, ROWS5, "ln5", gamma_beta=("gamma", "beta"),
                    combined=True)
            mt = bld.op("vector", "VE", lambda e: e.tensor_copy(out=MBC[:], in_=PA5[:, :]))
            rt = bld.op("vector", "VE", lambda e: e.tensor_copy(
                out=LXB[0:1, 0:NTOK], in_=ROWS5[64:65, 0:NTOK]))
            bld.wait("tensor", "VE", rt)
            bts = []
            for w in range(4):
                bts.append(bld.op("tensor", "TN", lambda e, w=w: e.matmul(
                    out=PB5[:, w * 512:(w + 1) * 512], lhsT=ONESR[:],
                    rhs=LXB[0:1, w * 512:(w + 1) * 512],
                    start=True, stop=True, skip_group_check=True)))
            bld.wait("vector", "TN", bts[-1])
            bld.op("vector", "VE", lambda e: e.tensor_copy(out=RSBC[:], in_=PB5[:, :]))
            bld.barrier(tag + "ln")
        PC = es.enter_context(nc.psum_tensor("PC5", [128, 2048], F32))
        PD = es.enter_context(nc.psum_tensor("PD5", [128, 1024], F32))
        pool = Pool(bld, [PC[:, 0:512], PC[:, 512:1024]])
        s2slot = [PC[:, 1024:1024 + W5], PC[:, 1536:1536 + W5]]
        XB2 = es.enter_context(nc.sbuf_tensor("XB2", [128, 2 * NTOK], F32))
        ZS2 = es.enter_context(nc.sbuf_tensor("ZS2", [128, 2 * NTOK], BF16))
        XACT = es.enter_context(nc.sbuf_tensor("XACT", [128, NTOK], F32))
        DELTA = es.enter_context(nc.sbuf_tensor("DELTA", [128, NTOK], F32))
        DBX = es.enter_context(nc.sbuf_tensor("DBX", [128, NTOK], F32))
        CMS = es.enter_context(nc.sbuf_tensor("CMS", [128, NTOK], F32))
        HSS = es.enter_context(nc.sbuf_tensor("HSS", [128, NTOK], F32))
        GATED = es.enter_context(nc.sbuf_tensor("GATED", [128, NTOK], BF16))
        GW = es.enter_context(nc.sbuf_tensor("GW", [128, NTOK], BF16))
        WSM = es.enter_context(nc.sbuf_tensor("WSM", [128, 8 * 96], F32))
        WDT = es.enter_context(nc.sbuf_tensor("WDT", [64, 1024], BF16))
        WBP = es.enter_context(nc.sbuf_tensor("WBP", [16, 1024], BF16))
        WCP = es.enter_context(nc.sbuf_tensor("WCP", [16, 1024], BF16))
        MB = es.enter_context(nc.sbuf_tensor("MB", [64, W5], BF16))
        WB = [es.enter_context(nc.sbuf_tensor(f"WB5_{i}", [128, 1024], BF16)) for i in range(4)]
        HLB = es.enter_context(nc.sbuf_tensor("HLB", [128, 16], BF16))
        TMP5 = es.enter_context(nc.sbuf_tensor("TMP5", [128, W5], F32))
        TMP6 = es.enter_context(nc.sbuf_tensor("TMP6", [128, W5], F32))

        for k in range(NCH):
            bld.op("vector", "VE", lambda e, k=k: e.tensor_copy(
                out=SM5[:, k:k + 1], in_=XN[:, k * NTOK:k * NTOK + 1]))
            bld.op("vector", "VE", lambda e, k=k: e.tensor_copy(
                out=SM5[:, 8 + k:9 + k], in_=XN[:, (k + 1) * NTOK - 1:(k + 1) * NTOK]))
        bld.wait("sync", "VE", bld.cnt["VE"])
        stok = bld.dma("sync", "LD", dr["agx5i"], SM5[:, 0:16])
        bld.wait("gpsimd", "LD", stok)
        agtok = bld.op("gpsimd", "GP", lambda e: e.collective_compute(
            "AllGather", ALU.bypass, replica_groups=[list(range(8))],
            ins=[dr["agx5i"]], outs=[dr["agx5o"]]))
        bld.wait("sync", "GP", agtok)
        rtok = bld.dma("sync", "LD", SM5[:, 16:144].rearrange("p (s c) -> p s c", c=16),
                       dr["agx5o"].rearrange("s p c -> p s c"))
        bld.wait("vector", "LD", rtok)
        bld.op("vector", "VE", lambda e: e.tensor_scalar(
            out=SM5[:, 144:152], in0=SM5[:, 24:32], scalar1=VEC[:, sp0:sp0 + 1],
            scalar2=None, op0=ALU.mult))
        bld.op("vector", "VE", lambda e: e.tensor_scalar(
            out=SM5[:, 152:160], in0=SM5[:, 16:24], scalar1=VEC[:, sn0:sn0 + 1],
            scalar2=None, op0=ALU.mult))
        for s in range(1, 8):
            bld.op("vector", "VE", lambda e, s=s: e.scalar_tensor_tensor(
                out=SM5[:, 144:152], in0=SM5[:, 16 + 16 * s + 8:16 + 16 * s + 16],
                scalar=VEC[:, sp0 + s:sp0 + s + 1], in1=SM5[:, 144:152],
                op0=ALU.mult, op1=ALU.add))
            bld.op("vector", "VE", lambda e, s=s: e.scalar_tensor_tensor(
                out=SM5[:, 152:160], in0=SM5[:, 16 + 16 * s:16 + 16 * s + 8],
                scalar=VEC[:, sn0 + s:sn0 + s + 1], in1=SM5[:, 152:160],
                op0=ALU.mult, op1=ALU.add))
        bld.op("vector", "VE", lambda e: e.tensor_copy(out=HLB[:, 0:8], in_=SM5[:, 144:152]))
        hlb_tok = bld.op("vector", "VE", lambda e: e.tensor_copy(out=HLB[:, 8:16], in_=SM5[:, 152:160]))
        wsmt = bld.dma("sync", "LD", WSM[:].rearrange("p (a c) -> p a c", c=96),
                       dr["Wsm"].rearrange("a p c -> p a c"))
        bld.dma("gpsimd", "LG", WDT[:], dr["W_dt"])
        bld.dma("gpsimd", "LG", WBP[:], dr["W_Bp"])
        wsmt2 = bld.dma("gpsimd", "LG", WCP[:], dr["W_Cp"])
        bld.wait("tensor", "VE", hlb_tok)
        wo_used = [None, None, None, None]
        for mo in range(NCH):
            par = mo % 4
            if wo_used[par] is not None:
                bld.wait("gpsimd", "TN", wo_used[par])
            wtok = bld.dma("gpsimd", "LG", WB[par][:].rearrange("p (a c) -> p a c", c=128),
                           dr["W_in"][:, :, mo * 128:(mo + 1) * 128].rearrange("a p c -> p a c"))
            bld.wait("tensor", "LG", wtok)
            for k in range(NCH):
                fn = lambda e, par=par, k=k, mo=mo: e.matmul(
                    out=PD[:, 512 + 2 * mo:514 + 2 * mo],
                    lhsT=WB[par][:, k * 128:(k + 1) * 128],
                    rhs=HLB[:, k:16:8], start=(k == 0), stop=(k == NCH - 1),
                    skip_group_check=True)
                if k == NCH - 1:
                    mtok = bld.op("tensor", "TN", fn)
                else:
                    bld.opn("tensor", fn)
            wo_used[par] = mtok
        bld.wait("vector", "TN", bld.cnt["TN"])
        bld.op("vector", "VE", lambda e: e.tensor_copy(
            out=SM5[:, 184:200], in_=PD[:, 512:528]))
        bld.barrier(tag + "pre")

        def prefetch(dram, mo, wpar):
            if wo_used[wpar] is not None:
                bld.wait("gpsimd", "TN", wo_used[wpar])
            return bld.dma("gpsimd", "LG", WB[wpar][:].rearrange("p (a c) -> p a c", c=128),
                           dram[:, :, mo * 128:(mo + 1) * 128].rearrange("a p c -> p a c"))

        def xbzs(wi):
            par = wi % 2
            wtoks = {}
            for mo in range(3):
                wtoks[mo] = prefetch(dr["W_in"], mo, mo % 4)
            for mo in range(16):
                pmo = mo % 8
                wpar = mo % 4
                bld.wait("tensor", "LG", wtoks[mo])
                if mo + 3 < 16:
                    wtoks[mo + 3] = prefetch(dr["W_in"], mo + 3, (mo + 3) % 4)
                ja, pa = pool.acq()
                for k in range(NCH):
                    fn = lambda e, p=pa, wpar=wpar, k=k, wi=wi: e.matmul(
                        out=p[:, 0:W5], lhsT=WB[wpar][:, k * 128:(k + 1) * 128],
                        rhs=XN[:, k * NTOK + wi * W5:k * NTOK + (wi + 1) * W5],
                        start=(k == 0), stop=(k == NCH - 1), skip_group_check=True)
                    if k == NCH - 1:
                        mtok = bld.op("tensor", "TN", fn)
                    else:
                        bld.opn("tensor", fn)
                wo_used[wpar] = mtok
                if mo < 8:
                    bld.wait("vector", "TN", mtok)
                    vtok = bld.op("vector", "VE", lambda e, p=pa, par=par, pmo=pmo: e.tensor_copy(
                        out=XB2[:, par * NTOK + pmo * W5:par * NTOK + (pmo + 1) * W5],
                        in_=p[:, 0:W5]))
                    pool.rel(ja, "VE", vtok)
                else:
                    bld.wait("scalar", "TN", mtok)
                    if pmo == 0:
                        bld.wait("scalar", "VE", bld.cnt["VE"])
                    stok = bld.op("scalar", "SC", lambda e, p=pa, par=par, pmo=pmo: e.activation(
                        out=ZS2[:, par * NTOK + pmo * W5:par * NTOK + (pmo + 1) * W5],
                        in_=p[:, 0:W5], func=AF.Silu))
                    pool.rel(ja, "SC", stok)

        def stage2(stat_t, stat_cols, consumer):
            free = [None, None]
            for mo in range(NCH):
                par = mo % 2
                if free[par] is not None:
                    bld.wait("tensor", free[par][0], free[par][1])
                mt = bld.op("tensor", "TN", lambda e, mo=mo, par=par: e.matmul(
                    out=s2slot[par], lhsT=stat_t[:, mo * 128:(mo + 1) * 128],
                    rhs=MB[0:stat_cols, :], start=True, stop=True, skip_group_check=True))
                free[par] = consumer(mo, s2slot[par], mt)
            bld.wait("tensor", free[0][0], free[0][1])
            bld.wait("tensor", free[1][0], free[1][1])

        xbzs(0)
        for wi in range(NW5):
            par = wi % 2
            if wi < NW5 - 1:
                xbzs(wi + 1)
            bld.wait("vector", "SC", bld.cnt["SC"])
            bld.wait("vector", "TN", bld.cnt["TN"])
            xb_c = lambda k, par=par: XB2[:, par * NTOK + k * W5:par * NTOK + (k + 1) * W5]
            xb_n = lambda k, par=par: XB2[:, (1 - par) * NTOK + k * W5:(1 - par) * NTOK + (k + 1) * W5]
            for k in range(NCH):
                bld.op("vector", "VE", lambda e, k=k, par=par: e.tensor_scalar(
                    out=XACT[:, k * W5:(k + 1) * W5], in0=xb_c(k, par),
                    scalar1=vc("cw1", k), scalar2=vc("conv_b", k), op0=ALU.mult, op1=ALU.add))
                bld.op("vector", "VE", lambda e, k=k, par=par: e.scalar_tensor_tensor(
                    out=XACT[:, k * W5 + 1:(k + 1) * W5], in0=xb_c(k, par)[:, 0:W5 - 1],
                    scalar=vc("cw0", k), in1=XACT[:, k * W5 + 1:(k + 1) * W5],
                    op0=ALU.mult, op1=ALU.add))
                if wi == 0:
                    bld.op("vector", "VE", lambda e, k=k: e.scalar_tensor_tensor(
                        out=XACT[:, k * W5:k * W5 + 1], in0=SM5[:, 184 + 2 * k:185 + 2 * k],
                        scalar=vc("cw0", k), in1=XACT[:, k * W5:k * W5 + 1],
                        op0=ALU.mult, op1=ALU.add))
                else:
                    bld.op("vector", "VE", lambda e, k=k: e.scalar_tensor_tensor(
                        out=XACT[:, k * W5:k * W5 + 1],
                        in0=SM5[:, 200 + k:201 + k],
                        scalar=vc("cw0", k), in1=XACT[:, k * W5:k * W5 + 1],
                        op0=ALU.mult, op1=ALU.add))
                bld.op("vector", "VE", lambda e, k=k, par=par: e.tensor_copy(
                    out=SM5[:, 200 + k:201 + k], in_=xb_c(k, par)[:, W5 - 1:W5]))
                bld.op("vector", "VE", lambda e, k=k, par=par: e.scalar_tensor_tensor(
                    out=XACT[:, k * W5:(k + 1) * W5 - 1], in0=xb_c(k, par)[:, 1:W5],
                    scalar=vc("cw2", k), in1=XACT[:, k * W5:(k + 1) * W5 - 1],
                    op0=ALU.mult, op1=ALU.add))
                if wi == NW5 - 1:
                    ctok = bld.op("vector", "VE", lambda e, k=k: e.scalar_tensor_tensor(
                        out=XACT[:, (k + 1) * W5 - 1:(k + 1) * W5], in0=SM5[:, 185 + 2 * k:186 + 2 * k],
                        scalar=vc("cw2", k), in1=XACT[:, (k + 1) * W5 - 1:(k + 1) * W5],
                        op0=ALU.mult, op1=ALU.add))
                else:
                    ctok = bld.op("vector", "VE", lambda e, k=k, par=par: e.scalar_tensor_tensor(
                        out=XACT[:, (k + 1) * W5 - 1:(k + 1) * W5], in0=xb_n(k, par)[:, 0:1],
                        scalar=vc("cw2", k), in1=XACT[:, (k + 1) * W5 - 1:(k + 1) * W5],
                        op0=ALU.mult, op1=ALU.add))
            bld.wait("scalar", "VE", ctok)
            sil = bld.op("scalar", "SC", lambda e: e.activation(
                out=XACT[:], in_=XACT[:], func=AF.Silu))
            T["s5_sil"] = sil
            bld.wait("tensor", "SC", sil)
            bld.wait("tensor", "LD", bld.cnt["LD"])
            bld.wait("tensor", "LG", bld.cnt["LG"])
            for k in range(NCH):
                fn = lambda e, k=k: e.matmul(
                    out=PD[0:64, 0:W5], lhsT=WSM[:, k * 96:k * 96 + 64],
                    rhs=XACT[:, k * W5:(k + 1) * W5],
                    start=(k == 0), stop=(k == NCH - 1), skip_group_check=True)
                if k == NCH - 1:
                    mtok = bld.op("tensor", "TN", fn)
                else:
                    bld.opn("tensor", fn)
            bld.wait("vector", "TN", mtok)
            mbt = bld.op("vector", "VE", lambda e: e.tensor_copy(out=MB[:], in_=PD[0:64, 0:W5]))
            bld.wait("tensor", "VE", mbt)

            def cons_delta(mo, slot, mt):
                bld.wait("scalar", "TN", mt)
                bld.op("scalar", "SC", lambda e, slot=slot, mo=mo: e.activation(
                    out=TMP5[:], in_=slot, func=AF.Exp, bias=vc("b_dt", mo)))
                st = bld.op("scalar", "SC", lambda e, mo=mo: e.activation(
                    out=DELTA[:, mo * W5:(mo + 1) * W5], in_=TMP5[:],
                    func=AF.Ln, bias=VEC[:, _vc("one"):_vc("one") + 1]))
                return ("SC", st)
            stage2(WDT, 64, cons_delta)
            for k in range(NCH):
                fn = lambda e, k=k: e.matmul(
                    out=PD[0:16, 0:W5], lhsT=WSM[:, k * 96 + 64:k * 96 + 80],
                    rhs=XACT[:, k * W5:(k + 1) * W5],
                    start=(k == 0), stop=(k == NCH - 1), skip_group_check=True)
                if k == NCH - 1:
                    mtok = bld.op("tensor", "TN", fn)
                else:
                    bld.opn("tensor", fn)
            bld.wait("vector", "TN", mtok)
            mbt = bld.op("vector", "VE", lambda e: e.tensor_copy(out=MB[0:16, :], in_=PD[0:16, 0:W5]))
            bld.wait("tensor", "VE", mbt)

            def cons_b(mo, slot, mt):
                bld.wait("vector", "TN", mt)
                bld.op("vector", "VE", lambda e, slot=slot, mo=mo: e.tensor_tensor(
                    out=TMP5[:], in0=DELTA[:, mo * W5:(mo + 1) * W5], in1=slot, op=ALU.mult))
                vt = bld.op("vector", "VE", lambda e, mo=mo: e.tensor_tensor(
                    out=DBX[:, mo * W5:(mo + 1) * W5], in0=TMP5[:],
                    in1=XACT[:, mo * W5:(mo + 1) * W5], op=ALU.mult))
                return ("VE", vt)
            stage2(WBP, 16, cons_b)
            for k in range(NCH):
                fn = lambda e, k=k: e.matmul(
                    out=PD[0:16, 0:W5], lhsT=WSM[:, k * 96 + 80:k * 96 + 96],
                    rhs=XACT[:, k * W5:(k + 1) * W5],
                    start=(k == 0), stop=(k == NCH - 1), skip_group_check=True)
                if k == NCH - 1:
                    mtok = bld.op("tensor", "TN", fn)
                else:
                    bld.opn("tensor", fn)
            bld.wait("vector", "TN", mtok)
            mbt = bld.op("vector", "VE", lambda e: e.tensor_copy(out=MB[0:16, :], in_=PD[0:16, 0:W5]))
            bld.wait("tensor", "VE", mbt)

            def cons_c(mo, slot, mt):
                bld.wait("vector", "TN", mt)
                vt = bld.op("vector", "VE", lambda e, slot=slot, mo=mo: e.tensor_copy(
                    out=CMS[:, mo * W5:(mo + 1) * W5], in_=slot))
                return ("VE", vt)
            stage2(WCP, 16, cons_c)
            T["s5_xact_tn"] = bld.cnt["TN"]
            bld.wait("scalar", "VE", bld.cnt["VE"])
            datok = None
            for k in range(NCH):
                datok = bld.op("scalar", "SC", lambda e, k=k: e.activation(
                    out=DELTA[:, k * W5:(k + 1) * W5], in_=DELTA[:, k * W5:(k + 1) * W5],
                    func=AF.Exp, scale=vc("A", k)))
            bld.wait("vector", "SC", datok)
            if T.get("gstok") is not None:
                bld.wait("vector", "LD", T["gstok"])
            gtok = None
            for k in range(NCH):
                init_h = 0.0 if wi == 0 else SM5[:, 160 + k:161 + k]
                init_p = 1.0 if wi == 0 else SM5[:, 168 + k:169 + k]
                bld.op("vector", "VE", lambda e, k=k, ih=init_h: e.tensor_tensor_scan(
                    out=HSS[:, k * W5:(k + 1) * W5], data0=DELTA[:, k * W5:(k + 1) * W5],
                    data1=DBX[:, k * W5:(k + 1) * W5], initial=ih,
                    op0=ALU.mult, op1=ALU.add))
                bld.op("vector", "VE", lambda e, k=k, ip=init_p: e.tensor_tensor_scan(
                    out=DBX[:, k * W5:(k + 1) * W5], data0=DELTA[:, k * W5:(k + 1) * W5],
                    data1=DELTA[:, k * W5:(k + 1) * W5], initial=ip,
                    op0=ALU.mult, op1=ALU.bypass))
                bld.op("vector", "VE", lambda e, k=k: e.tensor_copy(
                    out=SM5[:, 160 + k:161 + k], in_=HSS[:, (k + 1) * W5 - 1:(k + 1) * W5]))
                bld.op("vector", "VE", lambda e, k=k: e.tensor_copy(
                    out=SM5[:, 168 + k:169 + k], in_=DBX[:, (k + 1) * W5 - 1:(k + 1) * W5]))
                bld.op("vector", "VE", lambda e, k=k: e.tensor_tensor(
                    out=HSS[:, k * W5:(k + 1) * W5], in0=CMS[:, k * W5:(k + 1) * W5],
                    in1=HSS[:, k * W5:(k + 1) * W5], op=ALU.mult))
                bld.op("vector", "VE", lambda e, k=k: e.scalar_tensor_tensor(
                    out=HSS[:, k * W5:(k + 1) * W5], in0=XACT[:, k * W5:(k + 1) * W5],
                    scalar=vc("D_skip", k), in1=HSS[:, k * W5:(k + 1) * W5],
                    op0=ALU.mult, op1=ALU.add))
                bld.op("vector", "VE", lambda e, k=k, par=par: e.tensor_tensor(
                    out=GATED[:, k * W5:(k + 1) * W5], in0=HSS[:, k * W5:(k + 1) * W5],
                    in1=ZS2[:, par * NTOK + k * W5:par * NTOK + (k + 1) * W5], op=ALU.mult))
                bld.op("vector", "VE", lambda e, k=k: e.tensor_tensor(
                    out=CMS[:, k * W5:(k + 1) * W5], in0=CMS[:, k * W5:(k + 1) * W5],
                    in1=DBX[:, k * W5:(k + 1) * W5], op=ALU.mult))
                gtok = bld.op("vector", "VE", lambda e, k=k, par=par: e.tensor_tensor(
                    out=GW[:, k * W5:(k + 1) * W5], in0=CMS[:, k * W5:(k + 1) * W5],
                    in1=ZS2[:, par * NTOK + k * W5:par * NTOK + (k + 1) * W5], op=ALU.mult))
            if DBG == "s5w0" and wi == 0:
                bld.wait("gpsimd", "VE", bld.cnt["VE"])
                for nm, tt in (("XACT", XACT), ("DELTA", DELTA), ("DBX", DBX),
                               ("CMS", CMS), ("HSS", HSS)):
                    idx = {"XACT": 0, "DELTA": 1, "DBX": 2, "CMS": 3, "HSS": 4}[nm]
                    bld.dma("gpsimd", "LG", dr["dbg"][idx], tt[:])
                bld.dma("gpsimd", "LG", dr["dbg"][5], GATED[:])
                bld.dma("gpsimd", "LG", dr["dbg"][6], XN[:, 0:NTOK])
                bld.dma("gpsimd", "LG", dr["dbg"][7], XB2[:, 0:NTOK])
                bld.dma("gpsimd", "LG", dr["dbg"][8], ZS2[:, 0:NTOK])
                bld.dma("gpsimd", "LG", dr["dbg"][9], MBC[:])
                bld.dma("gpsimd", "LG", dr["dbg"][10], RSBC[:])
                bld.dma("gpsimd", "LG", dr["dbg"][11], GW[:])
                bld.barrier("s5dbg")
            bld.wait("sync", "VE", gtok)
            gstok = bld.dma("sync", "LD",
                            dr["gdef"][:, :, wi * W5:(wi + 1) * W5].rearrange("a p c -> p a c"),
                            GW[:].rearrange("p (a c) -> p a c", c=W5))
            T["gstok"] = gstok
            bld.wait("tensor", "VE", gtok)
            wtoks = {}
            for mo in range(3):
                wtoks[mo] = prefetch(dr["W_out"], mo, mo % 4)
            for mo in range(NCH):
                wpar = mo % 4
                bld.wait("tensor", "LG", wtoks[mo])
                if mo + 3 < NCH:
                    wtoks[mo + 3] = prefetch(dr["W_out"], mo + 3, (mo + 3) % 4)
                ja, pa = pool.acq()
                for k in range(NCH):
                    fn = lambda e, p=pa, wpar=wpar, k=k: e.matmul(
                        out=p[:, 0:W5], lhsT=WB[wpar][:, k * 128:(k + 1) * 128],
                        rhs=GATED[:, k * W5:(k + 1) * W5],
                        start=(k == 0), stop=(k == NCH - 1), skip_group_check=True)
                    if k == NCH - 1:
                        mtok = bld.op("tensor", "TN", fn)
                    else:
                        bld.opn("tensor", fn)
                wo_used[wpar] = mtok
                bld.wait("vector", "TN", mtok)
                xsl = XS[:, mo * NTOK + wi * W5:mo * NTOK + (wi + 1) * W5]
                bld.op("vector", "VE", lambda e, xsl=xsl, wi=wi: e.tensor_tensor(
                    out=TMP6[:], in0=xsl, in1=MBC[:, wi * W5:(wi + 1) * W5], op=ALU.subtract))
                bld.op("vector", "VE", lambda e, wi=wi: e.tensor_tensor(
                    out=TMP6[:], in0=TMP6[:], in1=RSBC[:, wi * W5:(wi + 1) * W5], op=ALU.mult))
                bld.op("vector", "VE", lambda e, p=pa: e.tensor_tensor(
                    out=TMP6[:], in0=TMP6[:], in1=p[:, 0:W5], op=ALU.add))
                vtok = bld.op("vector", "VE", lambda e, xsl=xsl: e.tensor_tensor(
                    out=xsl, in0=xsl, in1=TMP6[:], op=ALU.add))
                pool.rel(ja, "VE", vtok)
            if DBG == "s5win":
                bld.wait("gpsimd", "VE", bld.cnt["VE"])
                bld.dma("gpsimd", "LG", dr["dbg"][wi], XS[:, 0:NTOK])
                bld.dma("gpsimd", "LG", dr["dbg"][8 + wi], GATED[:])
                bld.barrier(f"s5wd{wi}")
            if DBG == "s5bar":
                bld.barrier(f"s5wb{wi}")
            if DBG == "s5tv":
                bld.wait("tensor", "VE", bld.cnt["VE"])
                bld.wait("vector", "TN", bld.cnt["TN"])
            if DBG == "s5ts":
                bld.wait("tensor", "SC", bld.cnt["SC"])
                bld.wait("scalar", "TN", bld.cnt["TN"])
                bld.wait("scalar", "VE", bld.cnt["VE"])
                bld.wait("vector", "SC", bld.cnt["SC"])
            if DBG == "s5gp":
                bld.wait("tensor", "LG", bld.cnt.get("LG", 0))
                bld.wait("gpsimd", "TN", bld.cnt["TN"])
                bld.wait("gpsimd", "VE", bld.cnt["VE"])
            if DBG == "s5f1":
                bld.wait("vector", "LD", bld.cnt.get("LD", 0))
            if DBG == "s5f2":
                bld.wait("scalar", "VE", bld.cnt["VE"])
            if DBG == "s5f3":
                bld.wait("tensor", "SC", bld.cnt["SC"])
            if DBG == "s5sy":
                bld.wait("sync", "VE", bld.cnt["VE"])
                bld.wait("sync", "TN", bld.cnt["TN"])
                bld.wait("vector", "LD", bld.cnt.get("LD", 0))
                bld.wait("tensor", "LD", bld.cnt.get("LD", 0))

        if DBG == "s5xs":
            bld.wait("gpsimd", "VE", bld.cnt["VE"])
            for kk in range(8):
                bld.dma("gpsimd", "LG", dr["dbg"][kk], XS[:, kk * NTOK:(kk + 1) * NTOK])
            bld.barrier("s5xsdbg")
        bld.wait("sync", "VE", bld.cnt["VE"])
        stok = bld.dma("sync", "LD", dr["aghi"], SM5[:, 160:168])
        bld.wait("gpsimd", "LD", stok)
        agtok = bld.op("gpsimd", "GP", lambda e: e.collective_compute(
            "AllGather", ALU.bypass, replica_groups=[list(range(8))],
            ins=[dr["aghi"]], outs=[dr["agho"]]))
        bld.wait("sync", "GP", agtok)
        rtok = bld.dma("sync", "LD", SM5[:, 16:80].rearrange("p (s c) -> p s c", c=8),
                       dr["agho"].rearrange("s p c -> p s c"))
        bld.wait("vector", "LD", rtok)
        h0tok = bld.op("vector", "VE", lambda e: e.tensor_scalar(
            out=SM5[:, 176:184], in0=SM5[:, 16:24], scalar1=VEC[:, sp0:sp0 + 1],
            scalar2=None, op0=ALU.mult))
        for s in range(1, 8):
            h0tok = bld.op("vector", "VE", lambda e, s=s: e.scalar_tensor_tensor(
                out=SM5[:, 176:184], in0=SM5[:, 16 + 8 * s:24 + 8 * s],
                scalar=VEC[:, sp0 + s:sp0 + s + 1], in1=SM5[:, 176:184],
                op0=ALU.mult, op1=ALU.add))
        for wi in range(NW5):
            grd = bld.dma("sync", "LD", GW[:].rearrange("p (a c) -> p a c", c=W5),
                          dr["gdef"][:, :, wi * W5:(wi + 1) * W5].rearrange("a p c -> p a c"))
            bld.wait("vector", "LD", grd)
            gct = None
            for k in range(NCH):
                gct = bld.op("vector", "VE", lambda e, k=k: e.tensor_scalar(
                    out=GW[:, k * W5:(k + 1) * W5], in0=GW[:, k * W5:(k + 1) * W5],
                    scalar1=SM5[:, 176 + k:177 + k], scalar2=None, op0=ALU.mult))
            bld.wait("tensor", "VE", gct)
            if DBG == "s5g" and wi == 0:
                bld.wait("gpsimd", "VE", bld.cnt["VE"])
                bld.dma("gpsimd", "LG", dr["dbg"][12], GW[:])
                bld.dma("gpsimd", "LG", dr["dbg"][13][:, 0:24], SM5[:, 160:184])
                bld.barrier("s5gdbg")
            wtoks = {}
            for mo in range(3):
                wtoks[mo] = prefetch(dr["W_out"], mo, mo % 4)
            for mo in range(NCH):
                wpar = mo % 4
                bld.wait("tensor", "LG", wtoks[mo])
                if mo + 3 < NCH:
                    wtoks[mo + 3] = prefetch(dr["W_out"], mo + 3, (mo + 3) % 4)
                ja, pa = pool.acq()
                for k in range(NCH):
                    fn = lambda e, p=pa, wpar=wpar, k=k: e.matmul(
                        out=p[:, 0:W5], lhsT=WB[wpar][:, k * 128:(k + 1) * 128],
                        rhs=GW[:, k * W5:(k + 1) * W5],
                        start=(k == 0), stop=(k == NCH - 1), skip_group_check=True)
                    if k == NCH - 1:
                        mtok = bld.op("tensor", "TN", fn)
                    else:
                        bld.opn("tensor", fn)
                wo_used[wpar] = mtok
                bld.wait("vector", "TN", mtok)
                xsl = XS[:, mo * NTOK + wi * W5:mo * NTOK + (wi + 1) * W5]
                vtok = bld.op("vector", "VE", lambda e, xsl=xsl, p=pa: e.tensor_tensor(
                    out=xsl, in0=xsl, in1=p[:, 0:W5], op=ALU.add))
                pool.rel(ja, "VE", vtok)
            bld.wait("sync", "TN", bld.cnt["TN"])
    bld.barrier(tag)


def build_program():
    nc = bass.Bass("TRN2", target_bir_lowering=False)
    NV = len(VCOLS)

    dr = {}
    dr["x"] = nc.dram_tensor("x", [NCH, 128, NTOK], F32, kind="ExternalInput").ap()
    dr["out"] = nc.dram_tensor("out", [NCH, 128, NTOK], F32, kind="ExternalOutput").ap()
    dr["dbg"] = nc.dram_tensor("dbg", [16, 128, NTOK], BF16, kind="ExternalOutput").ap()
    dr["vec"] = nc.dram_tensor("vec", [128, len(VCOLS)], F32, kind="ExternalInput").ap()
    dr["ones_m"] = nc.dram_tensor("ones_m", [128, 128], BF16, kind="ExternalInput").ap()
    dr["ones_r"] = nc.dram_tensor("ones_r", [1, 128], BF16, kind="ExternalInput").ap()
    dr["ident"] = nc.dram_tensor("ident", [128, 128], BF16, kind="ExternalInput").ap()
    dr["wgT"] = nc.dram_tensor("wgT", [8, 128, HID], I32, kind="ExternalInput").ap()
    dr["wuT"] = nc.dram_tensor("wuT", [8, 128, HID], I32, kind="ExternalInput").ap()
    dr["wdT"] = nc.dram_tensor("wdT", [24, 128, D], I32, kind="ExternalInput").ap()
    dr["uT"] = nc.dram_tensor("uT", [8, 128, 1024], F32, kind="ExternalInput").ap()
    dr["vT"] = nc.dram_tensor("vT", [8, 128, 1024], F32, kind="ExternalInput").ap()
    dr["u_t"] = nc.dram_tensor("u_t", [8, 128, 1024], F32, kind="ExternalInput").ap()
    dr["v_t"] = nc.dram_tensor("v_t", [8, 128, 1024], F32, kind="ExternalInput").ap()
    dr["ag1_in"] = [nc.dram_tensor(f"ag1_in{t}", [8, 128, NTOK], BF16, kind="Internal").ap() for t in range(2)]
    dr["ag1_out"] = [nc.dram_tensor(f"ag1_out{t}", [8, 8, 128, NTOK], BF16, kind="Internal").ap() for t in range(2)]
    dr["ag2_in"] = [nc.dram_tensor(f"ag2_in{t}", [128, 16384], BF16, kind="Internal").ap() for t in range(2)]
    dr["ag2_out"] = [nc.dram_tensor(f"ag2_out{t}", [8, 128, 16384], BF16, kind="Internal").ap() for t in range(2)]
    dr["WrT"] = nc.dram_tensor("WrT", [8, 128, 1024], F32, kind="ExternalInput").ap()
    dr["WkT"] = nc.dram_tensor("WkT", [8, 128, 1024], F32, kind="ExternalInput").ap()
    dr["WvT"] = nc.dram_tensor("WvT", [8, 128, 1024], F32, kind="ExternalInput").ap()
    dr["agx4i"] = nc.dram_tensor("agx4i", [128, 8], F32, kind="Internal").ap()
    dr["agx4o"] = nc.dram_tensor("agx4o", [8, 128, 8], F32, kind="Internal").ap()
    dr["agx6i"] = nc.dram_tensor("agx6i", [128, 8], F32, kind="Internal").ap()
    dr["agx6o"] = nc.dram_tensor("agx6o", [8, 128, 8], F32, kind="Internal").ap()
    dr["agx5i"] = nc.dram_tensor("agx5i", [128, 16], F32, kind="Internal").ap()
    dr["agx5o"] = nc.dram_tensor("agx5o", [8, 128, 16], F32, kind="Internal").ap()
    dr["aghi"] = nc.dram_tensor("aghi", [128, 8], F32, kind="Internal").ap()
    dr["agho"] = nc.dram_tensor("agho", [8, 128, 8], F32, kind="Internal").ap()
    dr["gdef"] = nc.dram_tensor("gdef", [8, 128, NTOK], BF16, kind="Internal").ap()
    dr["W_in"] = nc.dram_tensor("W_in", [8, 128, 2048], F32, kind="ExternalInput").ap()
    dr["W_out"] = nc.dram_tensor("W_out", [8, 128, 1024], F32, kind="ExternalInput").ap()
    dr["Wsm"] = nc.dram_tensor("Wsm", [8, 128, 96], F32, kind="ExternalInput").ap()
    dr["W_dt"] = nc.dram_tensor("W_dt", [64, 1024], F32, kind="ExternalInput").ap()
    dr["W_Bp"] = nc.dram_tensor("W_Bp", [16, 1024], F32, kind="ExternalInput").ap()
    dr["W_Cp"] = nc.dram_tensor("W_Cp", [16, 1024], F32, kind="ExternalInput").ap()

    es_top = ExitStack()
    bld = Bld(nc, es_top)
    with es_top:
        XS = es_top.enter_context(nc.sbuf_tensor("XS", [128, NCH * NTOK], F32))
        XN = es_top.enter_context(nc.sbuf_tensor("XN", [128, NCH * NTOK], BF16))
        VEC = es_top.enter_context(nc.sbuf_tensor("VEC", [128, NV], F32))
        ONESM = es_top.enter_context(nc.sbuf_tensor("ONESM", [128, 128], BF16))
        ONESR = es_top.enter_context(nc.sbuf_tensor("ONESR", [1, 128], BF16))
        IDENT = es_top.enter_context(nc.sbuf_tensor("IDENT", [128, 128], BF16))
        ROWS = es_top.enter_context(nc.sbuf_tensor("ROWS", [128, NTOK], F32))
        bld.dum = es_top.enter_context(nc.sbuf_tensor("DUM", [1, 8], F32))

        for k in range(NCH):
            bld.dma("sync", "LD", chs(XS, k), dr["x"][k])
        bld.dma("sync", "LD", VEC[:], dr["vec"])
        bld.dma("sync", "LD", ONESM[:], dr["ones_m"])
        bld.dma("sync", "LD", ONESR[:], dr["ones_r"])
        ldtok = bld.dma("sync", "LD", IDENT[:], dr["ident"])
        for e in ENGS:
            bld.wait(e, "LD", ldtok)

        T = {"VEC": VEC, "es": None}

        for sub in SUBS:
            if sub in (1, 3):
                with ExitStack() as es:
                    PA = es.enter_context(nc.psum_tensor("PA%d" % sub, [128, 2048], F32))
                    PB = es.enter_context(nc.psum_tensor("PB%d" % sub, [128, 2048], F32))
                    LXB = es.enter_context(nc.sbuf_tensor("LXB%d" % sub, [128, 2 * NTOK], BF16))
                    LSQ = es.enter_context(nc.sbuf_tensor("LSQ%d" % sub, [128, 2 * NTOK], BF16))
                    LE1 = es.enter_context(nc.sbuf_tensor("LE1%d" % sub, [128, NTOK], F32))
                    HACT = es.enter_context(nc.sbuf_tensor("HACT%d" % sub, [128, 6 * NTOK], BF16))
                    WG = [es.enter_context(nc.sbuf_tensor(f"WG{sub}_{i}", [128, 1024], BF16)) for i in range(2)]
                    WU = [es.enter_context(nc.sbuf_tensor(f"WU{sub}_{i}", [128, 1024], BF16)) for i in range(2)]
                    WD = es.enter_context(nc.sbuf_tensor(f"WD{sub}", [128, 6 * 1024], BF16))
                    pool = Pool(bld, [PA[:, i * W:(i + 1) * W] for i in range(4)]
                                + [PB[:, i * W:(i + 1) * W] for i in range(4)])
                    ln = emit_ln(bld, T, XS, PA, PB, XN, ONESM, ONESR, VEC, LXB, LSQ, LE1,
                                 ROWS, f"ln{sub}")
                    bld.wait("tensor", "VE", ln["applied"])
                    if DBG == "xn" and sub == 1:
                        bld.wait("sync", "VE", ln["applied"])
                        for k in range(NCH):
                            bld.dma("sync", "ST", dr["dbg"][k], chs(XN, k))
                    T2 = dict(T, HACT=HACT, WG=WG, WU=WU, WD=WD, TMPW=LE1[:, 0:W],
                              wgT_h=dr["wgT"], wuT_h=dr["wuT"], wdT_h=dr["wdT"])
                    emit_l0ffn(bld, T2, XS, XN, PA, PB, pool, f"ffn{sub}")
                    if DBG == "hact" and sub == 1:
                        bld.wait("sync", "VE", bld.cnt["VE"])
                        for k in range(12):
                            bld.dma("sync", "ST", dr["dbg"][k], HACT[:, k * NTOK:(k + 1) * NTOK])
                    bld.barrier(f"sub{sub}")
            elif sub == 2:
                with ExitStack() as es:
                    PA = es.enter_context(nc.psum_tensor("PA%d" % sub, [128, 2048], F32))
                    PB = es.enter_context(nc.psum_tensor("PB%d" % sub, [128, 2048], F32))
                    LXB = es.enter_context(nc.sbuf_tensor("LXB%d" % sub, [128, 2 * NTOK], BF16))
                    LSQ = es.enter_context(nc.sbuf_tensor("LSQ%d" % sub, [128, 2 * NTOK], BF16))
                    LE1 = es.enter_context(nc.sbuf_tensor("LE1%d" % sub, [128, NTOK], F32))
                    emit_ln(bld, T, XS, PA, PB, XN, ONESM, ONESR, VEC, LXB, LSQ, LE1,
                            ROWS, "ln2")
                    bld.barrier("ln2b")
                    if DBG == "xn2":
                        for kk in range(NCH):
                            bld.dma("sync", "ST", dr["dbg"][kk], chs(XN, kk))
                emit_l0attn(bld, T, XS, XN, dr, None, ONESM, ONESR, IDENT, VEC)
            elif sub == 5:
                emit_l1attn(bld, T, XS, XN, dr, ONESM, ONESR, VEC, ROWS)
            elif sub in (4, 6):
                emit_l1ffn(bld, T, XS, XN, dr, ONESM, ONESR, VEC, ROWS,
                           dr["agx4i" if sub == 4 else "agx6i"],
                           dr["agx4o" if sub == 4 else "agx6o"], f"s{sub}")
        st = None
        for k in range(NCH):
            st = bld.dma("sync", "ST", dr["out"][k], chs(XS, k))
        bld.wait("sync", "ST", st)

        with (nc.Block() as blk):
            @blk.sync
            def _(e):
                for f in bld.q["sync"]:
                    f(e)

            @blk.tensor
            def _(e):
                for f in bld.q["tensor"]:
                    f(e)

            @blk.vector
            def _(e):
                for f in bld.q["vector"]:
                    f(e)

            @blk.scalar
            def _(e):
                for f in bld.q["scalar"]:
                    f(e)

            @blk.gpsimd
            def _(e):
                for f in bld.q["gpsimd"]:
                    f(e)
    return nc


def host_inputs(inputs):
    x = np.ascontiguousarray(inputs["L0_attn_qz"], dtype=np.float32)
    xT = np.ascontiguousarray(x.reshape(16384, 1024).T)
    wgT = np.ascontiguousarray(inputs["L0_ffn_w_gate"].T).reshape(8, 128, HID)
    wuT = np.ascontiguousarray(inputs["L0_ffn_w_up"].T).reshape(8, 128, HID)
    wdT = np.ascontiguousarray(inputs["L0_ffn_w_down"].T).reshape(24, 128, D)
    u = np.ascontiguousarray(inputs["L0_attn_ternary_factor_u"], np.float32)
    v = np.ascontiguousarray(inputs["L0_attn_ternary_factor_v"], np.float32)
    uT = np.ascontiguousarray(u.T).reshape(8, 128, 1024)
    vT = np.ascontiguousarray(v.T).reshape(8, 128, 1024)
    u_t = u.reshape(8, 128, 1024)
    v_t = v.reshape(8, 128, 1024)
    WrT = np.ascontiguousarray(inputs["L1_ffn_W_r"].T).reshape(8, 128, 1024)
    WkT = np.ascontiguousarray(inputs["L1_ffn_W_k"].T).reshape(8, 128, 1024)
    WvT = np.ascontiguousarray(inputs["L1_ffn_W_v"].T).reshape(8, 128, 1024)
    W_in = np.ascontiguousarray(inputs["L1_attn_W_in"], np.float32).reshape(8, 128, 2048)
    W_out = np.ascontiguousarray(inputs["L1_attn_W_out"], np.float32).reshape(8, 128, 1024)
    Wsm = np.concatenate([
        np.asarray(inputs["L1_attn_W_xp"], np.float32).reshape(8, 128, 64),
        np.asarray(inputs["L1_attn_W_Bg"], np.float32).reshape(8, 128, 16),
        np.asarray(inputs["L1_attn_W_Cg"], np.float32).reshape(8, 128, 16)], axis=2)
    Wsm = np.ascontiguousarray(Wsm)

    NV = len(VCOLS)
    vec_base = np.zeros((128, NV), np.float32)

    def setv(name, val):
        vec_base[:, _vc(name)] = val

    def setvk(name, arr):
        a = np.asarray(arr, np.float32).reshape(8, 128).T
        for k in range(8):
            vec_base[:, _vc(f"{name}{k}")] = a[:, k]

    setv("eps", EPS)
    setv("one", 1.0)
    setv("neghalf", -0.5)
    setv("s127", S127)
    setvk("gamma", inputs["L1_attn_gamma"])
    setvk("beta", inputs["L1_attn_beta"])
    setvk("conv_b", inputs["L1_attn_conv_b"])
    setvk("b_dt", inputs["L1_attn_b_dt"])
    setvk("A", inputs["L1_attn_A"])
    setvk("D_skip", inputs["L1_attn_D_skip"])
    setvk("mu_r", inputs["L1_ffn_mu_r"])
    setvk("mu_k", inputs["L1_ffn_mu_k"])
    cw = np.asarray(inputs["L1_attn_conv_w"], np.float32)
    setvk("cw0", cw[0, 0])
    setvk("cw1", cw[1, 0])
    setvk("cw2", cw[2, 0])

    ones_m = np.full((128, 128), 1.0 / 1024.0, BF)
    ones_r = np.full((1, 128), 1.0, BF)
    ident = np.eye(128, dtype=BF)

    in_maps = []
    for c in range(NCORE):
        vec = vec_base.copy()
        selp = np.zeros(8, np.float32)
        seln = np.zeros(8, np.float32)
        if c % 2 == 1:
            selp[c - 1] = 1.0
        if c % 2 == 0:
            seln[c + 1] = 1.0
        for k in range(8):
            vec[:, _vc(f"selprev{k}")] = selp[k]
            vec[:, _vc(f"selnext{k}")] = seln[k]
        xc = np.ascontiguousarray(xT[:, c * NTOK:(c + 1) * NTOK]).reshape(NCH, 128, NTOK)
        in_maps.append(dict(x=xc, vec=vec, ones_m=ones_m, ones_r=ones_r, ident=ident,
                            wgT=wgT, wuT=wuT, wdT=wdT, uT=uT, vT=vT, u_t=u_t, v_t=v_t,
                            WrT=WrT, WkT=WkT, WvT=WvT, W_in=W_in, W_out=W_out,
                            Wsm=Wsm,
                            W_dt=np.ascontiguousarray(inputs["L1_attn_W_dt"], np.float32),
                            W_Bp=np.ascontiguousarray(inputs["L1_attn_W_Bp"], np.float32),
                            W_Cp=np.ascontiguousarray(inputs["L1_attn_W_Cp"], np.float32)))
    return in_maps


def kernel(**inputs):
    nc = build_program()
    in_maps = host_inputs(inputs)
    res = run_bass_kernel_spmd(nc, in_maps, core_ids=list(range(NCORE)))
    outT = np.concatenate([r["out"].reshape(1024, NTOK) for r in res.results], axis=1)
    return np.ascontiguousarray(outT.T).reshape(4, 4096, 1024)
